# version 22 (final)
# baseline (speedup 1.0000x reference)
import numpy as np
import ml_dtypes
from contextlib import ExitStack

import concourse.bass as bass
import concourse.mybir as mybir
import concourse.bass_utils as _bu
from concourse.bass_utils import run_bass_kernel_spmd

_orig_bvo = _bu.bir_verify_and_optimise


def _patched_bvo(tmpdir, inp="bir.json", outp="file.neff", arch=None, *, dve_root=None):
    orig_run = _bu.run_command

    def run_patched(cmd, **kw):
        cmd = ["--enable-ldw-opt=true" if c == "--enable-ldw-opt=false" else c for c in cmd]
        return orig_run(cmd, **kw)

    _bu.run_command = run_patched
    try:
        return _orig_bvo(tmpdir, inp, outp, arch, dve_root=dve_root)
    finally:
        _bu.run_command = orig_run


_bu.bir_verify_and_optimise = _patched_bvo

F32 = mybir.dt.float32
BF16 = mybir.dt.bfloat16
AF = mybir.ActivationFunctionType
ALU = mybir.AluOpType

D, T, DI, NST, DTR = 1024, 2048, 2048, 16, 64
NE, HEX, IE, ISH, NL = 16, 2048, 512, 1024, 4
LN_EPS, ATT_EPS = 1e-5, 1e-6
NCORES = 8
DT8 = D // 128
DIT = DI // 128
TC4 = T // 512
TT16 = T // 128


class KB:

    def __init__(self, nc):
        self.nc = nc
        self.es = ExitStack()
        self.cnt = {}
        self.sems = {}
        for name in ("pe", "act", "dve", "gp", "dg", "ds", "da", "dxt", "dw0", "dw1",
                     "dz0", "dz1", "dxc0", "dxc1", "du_", "dzr", "dyr", "dyt", "dyg",
                     "dof", "dob", "mpe", "mact", "mdve", "mw0", "mw1", "mxq", "mcw", "mgo"):
            self.sems[name] = self.es.enter_context(nc.semaphore("s_" + name))
            self.cnt[name] = 0

    def sem(self, n):
        return self.sems[n]

    def tag(self, instr, n, k=1):
        instr.then_inc(self.sems[n], k)
        self.cnt[n] += k
        return self.cnt[n]

    def val(self, n):
        return self.cnt[n]

    def sbuf(self, name, shape, dtype):
        return self.es.enter_context(self.nc.sbuf_tensor(name, shape, dtype))

    def psum(self, name, shape, dtype=F32):
        return self.es.enter_context(self.nc.psum_tensor(name, shape, dtype))

    DMA_SEMS = ("dg", "ds", "da", "dxt", "dw0", "dw1", "dz0", "dz1", "dxc0", "dxc1",
                "du_", "dzr", "dyr", "dyt", "dyg", "dof", "dob", "mw0", "mw1", "mxq", "mcw", "mgo")

    def drain_body(self, block):
        vals = [(self.sems[n], self.cnt[n]) for n in self.DMA_SEMS if self.cnt[n] > 0]

        def _(sync):
            for sem, v in vals:
                sync.wait_ge(sem, v)
        block.sync(_)


class SerialEmitter:

    ENGS = ("tensor", "vector", "scalar", "gpsimd", "sync")
    SEM = {"tensor": "pe", "vector": "dve", "scalar": "act", "gpsimd": "gp", "sync": "ds"}

    def __init__(self, kb):
        self.kb = kb
        self.steps = []

    def step(self, eng, fn, dma=False):
        self.steps.append((eng, fn, dma))

    def emit(self, block):
        kb = self.kb
        base = {e: kb.cnt[self.SEM[e]] for e in self.ENGS}
        cnt = dict(base)
        tagv = []
        for eng, fn, dma in self.steps:
            s = self.SEM[eng]
            cnt[eng] = cnt[eng]
            inc = 16 if dma else 1
            base[eng] += 0
            tagv.append((self.SEM[eng], None))
        run = {e: kb.cnt[self.SEM[e]] for e in self.ENGS}
        for i, (eng, fn, dma) in enumerate(self.steps):
            run[eng] += 16 if dma else 1
            tagv[i] = (self.SEM[eng], run[eng])

        def make_body(myeng):
            mysteps = [(i, fn, dma) for i, (e, fn, dma) in enumerate(self.steps) if e == myeng]
            if not mysteps:
                return None

            def body(engine):
                for i, fn, dma in mysteps:
                    if i > 0:
                        psem, pval = tagv[i - 1]
                        engine.wait_ge(kb.sems[psem], pval)
                    instr = fn(engine)
                    kb.tag(instr, tagv[i][0], 16 if dma else 1)
            return body

        for e in self.ENGS:
            b = make_body(e)
            if b is not None:
                getattr(block, {"tensor": "tensor", "vector": "vector", "scalar": "scalar",
                                "gpsimd": "gpsimd", "sync": "sync"}[e])(b)


INTER = {
    "x0T": ([D, T], F32),
    "xn0f": ([D, T], F32),
    "xn0b": ([D, T], BF16),
    "mx1f": ([D, T], F32), "mx1b": ([D, T], BF16),
    "mx2f": ([D, T], F32), "mx2b": ([D, T], BF16),
    "mx3f": ([D, T], F32), "mx3b": ([D, T], BF16),
    "s0f": ([D, T], F32), "s0b": ([D, T], BF16),
    "x1f": ([D, T], F32), "x1b": ([D, T], BF16),
    "xn1b": ([D, T], BF16),
    "s1f": ([D, T], F32),
    "x2f": ([D, T], F32),
    "zbuf": ([DI, T], BF16),
    "xcf": ([DI, T], F32),
    "ygb": ([DI, T], BF16),
    "ytmp": ([DI, T], BF16),
    "cwt0": ([16, T], F32),
    "b1c": ([128, 256], F32),
    "b2c": ([128, 128], F32),
    "g0f": ([D, T], F32),
}

WSPECS = {
    "L0_attn_x": ([T, D], F32),
    "L0_attn_in_proj_w": ([NL, 2 * DI, D], F32),
    "L0_attn_conv_w": ([NL, DI, 4], F32),
    "L0_attn_conv_b": ([NL, DI], F32),
    "L0_attn_x_proj_w": ([NL, DTR + 2 * NST, DI], F32),
    "L0_attn_dt_proj_w": ([NL, DI, DTR], F32),
    "L0_attn_dt_proj_b": ([NL, DI], F32),
    "L0_attn_A_log": ([NL, DI, NST], F32),
    "L0_attn_D_skip": ([NL, DI], F32),
    "L0_attn_out_proj_w": ([NL, D, DI], F32),
    "L0_attn_ln_w": ([NL, D], F32),
    "L0_attn_ln_b": ([NL, D], F32),
    "L0_ffn_Wg": ([D, NE], F32),
    "L0_ffn_bg": ([1, NE], F32),
    "L0_ffn_bias": ([1, NE], F32),
    "L0_ffn_W1": ([NE, D, HEX], F32),
    "L0_ffn_b1": ([NE, HEX], F32),
    "L0_ffn_W2": ([NE, HEX, D], F32),
    "L0_ffn_b2": ([NE, D], F32),
    "L1_attn_Wq": ([256, D], F32),
    "L1_attn_Wk": ([256, D], F32),
    "L1_attn_Wv": ([256, D], F32),
    "L1_attn_Wo": ([D, 256], F32),
    "L1_ffn_gate_w": ([NE, D], F32),
    "L1_ffn_w_gate": ([NE, D, IE], F32),
    "L1_ffn_w_up": ([NE, D, IE], F32),
    "L1_ffn_w_down": ([NE, IE, D], F32),
    "L1_ffn_ws_gate": ([D, ISH], F32),
    "L1_ffn_ws_up": ([D, ISH], F32),
    "L1_ffn_ws_down": ([ISH, D], F32),
    "c_eye_f": ([128, 128], F32),
    "c_eye_b": ([128, 128], BF16),
    "c_onesrow_f": ([1, 128], F32),
    "c_onesrow_b": ([1, 128], BF16),
    "c_onescol_f": ([128, 1], F32),
    "c_sel16_f": ([16, 16 * 128], F32),
    "c_sel32_b": ([32, 32 * 128], BF16),
    "c_maskT_b": ([128, 128], BF16),
}


def emit_ln_stats(kb, block_dummy, V, SQ, ROWS, PS1, PS2, ONESC, wait_list):
    raise RuntimeError("inlined")


TRUNC = 0


def sec_in(kb, hbm):
    nc = kb.nc
    esS = ExitStack()
    EYE = esS.enter_context(nc.sbuf_tensor("si_eye", [128, 128], F32))
    ONESC = esS.enter_context(nc.sbuf_tensor("si_onesc", [128, 1], F32))
    ONESR = esS.enter_context(nc.sbuf_tensor("si_onesr", [1, 128], F32))
    V = esS.enter_context(nc.sbuf_tensor("si_v", [128, DT8 * T], F32))
    ROWS = esS.enter_context(nc.sbuf_tensor("si_rows", [1, 4 * T], F32))
    SQ = esS.enter_context(nc.sbuf_tensor("si_sq", [128, 2 * 512], F32))
    XB16 = esS.enter_context(nc.sbuf_tensor("si_xb", [128, DT8 * T], BF16))
    XH = esS.enter_context(nc.sbuf_tensor("si_xh", [128, 512], F32))
    PT = esS.enter_context(nc.psum_tensor("si_pt", [128, 4 * 512], F32))
    PS1 = esS.enter_context(nc.psum_tensor("si_ps1", [1, 512], F32))
    PS2 = esS.enter_context(nc.psum_tensor("si_ps2", [1, 512], F32))
    PBa = esS.enter_context(nc.psum_tensor("si_pba", [128, 512], F32))
    PBb = esS.enter_context(nc.psum_tensor("si_pbb", [128, 512], F32))

    ds0, pe0, act0 = kb.val("ds"), kb.val("pe"), kb.val("act")

    with nc.Block() as block, nc.sbuf_tensor("si_xtok", [128, TT16 * D], F32) as XTOK:
        @block.sync
        def _(sync):
            kb.tag(sync.dma_start(out=EYE[:], in_=hbm["c_eye_f"]), "ds", 16)
            kb.tag(sync.dma_start(out=ONESC[:], in_=hbm["c_onescol_f"]), "ds", 16)
            kb.tag(sync.dma_start(out=ONESR[:], in_=hbm["c_onesrow_f"]), "ds", 16)
            xr = hbm["L0_attn_x"].rearrange("(n p) d -> n p d", p=128)
            for tt in range(TT16):
                kb.tag(sync.dma_start(out=XTOK[:, tt * D:(tt + 1) * D], in_=xr[tt]), "ds", 16)
        ld_done = ds0 + 16 * (3 + TT16)

        if TRUNC == 3:
            @block.sync
            def _(sync):
                sync.wait_ge(kb.sem("ds"), ld_done)
                kb.tag(sync.dma_start(out=hbm["x0T"][0:128, :], in_=XTOK[:, 0:T]), "ds", 16)
                sync.wait_ge(kb.sem("ds"), kb.val("ds"))
            return

        @block.tensor
        def _(tensor):
            tensor.wait_ge(kb.sem("ds"), ld_done)
            for i in range(128):
                tt, dk = divmod(i, DT8)
                if i >= 4:
                    tensor.wait_ge(kb.sem("act"), act0 + (i - 3))
                kb.tag(tensor.transpose(PT[:, (i % 4) * 512:(i % 4) * 512 + 128],
                                        XTOK[:, tt * D + dk * 128: tt * D + dk * 128 + 128],
                                        EYE[:]), "pe", 1)

        @block.scalar
        def _(scalar):
            for i in range(128):
                tt, dk = divmod(i, DT8)
                scalar.wait_ge(kb.sem("pe"), pe0 + i + 1)
                kb.tag(scalar.copy(out=V[:, dk * T + tt * 128: dk * T + tt * 128 + 128],
                                   in_=PT[:, (i % 4) * 512:(i % 4) * 512 + 128]), "act", 1)
    pe_b = pe0 + 128
    act_b = act0 + 128

    if TRUNC == 1:
        with nc.Block() as block:
            @block.sync
            def _(sync):
                kb.tag(sync.dma_start(out=hbm["x0T"].rearrange("(n p) t -> p n t", p=128),
                                      in_=V[:].rearrange("p (n t) -> p n t", t=T)), "ds", 16)
                kb.tag(sync.dma_start(out=hbm["xn0f"].rearrange("(n p) t -> p n t", p=128),
                                      in_=V[:].rearrange("p (n t) -> p n t", t=T)), "ds", 16)
                kb.tag(sync.dma_start(out=hbm["xn0b"].rearrange("(n p) t -> p n t", p=128),
                                      in_=XB16[:].rearrange("p (n t) -> p n t", t=T)), "ds", 16)
                sync.wait_ge(kb.sem("ds"), kb.val("ds"))
        return

    with nc.Block() as block:
        @block.sync
        def _(sync):
            kb.tag(sync.dma_start(out=hbm["x0T"].rearrange("(n p) t -> p n t", p=128),
                                  in_=V[:].rearrange("p (n t) -> p n t", t=T)), "ds", 16)
        x0T_done = kb.val("ds")

        @block.scalar
        def _(scalar):
            for tc in range(TC4):
                for dk in range(DT8):
                    i = tc * DT8 + dk
                    if i >= 2:
                        scalar.wait_ge(kb.sem("pe"), pe_b + (i - 2) + 1)
                    kb.tag(scalar.activation(out=SQ[:, (i % 2) * 512:(i % 2) * 512 + 512],
                                             in_=V[:, dk * T + tc * 512: dk * T + tc * 512 + 512],
                                             func=AF.Square), "act", 1)
                scalar.wait_ge(kb.sem("pe"), pe_b + (tc + 1) * DT8)
                scalar.copy(out=ROWS[:, 0 * T + tc * 512: 0 * T + tc * 512 + 512], in_=PS1[:])
                kb.tag(scalar.copy(out=ROWS[:, 1 * T + tc * 512: 1 * T + tc * 512 + 512], in_=PS2[:]), "act", 1)

        @block.tensor
        def _(tensor):
            for tc in range(TC4):
                for dk in range(DT8):
                    i = tc * DT8 + dk
                    tensor.wait_ge(kb.sem("act"), act_b + tc * 9 + dk + 1)
                    tensor.matmul(PS1[:], ONESC[:], V[:, dk * T + tc * 512: dk * T + tc * 512 + 512],
                                  start=(dk == 0), stop=(dk == DT8 - 1))
                    kb.tag(tensor.matmul(PS2[:], ONESC[:], SQ[:, (i % 2) * 512:(i % 2) * 512 + 512],
                                         start=(dk == 0), stop=(dk == DT8 - 1)), "pe", 1)
        pe_s = pe_b + 32
        act_s = act_b + 36

        dve0 = kb.val("dve")

        @block.vector
        def _(vector):
            vector.wait_ge(kb.sem("act"), act_s)
            sm = ROWS[:, 0:T]
            sq = ROWS[:, T:2 * T]
            al = ROWS[:, 2 * T:3 * T]
            be = ROWS[:, 3 * T:4 * T]
            t1 = kb.tag(vector.tensor_scalar(out=al, in0=sm, scalar1=1.0 / D, scalar2=None, op0=ALU.mult), "dve")
            vector.wait_ge(kb.sem("dve"), t1)
            t2 = kb.tag(vector.tensor_mul(be, al, al), "dve")
            t3 = kb.tag(vector.tensor_scalar(out=sm, in0=sq, scalar1=1.0 / D, scalar2=None, op0=ALU.mult), "dve")
            vector.wait_ge(kb.sem("dve"), t3)
            t4 = kb.tag(vector.tensor_sub(sm, sm, be), "dve")
            vector.wait_ge(kb.sem("dve"), t4)
            kb.tag(vector.tensor_scalar(out=sm, in0=sm, scalar1=LN_EPS, scalar2=None,
                                        op0=ALU.add), "dve", 1)

        @block.scalar
        def _(scalar):
            scalar.wait_ge(kb.sem("dve"), dve0 + 5)
            kb.tag(scalar.activation(out=ROWS[:, T:2 * T], in_=ROWS[:, 0:T],
                                     func=AF.Sqrt), "act", 1)

        @block.vector
        def _(vector):
            vector.wait_ge(kb.sem("act"), act_s + 1)
            t5 = kb.tag(vector.reciprocal(ROWS[:, 0:T], ROWS[:, T:2 * T]), "dve")
            vector.wait_ge(kb.sem("dve"), t5)
            t6 = kb.tag(vector.scalar_tensor_tensor(out=ROWS[:, 3 * T:4 * T], in0=ROWS[:, 2 * T:3 * T],
                                                    scalar=-1.0, in1=ROWS[:, 0:T],
                                                    op0=ALU.mult, op1=ALU.mult), "dve")
            vector.wait_ge(kb.sem("dve"), t6)
            kb.tag(vector.tensor_copy(ROWS[:, 2 * T:3 * T], ROWS[:, 0:T]), "dve", 1)
        dve_r = dve0 + 8

        pe_bc0 = pe_s

        @block.tensor
        def _(tensor):
            tensor.wait_ge(kb.sem("dve"), dve_r)
            for tc in range(TC4):
                if tc >= 1:
                    tensor.wait_ge(kb.sem("dve"), dve_r + tc * 16)
                tensor.matmul(PBa[:], ONESR[:], ROWS[:, 2 * T + tc * 512: 2 * T + tc * 512 + 512],
                              start=True, stop=True)
                kb.tag(tensor.matmul(PBb[:], ONESR[:], ROWS[:, 3 * T + tc * 512: 3 * T + tc * 512 + 512],
                                     start=True, stop=True), "pe", 1)

        @block.vector
        def _(vector):
            vector.wait_ge(kb.sem("ds"), x0T_done)
            prev_add = None
            for tc in range(TC4):
                vector.wait_ge(kb.sem("pe"), pe_bc0 + tc + 1)
                for dk in range(DT8):
                    sl = slice(dk * T + tc * 512, dk * T + tc * 512 + 512)
                    if prev_add is not None:
                        vector.wait_ge(kb.sem("dve"), prev_add)
                    tm = kb.tag(vector.tensor_mul(XH[:], V[:, sl], PBa[:]), "dve")
                    vector.wait_ge(kb.sem("dve"), tm)
                    prev_add = kb.tag(vector.tensor_add(V[:, sl], XH[:], PBb[:]), "dve", 1)

        @block.scalar
        def _(scalar):
            for tc in range(TC4):
                for dk in range(DT8):
                    scalar.wait_ge(kb.sem("dve"), dve_r + tc * 16 + 2 * dk + 2)
                    sl = slice(dk * T + tc * 512, dk * T + tc * 512 + 512)
                    kb.tag(scalar.copy(out=XB16[:, sl], in_=V[:, sl]), "act", 1)
        act_f = act_s + 1 + 32
        dve_f = dve_r + 64

        @block.sync
        def _(sync):
            sync.wait_ge(kb.sem("dve"), dve_f)
            kb.tag(sync.dma_start(out=hbm["xn0f"].rearrange("(n p) t -> p n t", p=128),
                                  in_=V[:].rearrange("p (n t) -> p n t", t=T)), "ds", 16)
            sync.wait_ge(kb.sem("act"), act_f)
            kb.tag(sync.dma_start(out=hbm["xn0b"].rearrange("(n p) t -> p n t", p=128),
                                  in_=XB16[:].rearrange("p (n t) -> p n t", t=T)), "ds", 16)
            sync.wait_ge(kb.sem("ds"), kb.val("ds"))

        kb.drain_body(block)
    esS.close()





def emit_ln_tail(kb, hbm, V, outf, outb, wvec, bvec, ONESC, ONESR, SQ, ROWS,
                 XOF, XOB, PS1, PS2, PBa, PBb):
    nc = kb.nc
    with nc.Block() as block:
        pe_b = kb.val("pe")
        act_b = kb.val("act")

        @block.scalar
        def _(scalar):
            for tc in range(TC4):
                for dk in range(DT8):
                    i = tc * DT8 + dk
                    if i >= 2:
                        scalar.wait_ge(kb.sem("pe"), pe_b + (i - 2) + 1)
                    kb.tag(scalar.activation(out=SQ[:, (i % 2) * 512:(i % 2) * 512 + 512],
                                             in_=V[:, dk * T + tc * 512: dk * T + tc * 512 + 512],
                                             func=AF.Square), "act", 1)
                scalar.wait_ge(kb.sem("pe"), pe_b + (tc + 1) * DT8)
                scalar.copy(out=ROWS[:, 0 * T + tc * 512: 0 * T + tc * 512 + 512], in_=PS1[:])
                kb.tag(scalar.copy(out=ROWS[:, 1 * T + tc * 512: 1 * T + tc * 512 + 512], in_=PS2[:]), "act", 1)

        @block.tensor
        def _(tensor):
            for tc in range(TC4):
                for dk in range(DT8):
                    i = tc * DT8 + dk
                    tensor.wait_ge(kb.sem("act"), act_b + tc * 9 + dk + 1)
                    tensor.matmul(PS1[:], ONESC[:], V[:, dk * T + tc * 512: dk * T + tc * 512 + 512],
                                  start=(dk == 0), stop=(dk == DT8 - 1))
                    kb.tag(tensor.matmul(PS2[:], ONESC[:], SQ[:, (i % 2) * 512:(i % 2) * 512 + 512],
                                         start=(dk == 0), stop=(dk == DT8 - 1)), "pe", 1)
        pe_s = pe_b + 32
        act_s = act_b + 36
        dve0 = kb.val("dve")

        @block.vector
        def _(vector):
            vector.wait_ge(kb.sem("act"), act_s)
            sm = ROWS[:, 0:T]
            sq = ROWS[:, T:2 * T]
            al = ROWS[:, 2 * T:3 * T]
            be = ROWS[:, 3 * T:4 * T]
            t1 = kb.tag(vector.tensor_scalar(out=al, in0=sm, scalar1=1.0 / D, scalar2=None, op0=ALU.mult), "dve")
            vector.wait_ge(kb.sem("dve"), t1)
            kb.tag(vector.tensor_mul(be, al, al), "dve")
            t3 = kb.tag(vector.tensor_scalar(out=sm, in0=sq, scalar1=1.0 / D, scalar2=None, op0=ALU.mult), "dve")
            vector.wait_ge(kb.sem("dve"), t3)
            t4 = kb.tag(vector.tensor_sub(sm, sm, be), "dve")
            vector.wait_ge(kb.sem("dve"), t4)
            kb.tag(vector.tensor_scalar(out=sm, in0=sm, scalar1=LN_EPS, scalar2=None, op0=ALU.add), "dve")

        @block.scalar
        def _(scalar):
            scalar.wait_ge(kb.sem("dve"), dve0 + 5)
            kb.tag(scalar.activation(out=ROWS[:, T:2 * T], in_=ROWS[:, 0:T], func=AF.Sqrt), "act", 1)

        @block.vector
        def _(vector):
            vector.wait_ge(kb.sem("act"), act_s + 1)
            t5 = kb.tag(vector.reciprocal(ROWS[:, 0:T], ROWS[:, T:2 * T]), "dve")
            vector.wait_ge(kb.sem("dve"), t5)
            t6 = kb.tag(vector.scalar_tensor_tensor(out=ROWS[:, 3 * T:4 * T], in0=ROWS[:, 2 * T:3 * T],
                                                    scalar=-1.0, in1=ROWS[:, 0:T],
                                                    op0=ALU.mult, op1=ALU.mult), "dve")
            vector.wait_ge(kb.sem("dve"), t6)
            kb.tag(vector.tensor_copy(ROWS[:, 2 * T:3 * T], ROWS[:, 0:T]), "dve")
        dve_r = dve0 + 8
        pe_bc0 = pe_s

        @block.tensor
        def _(tensor):
            tensor.wait_ge(kb.sem("dve"), dve_r)
            for tc in range(TC4):
                if tc >= 1:
                    tensor.wait_ge(kb.sem("dve"), dve_r + tc * 16)
                tensor.matmul(PBa[:], ONESR[:], ROWS[:, 2 * T + tc * 512: 2 * T + tc * 512 + 512],
                              start=True, stop=True)
                kb.tag(tensor.matmul(PBb[:], ONESR[:], ROWS[:, 3 * T + tc * 512: 3 * T + tc * 512 + 512],
                                     start=True, stop=True), "pe", 1)

        @block.vector
        def _(vector):
            prev_add = None
            for tc in range(TC4):
                vector.wait_ge(kb.sem("pe"), pe_bc0 + tc + 1)
                for dk in range(DT8):
                    sl = slice(dk * T + tc * 512, dk * T + tc * 512 + 512)
                    if prev_add is not None:
                        vector.wait_ge(kb.sem("dve"), prev_add)
                    tm = kb.tag(vector.tensor_mul(SQ[:, 0:512], V[:, sl], PBa[:]), "dve")
                    vector.wait_ge(kb.sem("dve"), tm)
                    prev_add = kb.tag(vector.tensor_add(V[:, sl], SQ[:, 0:512], PBb[:]), "dve")
        dve_a = dve_r + 64

        act_a0 = act_s + 1
        dof0 = kb.val("dof")
        dob0 = kb.val("dob")
        dve_a0 = dve_r + 64

        @block.vector
        def _(vector):
            for dk in range(DT8):
                for tc in range(TC4):
                    vector.wait_ge(kb.sem("dve"), dve_r + tc * 16 + 2 * (dk + 1))
                if dk >= 1:
                    if outf:
                        vector.wait_ge(kb.sem("dof"), dof0 + 16 * dk)
                    if outb:
                        vector.wait_ge(kb.sem("dob"), dob0 + 16 * dk)
                    vector.wait_ge(kb.sem("act"), act_a0 + dk)
                sl = slice(dk * T, (dk + 1) * T)
                if wvec is None:
                    kb.tag(vector.tensor_copy(XOF[:], V[:, sl]), "dve")
                else:
                    kb.tag(vector.tensor_scalar(out=XOF[:], in0=V[:, sl],
                                                scalar1=wvec[:, dk:dk + 1], scalar2=bvec[:, dk:dk + 1],
                                                op0=ALU.mult, op1=ALU.add), "dve")

        @block.scalar
        def _(scalar):
            for dk in range(DT8):
                scalar.wait_ge(kb.sem("dve"), dve_a0 + dk + 1)
                if dk >= 1 and outb:
                    scalar.wait_ge(kb.sem("dob"), dob0 + 16 * dk)
                kb.tag(scalar.copy(out=XOB[:], in_=XOF[:]), "act")

        @block.sync
        def _(sync):
            of = hbm[outf].rearrange("(n p) t -> n p t", p=128) if outf else None
            ob = hbm[outb].rearrange("(n p) t -> n p t", p=128) if outb else None
            for dk in range(DT8):
                if of is not None:
                    sync.wait_ge(kb.sem("dve"), dve_a0 + dk + 1)
                    kb.tag(sync.dma_start(out=of[dk], in_=XOF[:]), "dof", 16)
                if ob is not None:
                    sync.wait_ge(kb.sem("act"), act_a0 + dk + 1)
                    kb.tag(sync.dma_start(out=ob[dk], in_=XOB[:]), "dob", 16)
            if of is not None:
                sync.wait_ge(kb.sem("dof"), kb.val("dof"))
            if ob is not None:
                sync.wait_ge(kb.sem("dob"), kb.val("dob"))

        kb.drain_body(block)



class Moe0Slots:
    MPE_PE = 25
    MACT_PE = 17
    MDVE_PE = 16

    def __init__(self, kb, hbm, q, es, ONESR, ONESRB):
        nc = kb.nc
        self.kb = kb
        self.q = q
        self.ONESR = ONESR
        self.ONESRB = ONESRB
        self.W1 = hbm["L0_ffn_W1"]
        self.W2 = hbm["L0_ffn_W2"]
        self.b2c_h = hbm["b2c"]
        self.cwt = hbm["cwt0"]
        self.b1c_h = hbm["b1c"]
        self.xn_h = hbm["xn0b"].rearrange("(n p) t -> n p t", p=128)
        self.xn_pn = hbm["xn0b"].rearrange("(n p) t -> p n t", p=128)
        self.g_h = hbm["g0f"].rearrange("(n p) t -> n p t", p=128)
        self.XNQ = es.enter_context(nc.sbuf_tensor(f"o{q}_xnq", [128, DT8 * 512], BF16))
        self.W1B = es.enter_context(nc.sbuf_tensor(f"o{q}_w1b", [128, 2 * 4096], BF16))
        self.W2B = es.enter_context(nc.sbuf_tensor(f"o{q}_w2b", [128, 4 * 2048], BF16))
        self.HTC = es.enter_context(nc.sbuf_tensor(f"o{q}_htc", [128, 16 * 512], BF16))
        self.Y1 = es.enter_context(nc.sbuf_tensor(f"o{q}_y1", [128, 512], F32))
        self.CWB = es.enter_context(nc.sbuf_tensor(f"o{q}_cwb", [128, 512], F32))
        self.CWSTG = es.enter_context(nc.sbuf_tensor(f"o{q}_cwstg", [1, 2 * 512], BF16))
        self.B1C = es.enter_context(nc.sbuf_tensor(f"o{q}_b1c", [128, 256], BF16))
        self.ACC = es.enter_context(nc.sbuf_tensor(f"o{q}_acc", [128, DT8 * 512], F32))
        self.B2CQ = es.enter_context(nc.sbuf_tensor(f"o{q}_b2c", [128, 128], F32))
        self.PH = es.enter_context(nc.psum_tensor(f"o{q}_ph", [128, 2 * 512], F32))
        self.PO = es.enter_context(nc.psum_tensor(f"o{q}_po", [128, 2 * 512], F32))
        self.PCW = es.enter_context(nc.psum_tensor(f"o{q}_pcw", [128, 512], F32))
        self.mpe0 = kb.val("mpe")
        self.mact0 = kb.val("mact")
        self.mdve0 = kb.val("mdve")
        self.mw0_0 = kb.val("mw0")
        self.mw1_0 = kb.val("mw1")
        self.mxq0 = kb.val("mxq")
        self.mcw0 = kb.val("mcw")
        self.mgo0 = kb.val("mgo")
        self.gp_init = None
        self.ld_done = None

    def mpe_e(self, e):
        return self.mpe0 + e * self.MPE_PE

    def mact_e(self, e):
        return self.mact0 + e * self.MACT_PE

    def mdve_e(self, e):
        return self.mdve0 + e * self.MDVE_PE

    def preamble_sync(self, sy):
        kb = self.kb
        q = self.q
        sy.wait_ge(kb.sem("mpe"), self.mpe0)
        sy.wait_ge(kb.sem("mdve"), self.mdve0)
        kb.tag(sy.dma_start(out=self.XNQ[:].rearrange("p (dk c) -> p dk c", c=512),
                            in_=self.xn_pn[:, :, q * 512:(q + 1) * 512]), "mxq", 16)
        kb.tag(sy.dma_start(out=self.B2CQ[:], in_=self.b2c_h), "mxq", 16)

    def preamble_gp(self, g):
        kb = self.kb
        g.wait_ge(kb.sem("mact"), self.mact0)
        g.wait_ge(kb.sem("mpe"), self.mpe0)
        g.wait_ge(kb.sem("mdve"), self.mdve0)
        kb.tag(g.dma_start(out=self.B1C[:], in_=self.b1c_h), "mcw", 16)
        self.ld_done = kb.val("mxq")
        self.gp_init = kb.tag(g.memset(self.ACC[:], 0.0), "gp", 1)
        self.cw_fill(g, 0)
        self.w1_fill(g, 0)
        self.w1_fill(g, 1)

    def cw_fill(self, eng, e):
        kb = self.kb
        q = self.q
        if e >= 2:
            eng.wait_ge(kb.sem("mpe"), self.mpe_e(e - 2) + 1)
        kb.tag(eng.dma_start(out=self.CWSTG[0:1, (e % 2) * 512:(e % 2) * 512 + 512],
                             in_=self.cwt[e:e + 1, q * 512:(q + 1) * 512]), "mcw", 16)

    def w1_fill(self, g, f):
        kb = self.kb
        e, qr = divmod(f, 4)
        if f >= 2:
            ep, qp = divmod(f - 2, 4)
            g.wait_ge(kb.sem("mpe"), self.mpe_e(ep) + 4 * qp + 5)
        kb.tag(g.dma_start(
            out=self.W1B[:, (f % 2) * 4096:(f % 2) * 4096 + 4096].rearrange(
                "p (dk c) -> p dk c", c=512),
            in_=self.W1[e].rearrange("(dk p) h -> p dk h", p=128)[:, :, qr * 512:(qr + 1) * 512]),
            "mw0", 16)

    def w2_fill(self, g, f2):
        kb = self.kb
        e, dm = divmod(f2, 8)
        if f2 >= 4:
            ep, dp = divmod(f2 - 4, 8)
            g.wait_ge(kb.sem("mpe"), self.mpe_e(ep) + 18 + dp)
        kb.tag(g.dma_start(
            out=self.W2B[:, (f2 % 4) * 2048:(f2 % 4) * 2048 + 2048].rearrange(
                "p (ht c) -> p ht c", c=128),
            in_=self.W2[e].rearrange("(ht p) d -> p ht d", p=128)[:, :, dm * 128:(dm + 1) * 128]),
            "mw1", 16)

    def pe(self, tensor, s):
        kb = self.kb
        e, sub = divmod(s, 4)
        if sub == 0:
            tensor.wait_ge(kb.sem("mxq"), self.ld_done)
            tensor.wait_ge(kb.sem("mcw"), self.mcw0 + 16 * (e + 2))
            if e >= 1:
                tensor.wait_ge(kb.sem("mact"), self.mact_e(e - 1) + 1)
            else:
                tensor.wait_ge(kb.sem("mdve"), self.mdve0)
                tensor.wait_ge(kb.sem("mact"), self.mact0)
            kb.tag(tensor.matmul(self.PCW[:, 0:512], self.ONESRB[:],
                                 self.CWSTG[0:1, (e % 2) * 512:(e % 2) * 512 + 512],
                                 start=True, stop=True), "mpe", 1)
        if sub in (0, 1):
            for hh in range(8):
                ht = sub * 8 + hh
                f = 4 * e + ht // 4
                if ht % 4 == 0:
                    tensor.wait_ge(kb.sem("mw0"), self.mw0_0 + 16 * (max(f, 1) + 1))
                if ht >= 2:
                    tensor.wait_ge(kb.sem("mact"), self.mact_e(e) + 2 + (ht - 2))
                elif e >= 1:
                    tensor.wait_ge(kb.sem("mact"), self.mact_e(e - 1) + 16 + ht)
                for dk in range(DT8):
                    mm = tensor.matmul(self.PH[:, (ht % 2) * 512:(ht % 2) * 512 + 512],
                                       self.W1B[:, (f % 2) * 4096 + dk * 512 + (ht % 4) * 128:
                                                   (f % 2) * 4096 + dk * 512 + (ht % 4) * 128 + 128],
                                       self.XNQ[:, dk * 512:(dk + 1) * 512],
                                       start=(dk == 0), stop=(dk == DT8 - 1))
                kb.tag(mm, "mpe", 1)
        else:
            if sub == 2:
                tensor.wait_ge(kb.sem("mact"), self.mact_e(e) + 17)
            for dd in range(4):
                dm = (sub - 2) * 4 + dd
                f2 = 8 * e + dm
                tensor.wait_ge(kb.sem("mw1"), self.mw1_0 + 16 * ((f2 | 1 if f2 < 4 else f2) + 1))
                if dm >= 2:
                    tensor.wait_ge(kb.sem("mdve"), self.mdve_e(e) + 2 * (dm - 2) + 1)
                elif e >= 1:
                    tensor.wait_ge(kb.sem("mdve"), self.mdve_e(e - 1) + 2 * (dm + 6) + 1)
                for ht in range(16):
                    mm = tensor.matmul(self.PO[:, (dm % 2) * 512:(dm % 2) * 512 + 512],
                                       self.W2B[:, (f2 % 4) * 2048 + ht * 128:
                                                   (f2 % 4) * 2048 + ht * 128 + 128],
                                       self.HTC[:, ht * 512:(ht + 1) * 512],
                                       start=(ht == 0), stop=(ht == 15))
                kb.tag(mm, "mpe", 1)

    def act(self, scalar, s):
        kb = self.kb
        e, sub = divmod(s, 4)
        if sub == 0:
            scalar.wait_ge(kb.sem("mpe"), self.mpe_e(e) + 1)
            if e >= 1:
                scalar.wait_ge(kb.sem("mdve"), self.mdve_e(e - 1) + 15)
            else:
                scalar.wait_ge(kb.sem("mdve"), self.mdve0)
            kb.tag(scalar.copy(out=self.CWB[:], in_=self.PCW[:, 0:512]), "mact", 1)
        if sub in (0, 1):
            for hh in range(8):
                ht = sub * 8 + hh
                scalar.wait_ge(kb.sem("mpe"), self.mpe_e(e) + 2 + ht)
                kb.tag(scalar.activation(out=self.HTC[:, ht * 512:(ht + 1) * 512],
                                         in_=self.PH[:, (ht % 2) * 512:(ht % 2) * 512 + 512],
                                         func=AF.Relu,
                                         bias=self.B1C[:, ht * 16 + e: ht * 16 + e + 1]), "mact")

    def dve(self, v2, s):
        kb = self.kb
        e, sub = divmod(s, 4)
        if sub in (2, 3):
            if sub == 2:
                v2.wait_ge(kb.sem("mact"), self.mact_e(e) + 1)
                if e == 0:
                    v2.wait_ge(kb.sem("gp"), self.gp_init)
            for dd in range(4):
                dm = (sub - 2) * 4 + dd
                v2.wait_ge(kb.sem("mpe"), self.mpe_e(e) + 18 + dm)
                kb.tag(v2.scalar_tensor_tensor(out=self.Y1[:],
                                               in0=self.PO[:, (dm % 2) * 512:(dm % 2) * 512 + 512],
                                               scalar=self.B2CQ[:, dm * 16 + e: dm * 16 + e + 1],
                                               in1=self.CWB[:],
                                               op0=ALU.add, op1=ALU.mult), "mdve")
                kb.tag(v2.tensor_add(self.ACC[:, dm * 512:(dm + 1) * 512],
                                     self.ACC[:, dm * 512:(dm + 1) * 512], self.Y1[:]), "mdve")

    def tail_act(self, scalar):
        kb = self.kb
        for dm in range(DT8):
            if dm == 0:
                scalar.wait_ge(kb.sem("mdve"), self.mdve_e(16))
            else:
                scalar.wait_ge(kb.sem("mgo"), self.mgo0 + 16 * dm)
            kb.tag(scalar.activation(out=self.Y1[:],
                                     in_=self.ACC[:, dm * 512:(dm + 1) * 512],
                                     func=AF.Silu), "mact")

    def tail_sync(self, sy):
        kb = self.kb
        for dm in range(DT8):
            sy.wait_ge(kb.sem("mact"), self.mact_e(16) + dm + 1)
            kb.tag(sy.dma_start(out=self.g_h[dm][:, self.q * 512:(self.q + 1) * 512],
                                in_=self.Y1[:]), "mgo", 16)


def sec_mamba(kb, hbm, l, xin_f, xin_b, xout_f, xout_b):
    nc = kb.nc
    S = 2051

    iw = hbm["L0_attn_in_proj_w"][l]
    cw = hbm["L0_attn_conv_w"][l]
    cb = hbm["L0_attn_conv_b"][l]
    xpw = hbm["L0_attn_x_proj_w"][l]
    dtw = hbm["L0_attn_dt_proj_w"][l]
    dtb = hbm["L0_attn_dt_proj_b"][l]
    alog = hbm["L0_attn_A_log"][l]
    dsk = hbm["L0_attn_D_skip"][l]
    ow = hbm["L0_attn_out_proj_w"][l]
    lnw = hbm["L0_attn_ln_w"][l]
    lnb = hbm["L0_attn_ln_b"][l]
    zb_h = hbm["zbuf"].rearrange("(n p) t -> n p t", p=128)
    xc_h = hbm["xcf"].rearrange("(n p) t -> n p t", p=128)
    yg_h = hbm["ygb"].rearrange("(n p) t -> n p t", p=128)
    yt_h = hbm["ytmp"].rearrange("(n p) t -> n p t", p=128)

    esS = ExitStack()
    EYEB = esS.enter_context(nc.sbuf_tensor(f"m{l}_eyeb", [128, 128], BF16))
    ONESC = esS.enter_context(nc.sbuf_tensor(f"m{l}_onesc", [128, 1], F32))
    ONESR = esS.enter_context(nc.sbuf_tensor(f"m{l}_onesr", [1, 128], F32))
    CWC = esS.enter_context(nc.sbuf_tensor(f"m{l}_cwc", [128, 16 * 4], F32))
    CBC = esS.enter_context(nc.sbuf_tensor(f"m{l}_cbc", [128, 16], F32))
    DTBC = esS.enter_context(nc.sbuf_tensor(f"m{l}_dtbc", [128, 16], F32))
    DSKC = esS.enter_context(nc.sbuf_tensor(f"m{l}_dskc", [128, 16], F32))
    LNWC = esS.enter_context(nc.sbuf_tensor(f"m{l}_lnwc", [128, 8], F32))
    LNBC = esS.enter_context(nc.sbuf_tensor(f"m{l}_lnbc", [128, 8], F32))
    ACOL = esS.enter_context(nc.sbuf_tensor(f"m{l}_acol", [128, 16 * 16], F32))
    ONESRB = esS.enter_context(nc.sbuf_tensor(f"m{l}_onesrb", [1, 128], BF16))
    esCD = ExitStack()
    SEL32 = esCD.enter_context(nc.sbuf_tensor(f"m{l}_sel32", [32, 32 * 128], BF16))
    DTWT = esCD.enter_context(nc.sbuf_tensor(f"m{l}_dtwt", [64, 16 * 128], BF16))
    DTRB = esCD.enter_context(nc.sbuf_tensor(f"m{l}_dtrb", [64, T], BF16))
    BCB = esCD.enter_context(nc.sbuf_tensor(f"m{l}_bcb", [32, T], BF16))

    with nc.Block() as block:
        @block.sync
        def _(sync):
            kb.tag(sync.dma_start(out=EYEB[:], in_=hbm["c_eye_b"]), "ds", 16)
            kb.tag(sync.dma_start(out=ONESC[:], in_=hbm["c_onescol_f"]), "ds", 16)
            kb.tag(sync.dma_start(out=ONESR[:], in_=hbm["c_onesrow_f"]), "ds", 16)
            kb.tag(sync.dma_start(out=ONESRB[:], in_=hbm["c_onesrow_b"]), "ds", 16)
            kb.tag(sync.dma_start(out=SEL32[:], in_=hbm["c_sel32_b"]), "ds", 16)
            kb.tag(sync.dma_start(out=CWC[:].rearrange("p (a k) -> p a k", k=4), in_=cw.rearrange("(a p) k -> p a k", p=128)), "ds", 16)
            for a in range(16):
                kb.tag(sync.dma_start(out=CBC[:, a:a + 1],
                                      in_=cb[a * 128:(a + 1) * 128].rearrange("(p o) -> p o", o=1)), "ds", 16)
                kb.tag(sync.dma_start(out=DTBC[:, a:a + 1],
                                      in_=dtb[a * 128:(a + 1) * 128].rearrange("(p o) -> p o", o=1)), "ds", 16)
                kb.tag(sync.dma_start(out=DSKC[:, a:a + 1],
                                      in_=dsk[a * 128:(a + 1) * 128].rearrange("(p o) -> p o", o=1)), "ds", 16)
            for a in range(8):
                kb.tag(sync.dma_start(out=LNWC[:, a:a + 1],
                                      in_=lnw[a * 128:(a + 1) * 128].rearrange("(p o) -> p o", o=1)), "ds", 16)
                kb.tag(sync.dma_start(out=LNBC[:, a:a + 1],
                                      in_=lnb[a * 128:(a + 1) * 128].rearrange("(p o) -> p o", o=1)), "ds", 16)
            kb.tag(sync.dma_start(out=ACOL[:].rearrange("p (a n) -> p a n", n=NST), in_=alog.rearrange("(a p) n -> p a n", p=128)), "ds", 16)
        small_ld = kb.val("ds")

        @block.scalar
        def _(scalar):
            scalar.wait_ge(kb.sem("ds"), small_ld)
            kb.tag(scalar.activation(out=ACOL[:], in_=ACOL[:], func=AF.Exp), "act", 1)
        act_acol = kb.val("act")

        @block.vector
        def _(vector):
            vector.wait_ge(kb.sem("act"), act_acol)
            kb.tag(vector.tensor_scalar(out=ACOL[:], in0=ACOL[:], scalar1=-1.0, scalar2=None,
                                        op0=ALU.mult), "dve", 1)

    esA = ExitStack()
    XT = esA.enter_context(nc.sbuf_tensor(f"m{l}_xt", [128, DT8 * T], BF16))
    XS = esA.enter_context(nc.sbuf_tensor(f"m{l}_xs", [128, 16 * S], BF16))
    WST = esA.enter_context(nc.sbuf_tensor(f"m{l}_wst", [128, 2 * 1024], BF16))
    WT = esA.enter_context(nc.sbuf_tensor(f"m{l}_wt", [128, 2 * 1024], BF16))
    ZR = esA.enter_context(nc.sbuf_tensor(f"m{l}_zr", [128, 2 * T], BF16))
    CACC = esA.enter_context(nc.sbuf_tensor(f"m{l}_cacc", [128, 2 * T], F32))
    XCF = esA.enter_context(nc.sbuf_tensor(f"m{l}_xcf", [128, 2 * T], F32))
    DBLT = esA.enter_context(nc.sbuf_tensor(f"m{l}_dblt", [128, T], F32))
    XPWT = esA.enter_context(nc.sbuf_tensor(f"m{l}_xpwt", [128, 16 * 96], BF16))
    PTB = esA.enter_context(nc.psum_tensor(f"m{l}_ptb", [128, 4 * 1024], BF16))
    PZ = esA.enter_context(nc.psum_tensor(f"m{l}_pz", [128, 2 * 512], F32))

    dg0, pe0, act0, dve0, ds1 = (kb.val(x) for x in ("dg", "pe", "act", "dve", "ds"))
    dz_base = [kb.val("dz0"), kb.val("dz1")]
    dxc_base = [kb.val("dxc0"), kb.val("dxc1")]
    dxt0 = kb.val("dxt")
    dw_base = [kb.val("dw0"), kb.val("dw1")]
    gp0 = kb.val("gp")
    with nc.Block() as block:
        @block.gpsimd
        def _(g):
            kb.tag(g.dma_start(out=XT[:].rearrange("p (n t) -> p n t", t=T),
                               in_=hbm[xin_b].rearrange("(n p) t -> p n t", p=128)), "dxt", 16)
            for a in range(15):
                g.memset(XS[:, a * S:a * S + 3], 0.0)
            kb.tag(g.memset(XS[:, 15 * S:15 * S + 3], 0.0), "gp", 1)
            for et in range(32):
                if et >= 2:
                    g.wait_ge(kb.sem("pe"), pe0 + (et - 2) * 12 + 8)
                kb.tag(g.dma_start(out=WST[:, (et % 2) * 1024:(et % 2) * 1024 + 1024],
                                   in_=iw[et * 128:(et + 1) * 128, :]), "dw0" if et % 2 == 0 else "dw1", 16)
            for a in range(16):
                g.wait_ge(kb.sem("act"), act0 + 384 + 2 * (a + 1))
                kb.tag(g.dma_start(out=xc_h[a], in_=XCF[:, (a % 2) * T:(a % 2) * T + T]),
                       "dxc0" if a % 2 == 0 else "dxc1", 16)

        @block.tensor
        def _(tensor):
            tensor.wait_ge(kb.sem("dxt"), dxt0 + 16)
            for et in range(32):
                tensor.wait_ge(kb.sem("dw0" if et % 2 == 0 else "dw1"),
                               dw_base[et % 2] + 16 * (et // 2 + 1))
                for dk in range(DT8):
                    j = et * 8 + dk
                    if j >= 4:
                        et4, dk4 = divmod(j - 4, DT8)
                        tensor.wait_ge(kb.sem("act"), act0 + et4 * 12 + dk4 + 1)
                    kb.tag(tensor.transpose(PTB[:, (j % 4) * 1024:(j % 4) * 1024 + 128],
                                            WST[:, (et % 2) * 1024 + dk * 128:(et % 2) * 1024 + dk * 128 + 128],
                                            EYEB[:]), "pe", 1)
                tensor.wait_ge(kb.sem("act"), act0 + et * 12 + 8)
                for tc in range(TC4):
                    mmi = et * 4 + tc
                    if mmi >= 2:
                        et2, tc2 = divmod(mmi - 2, TC4)
                        tensor.wait_ge(kb.sem("act"), act0 + et2 * 12 + 8 + tc2 + 1)
                    for dk in range(DT8):
                        mm = tensor.matmul(PZ[:, (mmi % 2) * 512:(mmi % 2) * 512 + 512],
                                           WT[:, (et % 2) * 1024 + dk * 128:(et % 2) * 1024 + dk * 128 + 128],
                                           XT[:, dk * T + tc * 512: dk * T + tc * 512 + 512],
                                           start=(dk == 0), stop=(dk == DT8 - 1))
                    kb.tag(mm, "pe", 1)

        @block.scalar
        def _(scalar):
            for et in range(32):
                for dk in range(DT8):
                    scalar.wait_ge(kb.sem("pe"), pe0 + et * 12 + dk + 1)
                    j = et * 8 + dk
                    kb.tag(scalar.copy(out=WT[:, (et % 2) * 1024 + dk * 128:(et % 2) * 1024 + dk * 128 + 128],
                                       in_=PTB[:, (j % 4) * 1024:(j % 4) * 1024 + 128]), "act", 1)
                for tc in range(TC4):
                    mmi = et * 4 + tc
                    scalar.wait_ge(kb.sem("pe"), pe0 + et * 12 + 8 + tc + 1)
                    if et >= 18 and tc == 0:
                        zi = et - 16
                        scalar.wait_ge(kb.sem("dz0" if zi % 2 == 0 else "dz1"),
                                       dz_base[zi % 2] + 16 * ((zi - 2) // 2 + 1))
                    if et < 16:
                        dst = XS[:, et * S + 3 + tc * 512: et * S + 3 + tc * 512 + 512]
                    else:
                        dst = ZR[:, (et % 2) * T + tc * 512:(et % 2) * T + tc * 512 + 512]
                    kb.tag(scalar.copy(dst, PZ[:, (mmi % 2) * 512:(mmi % 2) * 512 + 512]), "act", 1)

        @block.vector
        def _(vector):
            vector.wait_ge(kb.sem("gp"), gp0 + 1)
            for a in range(16):
                base = a * S
                sl_out = CACC[:, (a % 2) * T:(a % 2) * T + T]
                vector.wait_ge(kb.sem("act"), act0 + a * 12 + 12)
                if a >= 2:
                    vector.wait_ge(kb.sem("act"), act0 + 384 + 2 * (a - 1))
                tprev = kb.tag(vector.tensor_scalar(out=sl_out, in0=XS[:, base + 0: base + 0 + T],
                                                    scalar1=CWC[:, a * 4 + 0:a * 4 + 1],
                                                    scalar2=CBC[:, a:a + 1],
                                                    op0=ALU.mult, op1=ALU.add), "dve")
                for k in range(1, 4):
                    vector.wait_ge(kb.sem("dve"), tprev)
                    tprev = kb.tag(vector.scalar_tensor_tensor(out=sl_out, in0=XS[:, base + k: base + k + T],
                                                               scalar=CWC[:, a * 4 + k:a * 4 + k + 1],
                                                               in1=sl_out, op0=ALU.mult, op1=ALU.add), "dve")

        @block.scalar
        def _(scalar):
            for a in range(16):
                scalar.wait_ge(kb.sem("dve"), dve0 + 4 * (a + 1))
                if a >= 2:
                    scalar.wait_ge(kb.sem("dxc0" if a % 2 == 0 else "dxc1"),
                                   dxc_base[a % 2] + 16 * ((a - 2) // 2 + 1))
                kb.tag(scalar.activation(out=XS[:, a * S + 3: a * S + 3 + T],
                                         in_=CACC[:, (a % 2) * T:(a % 2) * T + T], func=AF.Silu), "act", 1)
                kb.tag(scalar.activation(out=XCF[:, (a % 2) * T:(a % 2) * T + T],
                                         in_=CACC[:, (a % 2) * T:(a % 2) * T + T], func=AF.Silu), "act", 1)

        @block.sync
        def _(sync):
            for zi in range(16):
                et = 16 + zi
                sync.wait_ge(kb.sem("act"), act0 + et * 12 + 12)
                kb.tag(sync.dma_start(out=zb_h[zi], in_=ZR[:, (et % 2) * T:(et % 2) * T + T]),
                       "dz0" if zi % 2 == 0 else "dz1", 16)

    dg2, pe2, act2, dve2, ds3 = (kb.val(x) for x in ("dg", "pe", "act", "dve", "ds"))
    with nc.Block() as block:
        @block.gpsimd
        def _(g):
            kb.tag(g.dma_start(out=WST[0:96, 0:DI], in_=xpw), "dg", 16)
            kb.tag(g.dma_start(out=WT[:, 0:16 * 64].rearrange("p (a r) -> p a r", r=DTR), in_=dtw.rearrange("(a p) r -> p a r", p=128)), "dg", 16)

        @block.tensor
        def _(tensor):
            tensor.wait_ge(kb.sem("dg"), dg2 + 32)
            for j in range(16):
                if j >= 4:
                    tensor.wait_ge(kb.sem("act"), act2 + (j - 3))
                kb.tag(tensor.transpose(PTB[:, (j % 4) * 1024:(j % 4) * 1024 + 96],
                                        WST[0:96, j * 128:(j + 1) * 128], EYEB[0:96, 0:96]), "pe", 1)
            for j in range(16):
                jj = 16 + j
                tensor.wait_ge(kb.sem("act"), act2 + jj - 3)
                kb.tag(tensor.transpose(PTB[0:64, (jj % 4) * 1024:(jj % 4) * 1024 + 128],
                                        WT[:, j * 64:(j + 1) * 64], EYEB[:]), "pe", 1)
            tensor.wait_ge(kb.sem("act"), act2 + 32)
            for tc in range(TC4):
                if tc >= 2:
                    tensor.wait_ge(kb.sem("act"), act2 + 32 + tc - 1)
                for ht in range(16):
                    mm = tensor.matmul(PZ[0:96, (tc % 2) * 512:(tc % 2) * 512 + 512],
                                       XPWT[:, ht * 96:(ht + 1) * 96],
                                       XS[:, ht * S + 3 + tc * 512: ht * S + 3 + tc * 512 + 512],
                                       start=(ht == 0), stop=(ht == 15))
                kb.tag(mm, "pe", 1)

        @block.scalar
        def _(scalar):
            for j in range(16):
                scalar.wait_ge(kb.sem("pe"), pe2 + j + 1)
                kb.tag(scalar.copy(out=XPWT[:, j * 96:(j + 1) * 96],
                                   in_=PTB[:, (j % 4) * 1024:(j % 4) * 1024 + 96]), "act", 1)
            for j in range(16):
                jj = 16 + j
                scalar.wait_ge(kb.sem("pe"), pe2 + jj + 1)
                kb.tag(scalar.copy(out=DTWT[:, j * 128:(j + 1) * 128],
                                   in_=PTB[0:64, (jj % 4) * 1024:(jj % 4) * 1024 + 128]), "act", 1)
            for tc in range(TC4):
                scalar.wait_ge(kb.sem("pe"), pe2 + 32 + tc + 1)
                kb.tag(scalar.copy(out=DBLT[0:96, tc * 512:(tc + 1) * 512],
                                   in_=PZ[0:96, (tc % 2) * 512:(tc % 2) * 512 + 512]), "act", 1)

        @block.vector
        def _(vector):
            vector.wait_ge(kb.sem("act"), act2 + 36)
            kb.tag(vector.tensor_copy(DTRB[:], DBLT[0:64, :]), "dve", 1)
            kb.tag(vector.tensor_copy(BCB[:], DBLT[64:96, :]), "dve", 1)

        kb.drain_body(block)

    esA.close()
    esD = ExitStack()
    BBC = esD.enter_context(nc.sbuf_tensor(f"m{l}_bbc", [128, 8 * T], BF16))
    DELTA = esD.enter_context(nc.sbuf_tensor(f"m{l}_delta", [128, T], BF16))
    SPB = esD.enter_context(nc.sbuf_tensor(f"m{l}_spb", [128, 512], F32))
    DU = esD.enter_context(nc.sbuf_tensor(f"m{l}_du", [128, T], BF16))
    DAB = esD.enter_context(nc.sbuf_tensor(f"m{l}_dab", [128, 2 * T], F32))
    BSC = esD.enter_context(nc.sbuf_tensor(f"m{l}_bsc", [128, T], BF16))
    HH = esD.enter_context(nc.sbuf_tensor(f"m{l}_hh", [128, T], BF16))
    PP = esD.enter_context(nc.sbuf_tensor(f"m{l}_pp", [128, T], BF16))
    YY = esD.enter_context(nc.sbuf_tensor(f"m{l}_yy", [128, 2 * T], BF16))
    UB = esD.enter_context(nc.sbuf_tensor(f"m{l}_ub", [128, 2 * T], BF16))
    ZBL = esD.enter_context(nc.sbuf_tensor(f"m{l}_zbl", [128, T], BF16))
    YGS = esD.enter_context(nc.sbuf_tensor(f"m{l}_ygs", [128, T], BF16))
    PBC = esD.enter_context(nc.psum_tensor(f"m{l}_pbc", [128, 2 * 512], F32))
    PDL = esD.enter_context(nc.psum_tensor(f"m{l}_pdl", [128, 512], F32))

    esM = ExitStack()
    mo = Moe0Slots(kb, hbm, l, esM, ONESR, ONESRB)

    dyt_pass = {}
    dyr_pass = {}
    with nc.Block() as block:
        @block.sync
        def _(sy):
            mo.preamble_sync(sy)

        @block.gpsimd
        def _(g):
            mo.preamble_gp(g)

        for pss in range(4):
            pe3, act3, dve3 = kb.val("pe"), kb.val("act"), kb.val("dve")
            dyt_pass[pss] = kb.val("dyt")
            dyr_pass[pss] = kb.val("dyr")

            @block.tensor
            def _(tensor):
                for j in range(32):
                    ni, r = divmod(j, 8)
                    bc, tcq = divmod(r, 4)
                    row = (16 if bc else 0) + pss * 4 + ni
                    if pss == 0:
                        tensor.wait_ge(kb.sem("dve"), dve3)
                    else:
                        tensor.wait_ge(kb.sem("dve"), dve3 - 17 + 4 * ni + 4)
                    if j >= 2:
                        tensor.wait_ge(kb.sem("act"), act3 + j - 1)
                    kb.tag(tensor.matmul(PBC[:, (j % 2) * 512:(j % 2) * 512 + 512],
                                         SEL32[:, row * 128:(row + 1) * 128],
                                         BCB[:, tcq * 512:(tcq + 1) * 512],
                                         start=True, stop=True), "pe", 1)

            @block.scalar
            def _(scalar):
                for j in range(32):
                    ni, r = divmod(j, 8)
                    bc, tcq = divmod(r, 4)
                    scalar.wait_ge(kb.sem("pe"), pe3 + j + 1)
                    kb.tag(scalar.copy(out=BBC[:, (2 * ni + bc) * T + tcq * 512:(2 * ni + bc) * T + tcq * 512 + 512],
                                       in_=PBC[:, (j % 2) * 512:(j % 2) * 512 + 512]), "act", 1)
            pe_bc = pe3 + 32
            act_bc = act3 + 32

            SP_OFF = 1 if pss == 0 else 0
            ACT_PD = 8 + 4 + (1 if pss in (0, 3) else 0)
            DVE_PD = 17 + (1 if pss == 3 else 0)

            def pe_delta_tag(dblk, tc):
                return pe_bc + dblk * 4 + tc + 1

            def act_base(dblk):
                return act_bc + dblk * ACT_PD

            def act_exp_tag(dblk, tc):
                return act_base(dblk) + SP_OFF + 2 * tc + 1

            def act_sp_tag(dblk, tc):
                return act_base(dblk) + SP_OFF + 2 * tc + 2

            def act_da_tag(dblk, ni):
                return act_base(dblk) + SP_OFF + 8 + ni + 1

            def dve_base(dblk):
                return dve3 + dblk * DVE_PD

            def dve_scan_tag(dblk, ni):
                return dve_base(dblk) + 4 * ni + 3

            du0 = kb.val("du_")
            dzr0 = kb.val("dzr")
            dyg0 = kb.val("dyg")

            @block.tensor
            def _(tensor2):
                for dblk in range(16):
                    s = pss * 16 + dblk
                    for tc in range(TC4):
                        gi = dblk * 4 + tc
                        if gi >= 1:
                            tensor2.wait_ge(kb.sem("act"), act_exp_tag((gi - 1) // 4, (gi - 1) % 4))
                        else:
                            tensor2.wait_ge(kb.sem("act"), act3)
                        mm = tensor2.matmul(PDL[:, 0:512],
                                            DTWT[:, dblk * 128:(dblk + 1) * 128],
                                            DTRB[:, tc * 512:(tc + 1) * 512],
                                            start=True, stop=True)
                        kb.tag(mm, "pe", 1)
                    if dblk >= 1:
                        mo.pe(tensor2, s - 1)
                mo.pe(tensor2, pss * 16 + 15)

            @block.scalar
            def _(scalar2):
                for dblk in range(16):
                    s = pss * 16 + dblk
                    ab = act_base(dblk)
                    if pss == 0:
                        scalar2.wait_ge(kb.sem("du_"), du0 + 16 * (max(dblk, 1) + 1))
                        if dblk >= 2:
                            scalar2.wait_ge(kb.sem("dyt"), dyt_pass[0] + 16 * (dblk - 1))
                        kb.tag(scalar2.activation(out=YY[:, (dblk % 2) * T:(dblk % 2) * T + T],
                                                  in_=UB[:, (dblk % 2) * T:(dblk % 2) * T + T],
                                                  func=AF.Copy,
                                                  scale=DSKC[:, dblk:dblk + 1]), "act")
                    if dblk >= 1:
                        scalar2.wait_ge(kb.sem("dve"), dve_base(dblk - 1) + (18 if pss == 3 else 1))
                    else:
                        scalar2.wait_ge(kb.sem("dve"), dve3)
                    for tc in range(TC4):
                        scalar2.wait_ge(kb.sem("pe"), pe_delta_tag(dblk, tc))
                        kb.tag(scalar2.activation(out=SPB[:],
                                                  in_=PDL[:, 0:512],
                                                  func=AF.Exp, bias=DTBC[:, dblk:dblk + 1]), "act")
                        kb.tag(scalar2.activation(out=DELTA[:, tc * 512:(tc + 1) * 512],
                                                  in_=SPB[:], func=AF.Ln, bias=1.0), "act")
                    for ni in range(4):
                        n = pss * 4 + ni
                        if ni >= 2:
                            scalar2.wait_ge(kb.sem("dve"), dve_scan_tag(dblk, ni - 2))
                        elif dblk >= 1:
                            scalar2.wait_ge(kb.sem("dve"), dve_scan_tag(dblk - 1, 2 + ni))
                        else:
                            scalar2.wait_ge(kb.sem("dve"), dve3)
                        kb.tag(scalar2.activation(out=DAB[:, (ni % 2) * T:(ni % 2) * T + T],
                                                  in_=DELTA[:],
                                                  func=AF.Exp,
                                                  scale=ACOL[:, dblk * 16 + n: dblk * 16 + n + 1]), "act")
                    if pss == 3:
                        scalar2.wait_ge(kb.sem("dzr"), dzr0 + 16 * (dblk + 1))
                        scalar2.wait_ge(kb.sem("dve"), dve_base(dblk) + 1)
                        kb.tag(scalar2.activation(out=DELTA[:], in_=ZBL[:],
                                                  func=AF.Silu), "act")
                    if dblk >= 1:
                        mo.act(scalar2, s - 1)
                mo.act(scalar2, pss * 16 + 15)
                if pss == 3:
                    mo.tail_act(scalar2)

            @block.vector
            def _(v2):
                for dblk in range(16):
                    s = pss * 16 + dblk
                    db = dve_base(dblk)
                    if dblk >= 2:
                        mo.dve(v2, s - 2)
                    v2.wait_ge(kb.sem("act"), act_sp_tag(dblk, 3))
                    v2.wait_ge(kb.sem("du_"), du0 + 16 * (max(dblk, 1) + 1))
                    kb.tag(v2.tensor_mul(DU[:], DELTA[:], UB[:, (dblk % 2) * T:(dblk % 2) * T + T]), "dve")
                    for ni in range(4):
                        kb.tag(v2.tensor_tensor(out=BSC[:],
                                                in0=DU[:], in1=BBC[:, (2 * ni) * T:(2 * ni) * T + T],
                                                op=ALU.mult), "dve")
                        v2.wait_ge(kb.sem("act"), act_da_tag(dblk, ni))
                        kb.tag(v2.tensor_tensor_scan(out=HH[:], data0=DAB[:, (ni % 2) * T:(ni % 2) * T + T],
                                                     data1=BSC[:],
                                                     initial=0.0, op0=ALU.mult, op1=ALU.add), "dve")
                        kb.tag(v2.tensor_tensor(out=PP[:],
                                                in0=HH[:], in1=BBC[:, (2 * ni + 1) * T:(2 * ni + 1) * T + T],
                                                op=ALU.mult), "dve")
                        if ni == 0:
                            if pss == 0:
                                v2.wait_ge(kb.sem("act"), act_base(dblk) + 1)
                            else:
                                v2.wait_ge(kb.sem("dyr"), dyr_pass[pss] + 16 * (dblk + 1))
                        kb.tag(v2.tensor_tensor(out=YY[:, (dblk % 2) * T:(dblk % 2) * T + T],
                                                in0=YY[:, (dblk % 2) * T:(dblk % 2) * T + T],
                                                in1=PP[:], op=ALU.add), "dve")
                    if pss == 3:
                        v2.wait_ge(kb.sem("act"), act_base(dblk) + 13)
                        if dblk >= 1:
                            v2.wait_ge(kb.sem("dyg"), dyg0 + 16 * dblk)
                        kb.tag(v2.tensor_mul(YGS[:],
                                             YY[:, (dblk % 2) * T:(dblk % 2) * T + T],
                                             DELTA[:]), "dve")
                mo.dve(v2, pss * 16 + 14)
                mo.dve(v2, pss * 16 + 15)

            @block.gpsimd
            def _(g3):
                if pss == 0:
                    g3.wait_ge(kb.sem("dxc0"), kb.val("dxc0"))
                    g3.wait_ge(kb.sem("dxc1"), kb.val("dxc1"))
                g3.wait_ge(kb.sem("dve"), dve3)
                g3.wait_ge(kb.sem("act"), act3)
                kb.tag(g3.dma_start(out=UB[:, 0:T], in_=xc_h[0]), "du_", 16)
                kb.tag(g3.dma_start(out=UB[:, T:2 * T], in_=xc_h[1]), "du_", 16)
                for dblk in range(16):
                    s = pss * 16 + dblk
                    if dblk + 2 < 16:
                        g3.wait_ge(kb.sem("dve"), dve_base(dblk) + 1)
                        if pss == 0:
                            g3.wait_ge(kb.sem("act"), act_base(dblk) + 1)
                        kb.tag(g3.dma_start(out=UB[:, (dblk % 2) * T:(dblk % 2) * T + T],
                                            in_=xc_h[dblk + 2]), "du_", 16)
                    if s + 2 <= 63:
                        mo.w1_fill(g3, s + 2)
                    for f2 in (2 * s, 2 * s + 1):
                        if f2 <= 127:
                            mo.w2_fill(g3, f2)
                    if (s + 2) % 4 == 0 and (s + 2) // 4 <= 15:
                        mo.cw_fill(g3, (s + 2) // 4)

            @block.sync
            def _(sy):
                if pss >= 1:
                    sy.wait_ge(kb.sem("dyt"), dyt_pass[pss - 1] + 16 * 15)
                    kb.tag(sy.dma_start(out=YY[:, 0:T], in_=yt_h[0]), "dyr", 16)
                if pss == 3:
                    sy.wait_ge(kb.sem("dz0"), kb.val("dz0"))
                    sy.wait_ge(kb.sem("dz1"), kb.val("dz1"))
                    kb.tag(sy.dma_start(out=ZBL[:], in_=zb_h[0]), "dzr", 16)
                for dblk in range(16):
                    if pss >= 1 and dblk + 1 < 16:
                        d = dblk + 1
                        if d == 1:
                            sy.wait_ge(kb.sem("dyt"), dyt_pass[pss - 1] + 256)
                        elif pss < 3:
                            sy.wait_ge(kb.sem("dyt"), dyt_pass[pss] + 16 * (d - 1))
                        else:
                            sy.wait_ge(kb.sem("dyt"), dyt_pass[2] + 16 * (d + 1))
                            sy.wait_ge(kb.sem("dve"), dve_base(d - 2) + 18)
                        kb.tag(sy.dma_start(out=YY[:, (d % 2) * T:(d % 2) * T + T],
                                            in_=yt_h[d]), "dyr", 16)
                    if pss == 3 and dblk + 1 < 16:
                        sy.wait_ge(kb.sem("act"), act_base(dblk) + 13)
                        kb.tag(sy.dma_start(out=ZBL[:], in_=zb_h[dblk + 1]), "dzr", 16)
                    if pss < 3:
                        sy.wait_ge(kb.sem("dve"), dve_base(dblk) + 17)
                        kb.tag(sy.dma_start(out=yt_h[dblk], in_=YY[:, (dblk % 2) * T:(dblk % 2) * T + T]),
                               "dyt", 16)
                    else:
                        sy.wait_ge(kb.sem("dve"), dve_base(dblk) + 18)
                        kb.tag(sy.dma_start(out=yg_h[dblk], in_=YGS[:]), "dyg", 16)
                if pss == 3:
                    mo.tail_sync(sy)

            if pss == 3:
                kb.drain_body(block)

    esM.close()
    esD.close()
    esCD.close()


    esF = ExitStack()
    YG = esF.enter_context(nc.sbuf_tensor(f"m{l}_yg", [128, 16 * T], BF16))
    V = esF.enter_context(nc.sbuf_tensor(f"m{l}_v", [128, DT8 * T], F32))
    WSTF = esF.enter_context(nc.sbuf_tensor(f"m{l}_wstf", [128, 2 * DI], BF16))
    OWT = esF.enter_context(nc.sbuf_tensor(f"m{l}_owt", [128, 2 * DI], BF16))
    XCUR = esF.enter_context(nc.sbuf_tensor(f"m{l}_xcur", [128, T], F32))
    SQ = esF.enter_context(nc.sbuf_tensor(f"m{l}_sq", [128, 2 * 512], F32))
    ROWS = esF.enter_context(nc.sbuf_tensor(f"m{l}_rows", [1, 4 * T], F32))
    XOF = esF.enter_context(nc.sbuf_tensor(f"m{l}_xof", [128, T], F32))
    XOB = esF.enter_context(nc.sbuf_tensor(f"m{l}_xob", [128, T], BF16))
    PTB2 = esF.enter_context(nc.psum_tensor(f"m{l}_ptb2", [128, 2 * 1024], BF16))
    PO = esF.enter_context(nc.psum_tensor(f"m{l}_po", [128, 2 * 512], F32))
    PS1 = esF.enter_context(nc.psum_tensor(f"m{l}_ps1", [1, 512], F32))
    PS2 = esF.enter_context(nc.psum_tensor(f"m{l}_ps2", [1, 512], F32))
    PBa = esF.enter_context(nc.psum_tensor(f"m{l}_pba", [128, 512], F32))
    PBb = esF.enter_context(nc.psum_tensor(f"m{l}_pbb", [128, 512], F32))

    dg4, pe4, act4, dve4, ds5 = (kb.val(x) for x in ("dg", "pe", "act", "dve", "ds"))
    with nc.Block() as block:
        def pe_wst_done(r):
            return pe4 + r * 20 + 16

        def dve_v_done(r):
            return dve4 + (r + 1) * 4

        dxtF = kb.val("dxt")
        dwF = [kb.val("dw0"), kb.val("dw1")]
        duF = kb.val("du_")

        @block.gpsimd
        def _(g):
            g.wait_ge(kb.sem("dyg"), kb.val("dyg"))
            kb.tag(g.dma_start(out=YG[:].rearrange("p (n t) -> p n t", t=T),
                               in_=hbm["ygb"].rearrange("(n p) t -> p n t", p=128)), "dxt", 16)
            for r in range(DT8):
                if r >= 2:
                    g.wait_ge(kb.sem("pe"), pe_wst_done(r - 2))
                kb.tag(g.dma_start(out=WSTF[:, (r % 2) * DI:(r % 2) * DI + DI],
                                   in_=ow[r * 128:(r + 1) * 128, :]), "dw0" if r % 2 == 0 else "dw1", 16)
                if r >= 1:
                    g.wait_ge(kb.sem("dve"), dve_v_done(r - 1))
                kb.tag(g.dma_start(out=XCUR[:],
                                   in_=hbm[xin_f].rearrange("(n p) t -> n p t", p=128)[r]), "du_", 16)

        @block.tensor
        def _(tensor):
            tensor.wait_ge(kb.sem("dxt"), dxtF + 16)
            for r in range(DT8):
                tensor.wait_ge(kb.sem("dw0" if r % 2 == 0 else "dw1"),
                               dwF[r % 2] + 16 * (r // 2 + 1))
                for j in range(16):
                    jj = r * 16 + j
                    if jj >= 2:
                        tensor.wait_ge(kb.sem("act"), act4 + jj - 1)
                    kb.tag(tensor.transpose(PTB2[:, (jj % 2) * 1024:(jj % 2) * 1024 + 128],
                                            WSTF[:, (r % 2) * DI + j * 128:(r % 2) * DI + j * 128 + 128],
                                            EYEB[:]), "pe", 1)
                tensor.wait_ge(kb.sem("act"), act4 + (r + 1) * 16)
                for tc in range(TC4):
                    mmi = r * 4 + tc
                    if mmi >= 2:
                        tensor.wait_ge(kb.sem("dve"), dve4 + mmi - 1)
                    for ki in range(16):
                        mm = tensor.matmul(PO[:, (mmi % 2) * 512:(mmi % 2) * 512 + 512],
                                           OWT[:, (r % 2) * DI + ki * 128:(r % 2) * DI + ki * 128 + 128],
                                           YG[:, ki * T + tc * 512: ki * T + tc * 512 + 512],
                                           start=(ki == 0), stop=(ki == 15))
                    kb.tag(mm, "pe", 1)

        @block.scalar
        def _(scalar):
            for r in range(DT8):
                for j in range(16):
                    jj = r * 16 + j
                    scalar.wait_ge(kb.sem("pe"), pe4 + r * 20 + j + 1)
                    kb.tag(scalar.copy(out=OWT[:, (r % 2) * DI + j * 128:(r % 2) * DI + j * 128 + 128],
                                       in_=PTB2[:, (jj % 2) * 1024:(jj % 2) * 1024 + 128]), "act", 1)

        @block.vector
        def _(vector):
            for r in range(DT8):
                for tc in range(TC4):
                    mmi = r * 4 + tc
                    vector.wait_ge(kb.sem("pe"), pe4 + r * 20 + 16 + tc + 1)
                    vector.wait_ge(kb.sem("du_"), duF + 16 * (r + 1))
                    kb.tag(vector.tensor_add(V[:, r * T + tc * 512: r * T + tc * 512 + 512],
                                             PO[:, (mmi % 2) * 512:(mmi % 2) * 512 + 512],
                                             XCUR[:, tc * 512:tc * 512 + 512]), "dve")

    emit_ln_tail(kb, hbm, V, xout_f, xout_b, LNWC, LNBC,
                 ONESC, ONESR, SQ, ROWS, XOF, XOB, PS1, PS2, PBa, PBb)
    esF.close()
    esS.close()




def sec_gate0(kb, hbm):
    nc = kb.nc
    Wg, bg, bias = hbm["L0_ffn_Wg"], hbm["L0_ffn_bg"], hbm["L0_ffn_bias"]
    b1, b2 = hbm["L0_ffn_b1"], hbm["L0_ffn_b2"]

    esS = ExitStack()
    EYE = esS.enter_context(nc.sbuf_tensor("g_eye", [128, 128], F32))
    ONESR = esS.enter_context(nc.sbuf_tensor("g_onesr", [1, 128], F32))
    SEL16 = esS.enter_context(nc.sbuf_tensor("g_sel16", [16, 16 * 128], F32))
    XN = esS.enter_context(nc.sbuf_tensor("g_xn", [128, DT8 * T], BF16))
    CWT = esS.enter_context(nc.sbuf_tensor("g_cwt", [16, T], F32))
    B1C = esS.enter_context(nc.sbuf_tensor("g_b1c", [128, 16 * 16], F32))

    ds0 = kb.val("ds")
    pe0, act0, dve0, gp0 = (kb.val(x) for x in ("pe", "act", "dve", "gp"))
    dxt0 = kb.val("dxt")

    with nc.Block() as block, ExitStack() as esG0:
        WGB = esG0.enter_context(nc.sbuf_tensor("g_wgb", [128, DT8 * 16], BF16))
        BGB = esG0.enter_context(nc.sbuf_tensor("g_bgb", [128, 16], F32))
        RB = esG0.enter_context(nc.sbuf_tensor("g_rb", [1, 64], F32))
        B1S = esG0.enter_context(nc.sbuf_tensor("g_b1s", [16, HEX], F32))
        LGA = esG0.enter_context(nc.sbuf_tensor("g_lga", [128, 16 * 16], F32))
        MXA = esG0.enter_context(nc.sbuf_tensor("g_mxa", [128, 16 * 2], F32))
        DCOL = esG0.enter_context(nc.sbuf_tensor("g_dcol", [128, 16], F32))
        W2A = esG0.enter_context(nc.sbuf_tensor("g_w2a", [128, 16], F32))
        W1A = esG0.enter_context(nc.sbuf_tensor("g_w1a", [128, 16], F32))
        MSK = esG0.enter_context(nc.sbuf_tensor("g_msk", [128, 3 * 16], F32))
        CWA = esG0.enter_context(nc.sbuf_tensor("g_cwa", [128, 16 * 16], F32))
        PL = esG0.enter_context(nc.psum_tensor("g_pl", [128, 2 * 512], F32))
        PT = esG0.enter_context(nc.psum_tensor("g_pt", [128, 2 * 512], F32))

        @block.sync
        def _(sync):
            kb.tag(sync.dma_start(out=EYE[:], in_=hbm["c_eye_f"]), "ds", 16)
            kb.tag(sync.dma_start(out=ONESR[:], in_=hbm["c_onesrow_f"]), "ds", 16)
            kb.tag(sync.dma_start(out=SEL16[:], in_=hbm["c_sel16_f"]), "ds", 16)
            kb.tag(sync.dma_start(out=RB[:, 0:16], in_=bg), "ds", 16)
            kb.tag(sync.dma_start(out=RB[:, 16:32], in_=bias), "ds", 16)
            kb.tag(sync.dma_start(out=B1S[:], in_=b1), "ds", 16)
        g0_ld = kb.val("ds")

        @block.gpsimd
        def _(g):
            kb.tag(g.dma_start(out=WGB[:].rearrange("p (a e) -> p a e", e=16),
                               in_=Wg.rearrange("(a p) e -> p a e", p=128)), "dg", 16)
            kb.tag(g.dma_start(out=XN[:].rearrange("p (n t) -> p n t", t=T),
                               in_=hbm["xn0b"].rearrange("(n p) t -> p n t", p=128)), "dxt", 16)

        @block.vector
        def _(vector):
            vector.wait_ge(kb.sem("ds"), g0_ld)
            kb.tag(vector.tensor_add(RB[:, 32:48], RB[:, 0:16], RB[:, 16:32]), "dve", 1)

        @block.tensor
        def _(tensor):
            tensor.wait_ge(kb.sem("dve"), dve0 + 1)
            kb.tag(tensor.matmul(PL[:, 0:16], ONESR[:], RB[:, 32:48], start=True, stop=True), "pe", 1)
            for j in range(16):
                if j >= 2:
                    tensor.wait_ge(kb.sem("act"), act0 + j)
                kb.tag(tensor.transpose(PT[:, (j % 2) * 512:(j % 2) * 512 + 16],
                                        B1S[:, j * 128:(j + 1) * 128], EYE[0:16, 0:16]), "pe", 1)
            tensor.wait_ge(kb.sem("dxt"), dxt0 + 16)
            tensor.wait_ge(kb.sem("dg"), kb.val("dg"))
            tensor.wait_ge(kb.sem("act"), act0 + 1)
            for tt in range(TT16):
                if tt >= 2:
                    tensor.wait_ge(kb.sem("act"), act0 + 17 + tt - 1)
                for dk in range(DT8):
                    mm = tensor.matmul(PL[:, (tt % 2) * 512:(tt % 2) * 512 + 16],
                                       XN[:, dk * T + tt * 128: dk * T + tt * 128 + 128],
                                       WGB[:, dk * 16:(dk + 1) * 16],
                                       start=(dk == 0), stop=(dk == DT8 - 1))
                kb.tag(mm, "pe", 1)

        @block.scalar
        def _(scalar):
            scalar.wait_ge(kb.sem("pe"), pe0 + 1)
            kb.tag(scalar.copy(out=BGB[:], in_=PL[:, 0:16]), "act", 1)
            for j in range(16):
                scalar.wait_ge(kb.sem("pe"), pe0 + 1 + j + 1)
                kb.tag(scalar.copy(out=B1C[:, j * 16:(j + 1) * 16],
                                   in_=PT[:, (j % 2) * 512:(j % 2) * 512 + 16]), "act", 1)
            for tt in range(TT16):
                scalar.wait_ge(kb.sem("pe"), pe0 + 17 + tt + 1)
                kb.tag(scalar.copy(out=LGA[:, tt * 16:(tt + 1) * 16],
                                   in_=PL[:, (tt % 2) * 512:(tt % 2) * 512 + 16]), "act", 1)
        act_g = kb.val("act")

        @block.vector
        def _(vector):
            vector.wait_ge(kb.sem("act"), act_g)
            for tt in range(TT16):
                sl = slice(tt * 16, (tt + 1) * 16)
                t1 = kb.tag(vector.tensor_add(LGA[:, sl], LGA[:, sl], BGB[:]), "dve")
                vector.wait_ge(kb.sem("dve"), t1)
                t2 = kb.tag(vector.max(out=MSK[:, 0:8], in_=LGA[:, sl]), "dve")
                vector.wait_ge(kb.sem("dve"), t2)
                t3 = kb.tag(vector.tensor_copy(MXA[:, tt * 2:tt * 2 + 2], MSK[:, 0:2]), "dve")
                vector.wait_ge(kb.sem("dve"), t3)
                t4 = kb.tag(vector.tensor_sub(DCOL[:, tt:tt + 1], MSK[:, 1:2], MSK[:, 0:1]), "dve")
                vector.wait_ge(kb.sem("dve"), t4)
        dve_a = kb.val("dve")

        @block.scalar
        def _(scalar):
            scalar.wait_ge(kb.sem("dve"), dve_a)
            kb.tag(scalar.activation(out=W2A[:], in_=DCOL[:], func=AF.Sigmoid), "act", 1)
        act_sig = kb.val("act")

        @block.vector
        def _(vector):
            vector.wait_ge(kb.sem("act"), act_sig)
            t0 = kb.tag(vector.tensor_scalar(out=W1A[:], in0=W2A[:], scalar1=-1.0, scalar2=1.0,
                                             op0=ALU.mult, op1=ALU.add), "dve")
            vector.wait_ge(kb.sem("dve"), t0)
            for tt in range(TT16):
                sl = slice(tt * 16, (tt + 1) * 16)
                t1 = kb.tag(vector.tensor_scalar(out=MSK[:, 16:32], in0=LGA[:, sl],
                                                 scalar1=MXA[:, tt * 2:tt * 2 + 1], scalar2=None,
                                                 op0=ALU.is_ge), "dve")
                t2 = kb.tag(vector.tensor_scalar(out=MSK[:, 32:48], in0=LGA[:, sl],
                                                 scalar1=MXA[:, tt * 2 + 1:tt * 2 + 2], scalar2=None,
                                                 op0=ALU.is_ge), "dve")
                vector.wait_ge(kb.sem("dve"), t2)
                t3 = kb.tag(vector.tensor_sub(MSK[:, 32:48], MSK[:, 32:48], MSK[:, 16:32]), "dve")
                vector.wait_ge(kb.sem("dve"), t3)
                t4 = kb.tag(vector.tensor_scalar(out=CWA[:, sl], in0=MSK[:, 16:32],
                                                 scalar1=W1A[:, tt:tt + 1], scalar2=None,
                                                 op0=ALU.mult), "dve")
                vector.wait_ge(kb.sem("dve"), t4)
                t5 = kb.tag(vector.scalar_tensor_tensor(out=CWA[:, sl], in0=MSK[:, 32:48],
                                                        scalar=W2A[:, tt:tt + 1], in1=CWA[:, sl],
                                                        op0=ALU.mult, op1=ALU.add), "dve")
                vector.wait_ge(kb.sem("dve"), t5)
        dve_b0 = dve_a + 1

        @block.tensor
        def _(tensor):
            for tt in range(TT16):
                tensor.wait_ge(kb.sem("dve"), dve_b0 + tt * 5 + 5)
                if tt >= 2:
                    tensor.wait_ge(kb.sem("act"), act_sig + tt - 1)
                kb.tag(tensor.transpose(PT[0:16, (tt % 2) * 512:(tt % 2) * 512 + 128],
                                        CWA[:, tt * 16:(tt + 1) * 16], EYE[:]), "pe", 1)

        pe_cw = kb.val("pe")

        @block.scalar
        def _(scalar):
            for tt in range(TT16):
                scalar.wait_ge(kb.sem("pe"), pe_cw - 16 + tt + 1)
                kb.tag(scalar.copy(out=CWT[0:16, tt * 128:(tt + 1) * 128],
                                   in_=PT[0:16, (tt % 2) * 512:(tt % 2) * 512 + 128]), "act", 1)

    with nc.Block() as block, ExitStack() as esB2:
        B2Sg = esB2.enter_context(nc.sbuf_tensor("g_b2sg", [16, 1024], F32))
        B2CG = esB2.enter_context(nc.sbuf_tensor("g_b2cg", [128, 128], F32))
        PT2 = esB2.enter_context(nc.psum_tensor("g_pt2", [128, 2 * 512], F32))
        act_pre = kb.val("act")
        pe_pre = kb.val("pe")

        @block.sync
        def _(sync):
            kb.tag(sync.dma_start(out=B2Sg[:], in_=b2), "ds", 16)
        b2_ld = kb.val("ds")

        @block.tensor
        def _(tensor):
            tensor.wait_ge(kb.sem("ds"), b2_ld)
            for j in range(DT8):
                if j >= 2:
                    tensor.wait_ge(kb.sem("act"), act_pre + j - 1)
                kb.tag(tensor.transpose(PT2[:, (j % 2) * 512:(j % 2) * 512 + 16],
                                        B2Sg[:, j * 128:(j + 1) * 128], EYE[0:16, 0:16]), "pe", 1)

        @block.scalar
        def _(scalar):
            for j in range(DT8):
                scalar.wait_ge(kb.sem("pe"), pe_pre + j + 1)
                kb.tag(scalar.copy(out=B2CG[:, j * 16:(j + 1) * 16],
                                   in_=PT2[:, (j % 2) * 512:(j % 2) * 512 + 16]), "act", 1)

        @block.sync
        def _(sync):
            sync.wait_ge(kb.sem("act"), kb.val("act"))
            kb.tag(sync.dma_start(out=hbm["cwt0"], in_=CWT[:]), "dof", 16)
            kb.tag(sync.dma_start(out=hbm["b1c"], in_=B1C[:]), "dof", 16)
            kb.tag(sync.dma_start(out=hbm["b2c"], in_=B2CG[:]), "dof", 16)
            sync.wait_ge(kb.sem("dof"), kb.val("dof"))

        kb.drain_body(block)
    esS.close()


def sec_moe0fin(kb, hbm):
    nc = kb.nc
    esS = ExitStack()
    ONESC = esS.enter_context(nc.sbuf_tensor("g_onesc", [128, 1], F32))
    ONESR2 = esS.enter_context(nc.sbuf_tensor("g_onesr2", [1, 128], F32))
    V = esS.enter_context(nc.sbuf_tensor("g_v", [128, DT8 * T], F32))
    ROWS = esS.enter_context(nc.sbuf_tensor("g_rows", [1, 4 * T], F32))
    SQ = esS.enter_context(nc.sbuf_tensor("g_sq", [128, 2 * 512], F32))
    XOF = esS.enter_context(nc.sbuf_tensor("g_xof", [128, T], F32))
    XOB = esS.enter_context(nc.sbuf_tensor("g_xob", [128, T], BF16))
    PS1 = esS.enter_context(nc.psum_tensor("g_ps1", [1, 512], F32))
    PS2 = esS.enter_context(nc.psum_tensor("g_ps2", [1, 512], F32))
    PBa = esS.enter_context(nc.psum_tensor("g_pba", [128, 512], F32))
    PBb = esS.enter_context(nc.psum_tensor("g_pbb", [128, 512], F32))

    with nc.Block() as block, ExitStack() as esG2:
        GS = esG2.enter_context(nc.sbuf_tensor("g_gs", [128, T], F32))
        S0T = esG2.enter_context(nc.sbuf_tensor("g_s0t", [128, T], F32))
        X0T = esG2.enter_context(nc.sbuf_tensor("g_x0t", [128, T], F32))

        act2, dve2 = kb.val("act"), kb.val("dve")
        du0, dzr0, dg0 = kb.val("du_"), kb.val("dzr"), kb.val("dg")
        dof0 = kb.val("dof")
        ds0 = kb.val("ds")

        @block.sync
        def _(sync):
            kb.tag(sync.dma_start(out=ONESC[:], in_=hbm["c_onescol_f"]), "ds", 16)
            kb.tag(sync.dma_start(out=ONESR2[:], in_=hbm["c_onesrow_f"]), "ds", 16)

        @block.gpsimd
        def _(g):
            for dk in range(DT8):
                if dk >= 1:
                    g.wait_ge(kb.sem("dve"), dve2 + dk * 2)
                kb.tag(g.dma_start(out=S0T[:], in_=hbm["s0f"].rearrange("(n p) t -> n p t", p=128)[dk]), "du_", 16)
                kb.tag(g.dma_start(out=X0T[:], in_=hbm["x0T"].rearrange("(n p) t -> n p t", p=128)[dk]), "dzr", 16)
                kb.tag(g.dma_start(out=GS[:], in_=hbm["g0f"].rearrange("(n p) t -> n p t", p=128)[dk]), "dg", 16)

        @block.vector
        def _(vector):
            for dm in range(DT8):
                vector.wait_ge(kb.sem("du_"), du0 + 16 * (dm + 1))
                vector.wait_ge(kb.sem("dzr"), dzr0 + 16 * (dm + 1))
                vector.wait_ge(kb.sem("dg"), dg0 + 16 * (dm + 1))
                t1 = kb.tag(vector.tensor_mul(GS[:], GS[:], S0T[:]), "dve")
                vector.wait_ge(kb.sem("dve"), t1)
                kb.tag(vector.tensor_add(V[:, dm * T:(dm + 1) * T], GS[:], X0T[:]), "dve")

        @block.sync
        def _(sync):
            x1f_h = hbm["x1f"].rearrange("(n p) t -> n p t", p=128)
            for dm in range(DT8):
                sync.wait_ge(kb.sem("dve"), dve2 + dm * 2 + 2)
                kb.tag(sync.dma_start(out=x1f_h[dm], in_=V[:, dm * T:(dm + 1) * T]), "dof", 16)
            sync.wait_ge(kb.sem("dof"), kb.val("dof"))

        kb.drain_body(block)

    emit_ln_tail(kb, hbm, V, None, "xn1b", None, None,
                 ONESC, ONESR2, SQ, ROWS, XOF, XOB, PS1, PS2, PBa, PBb)
    esS.close()





def sec_ln1(kb, hbm):
    nc = kb.nc
    esS = ExitStack()
    ONESC = esS.enter_context(nc.sbuf_tensor("l1_onesc", [128, 1], F32))
    ONESR = esS.enter_context(nc.sbuf_tensor("l1_onesr", [1, 128], F32))
    V = esS.enter_context(nc.sbuf_tensor("l1_v", [128, DT8 * T], F32))
    ROWS = esS.enter_context(nc.sbuf_tensor("l1_rows", [1, 4 * T], F32))
    SQ = esS.enter_context(nc.sbuf_tensor("l1_sq", [128, 2 * 512], F32))
    XOF = esS.enter_context(nc.sbuf_tensor("l1_xof", [128, T], F32))
    XOB = esS.enter_context(nc.sbuf_tensor("l1_xob", [128, T], BF16))
    PS1 = esS.enter_context(nc.psum_tensor("l1_ps1", [1, 512], F32))
    PS2 = esS.enter_context(nc.psum_tensor("l1_ps2", [1, 512], F32))
    PBa = esS.enter_context(nc.psum_tensor("l1_pba", [128, 512], F32))
    PBb = esS.enter_context(nc.psum_tensor("l1_pbb", [128, 512], F32))

    ds0 = kb.val("ds")
    with nc.Block() as block:
        @block.sync
        def _(sync):
            kb.tag(sync.dma_start(out=ONESC[:], in_=hbm["c_onescol_f"]), "ds", 16)
            kb.tag(sync.dma_start(out=ONESR[:], in_=hbm["c_onesrow_f"]), "ds", 16)
            kb.tag(sync.dma_start(out=V[:].rearrange("p (n t) -> p n t", t=T),
                                  in_=hbm["x1f"].rearrange("(n p) t -> p n t", p=128)), "ds", 16)
            sync.wait_ge(kb.sem("ds"), kb.val("ds"))
    emit_ln_tail(kb, hbm, V, None, "xn1b", None, None,
                 ONESC, ONESR, SQ, ROWS, XOF, XOB, PS1, PS2, PBa, PBb)
    esS.close()




def sec_att1(kb, hbm):
    nc = kb.nc

    esS = ExitStack()
    EYE = esS.enter_context(nc.sbuf_tensor("a_eye", [128, 128], F32))
    EYEB = esS.enter_context(nc.sbuf_tensor("a_eyeb", [128, 128], BF16))
    ONESC = esS.enter_context(nc.sbuf_tensor("a_onesc", [128, 1], F32))
    ONESR = esS.enter_context(nc.sbuf_tensor("a_onesr", [1, 128], F32))
    MASKT = esS.enter_context(nc.sbuf_tensor("a_maskt", [128, 128], BF16))
    WOT = esS.enter_context(nc.sbuf_tensor("a_wot", [128, 2 * 1024], BF16))
    QH = esS.enter_context(nc.sbuf_tensor("a_qh", [128, 2 * T], BF16))
    KH = esS.enter_context(nc.sbuf_tensor("a_kh", [128, 2 * T], BF16))
    VTOK = esS.enter_context(nc.sbuf_tensor("a_vtok", [128, 16 * 256], BF16))
    KTOK = esS.enter_context(nc.sbuf_tensor("a_ktok", [128, 16 * 256], BF16))
    KCUM = esS.enter_context(nc.sbuf_tensor("a_kcum", [128, 2 * T], F32))
    DNB = esS.enter_context(nc.sbuf_tensor("a_dnb", [128, 2 * T], F32))
    YT = esS.enter_context(nc.sbuf_tensor("a_yt", [128, 2 * T], BF16))
    ROWA = esS.enter_context(nc.sbuf_tensor("a_rowa", [1, T], F32))
    ROWB = esS.enter_context(nc.sbuf_tensor("a_rowb", [1, T], F32))
    ZRO = esS.enter_context(nc.sbuf_tensor("a_zro", [128, T], F32))

    esQ = ExitStack()
    XN1 = esQ.enter_context(nc.sbuf_tensor("a_xn1", [128, DT8 * T], BF16))
    WS = esQ.enter_context(nc.sbuf_tensor("a_ws", [128, 2 * 1024], BF16))
    WS2 = esQ.enter_context(nc.sbuf_tensor("a_ws2", [128, 8 * 256], BF16))
    WQT = esQ.enter_context(nc.sbuf_tensor("a_wqt", [128, DT8 * 256], BF16))
    WKT = esQ.enter_context(nc.sbuf_tensor("a_wkt", [128, DT8 * 256], BF16))
    WVT = esQ.enter_context(nc.sbuf_tensor("a_wvt", [128, DT8 * 256], BF16))
    QT = esQ.enter_context(nc.sbuf_tensor("a_qt", [128, 2 * T], F32))
    KT = esQ.enter_context(nc.sbuf_tensor("a_kt", [128, 2 * T], F32))
    MMS = esQ.enter_context(nc.sbuf_tensor("a_mms", [128, T], F32))
    SQS = esQ.enter_context(nc.sbuf_tensor("a_sqs", [128, T], F32))
    PTRB = esQ.enter_context(nc.psum_tensor("a_ptrb", [128, 2 * 1024], BF16))
    PQ = esQ.enter_context(nc.psum_tensor("a_pq", [128, 2 * 512], F32))
    PN = esQ.enter_context(nc.psum_tensor("a_pn", [1, 512], F32))
    PBB = esQ.enter_context(nc.psum_tensor("a_pbb", [128, 512], F32))

    pe0, act0, dve0 = (kb.val(x) for x in ("pe", "act", "dve"))
    dxt0, dw00, dw10 = kb.val("dxt"), kb.val("dw0"), kb.val("dw1")
    with nc.Block() as block:
        @block.sync
        def _(sync):
            kb.tag(sync.dma_start(out=EYE[:], in_=hbm["c_eye_f"]), "ds", 16)
            kb.tag(sync.dma_start(out=EYEB[:], in_=hbm["c_eye_b"]), "ds", 16)
            kb.tag(sync.dma_start(out=ONESC[:], in_=hbm["c_onescol_f"]), "ds", 16)
            kb.tag(sync.dma_start(out=ONESR[:], in_=hbm["c_onesrow_f"]), "ds", 16)
            kb.tag(sync.dma_start(out=MASKT[:], in_=hbm["c_maskT_b"]), "ds", 16)
        a_ld = kb.val("ds")

        @block.gpsimd
        def _(g):
            kb.tag(g.dma_start(out=XN1[:].rearrange("p (n t) -> p n t", t=T),
                               in_=hbm["xn1b"].rearrange("(n p) t -> p n t", p=128)), "dxt", 16)
            for wi, wname in enumerate(("L1_attn_Wq", "L1_attn_Wk", "L1_attn_Wv")):
                if wi >= 1:
                    g.wait_ge(kb.sem("pe"), pe0 + wi * 16)
                kb.tag(g.dma_start(out=WS[:].rearrange("p (r d) -> p r d", d=1024),
                                   in_=hbm[wname].rearrange("(r p) d -> p r d", p=128)), "dw0", 16)
            kb.tag(g.dma_start(out=WS2[:].rearrange("p (a j) -> p a j", j=256),
                               in_=hbm["L1_attn_Wo"].rearrange("(a p) j -> p a j", p=128)), "dw1", 16)
            kb.tag(g.memset(ZRO[:], 0.0), "gp", 1)

        @block.tensor
        def _(tensor):
            tensor.wait_ge(kb.sem("ds"), a_ld)
            for wi in range(3):
                tensor.wait_ge(kb.sem("dw0"), dw00 + 16 * (wi + 1))
                for j in range(16):
                    r, dk = divmod(j, DT8)
                    gj = wi * 16 + j
                    if gj >= 2:
                        tensor.wait_ge(kb.sem("act"), act0 + gj - 1)
                    kb.tag(tensor.transpose(PTRB[:, (gj % 2) * 1024:(gj % 2) * 1024 + 128],
                                            WS[:, r * 1024 + dk * 128: r * 1024 + dk * 128 + 128],
                                            EYEB[:]), "pe", 1)
            tensor.wait_ge(kb.sem("dw1"), dw10 + 16)
            for j in range(16):
                a, ji = divmod(j, 2)
                gj = 48 + j
                tensor.wait_ge(kb.sem("act"), act0 + gj - 1)
                kb.tag(tensor.transpose(PTRB[:, (gj % 2) * 1024:(gj % 2) * 1024 + 128],
                                        WS2[:, a * 256 + ji * 128: a * 256 + ji * 128 + 128],
                                        EYEB[:]), "pe", 1)
            tensor.wait_ge(kb.sem("dxt"), dxt0 + 16)
            tensor.wait_ge(kb.sem("act"), act0 + 64)
            for gi in range(32):
                if gi >= 2:
                    tensor.wait_ge(kb.sem("act"), act0 + 64 + gi - 1)
                if gi < 8:
                    mi, tc = divmod(gi, 4)
                    for dk in range(DT8):
                        mm = tensor.matmul(PQ[:, (gi % 2) * 512:(gi % 2) * 512 + 512],
                                           WQT[:, dk * 256 + mi * 128: dk * 256 + mi * 128 + 128],
                                           XN1[:, dk * T + tc * 512: dk * T + tc * 512 + 512],
                                           start=(dk == 0), stop=(dk == DT8 - 1))
                elif gi < 16:
                    mi, tc = divmod(gi - 8, 4)
                    for dk in range(DT8):
                        mm = tensor.matmul(PQ[:, (gi % 2) * 512:(gi % 2) * 512 + 512],
                                           WKT[:, dk * 256 + mi * 128: dk * 256 + mi * 128 + 128],
                                           XN1[:, dk * T + tc * 512: dk * T + tc * 512 + 512],
                                           start=(dk == 0), stop=(dk == DT8 - 1))
                else:
                    tt = gi - 16
                    for dk in range(DT8):
                        mm = tensor.matmul(PQ[:, (gi % 2) * 512:(gi % 2) * 512 + 256],
                                           XN1[:, dk * T + tt * 128: dk * T + tt * 128 + 128],
                                           WVT[:, dk * 256:(dk + 1) * 256],
                                           start=(dk == 0), stop=(dk == DT8 - 1))
                kb.tag(mm, "pe", 1)

        @block.scalar
        def _(scalar):
            for gj in range(64):
                scalar.wait_ge(kb.sem("pe"), pe0 + gj + 1)
                if gj < 48:
                    wi, j = divmod(gj, 16)
                    r, dk = divmod(j, DT8)
                    WDST = (WQT, WKT, WVT)[wi]
                    kb.tag(scalar.copy(out=WDST[:, dk * 256 + r * 128: dk * 256 + r * 128 + 128],
                                       in_=PTRB[:, (gj % 2) * 1024:(gj % 2) * 1024 + 128]), "act", 1)
                else:
                    j = gj - 48
                    a, ji = divmod(j, 2)
                    kb.tag(scalar.copy(out=WOT[:, ji * 1024 + a * 128: ji * 1024 + a * 128 + 128],
                                       in_=PTRB[:, (gj % 2) * 1024:(gj % 2) * 1024 + 128]), "act", 1)
            for gi in range(32):
                scalar.wait_ge(kb.sem("pe"), pe0 + 64 + gi + 1)
                if gi < 8:
                    mi, tc = divmod(gi, 4)
                    kb.tag(scalar.copy(out=QT[:, mi * T + tc * 512: mi * T + tc * 512 + 512],
                                       in_=PQ[:, (gi % 2) * 512:(gi % 2) * 512 + 512]), "act", 1)
                elif gi < 16:
                    mi, tc = divmod(gi - 8, 4)
                    kb.tag(scalar.copy(out=KT[:, mi * T + tc * 512: mi * T + tc * 512 + 512],
                                       in_=PQ[:, (gi % 2) * 512:(gi % 2) * 512 + 512]), "act", 1)
                else:
                    tt = gi - 16
                    kb.tag(scalar.copy(out=VTOK[:, tt * 256:(tt + 1) * 256],
                                       in_=PQ[:, (gi % 2) * 512:(gi % 2) * 512 + 256]), "act", 1)

    with nc.Block() as block:
        se = SerialEmitter(kb)
        for SRC, DSTH in ((QT, QH), (KT, KH)):
            for mi in range(2):
                sl = slice(mi * T, mi * T + T)
                se.step("vector", lambda e, SRC=SRC, sl=sl: e.tensor_scalar(
                    out=MMS[:], in0=SRC[:, sl], scalar1=0.0, scalar2=None, op0=ALU.min))
                se.step("scalar", lambda e: e.activation(out=SQS[:], in_=MMS[:], func=AF.Exp))
                se.step("vector", lambda e, SRC=SRC, sl=sl: e.scalar_tensor_tensor(
                    out=SRC[:, sl], in0=SRC[:, sl], scalar=0.0, in1=SQS[:],
                    op0=ALU.max, op1=ALU.add))
                se.step("scalar", lambda e, SRC=SRC, sl=sl: e.activation(
                    out=SQS[:], in_=SRC[:, sl], func=AF.Square))
                for hh in range(2):
                    hr = slice(hh * 64, hh * 64 + 64)
                    for tc in range(TC4):
                        se.step("tensor", lambda e, hr=hr, tc=tc: e.matmul(
                            PN[:], ONESC[hr, :], SQS[hr, tc * 512:(tc + 1) * 512],
                            start=True, stop=True))
                        se.step("scalar", lambda e, tc=tc: e.copy(
                            out=ROWA[:, tc * 512:(tc + 1) * 512], in_=PN[:]))
                    se.step("scalar", lambda e: e.activation(out=ROWB[:], in_=ROWA[:], func=AF.Sqrt))
                    se.step("vector", lambda e: e.tensor_scalar(
                        out=ROWB[:], in0=ROWB[:], scalar1=ATT_EPS, scalar2=None, op0=ALU.add))
                    se.step("vector", lambda e: e.reciprocal(ROWB[:], ROWB[:]))
                    for tc in range(TC4):
                        se.step("tensor", lambda e, hr=hr, tc=tc: e.matmul(
                            PBB[hr, 0:512], ONESR[:, 0:64], ROWB[:, tc * 512:(tc + 1) * 512],
                            start=True, stop=True))
                        se.step("vector", lambda e, SRC=SRC, DSTH=DSTH, hr=hr, mi=mi, tc=tc: e.tensor_mul(
                            DSTH[hr, mi * T + tc * 512: mi * T + tc * 512 + 512],
                            SRC[hr, mi * T + tc * 512: mi * T + tc * 512 + 512],
                            PBB[hr, 0:512]))
        for mi in range(2):
            for tt in range(TT16):
                se.step("tensor", lambda e, mi=mi, tt=tt: e.transpose(
                    PTRB[:, 0:128], KH[:, mi * T + tt * 128: mi * T + tt * 128 + 128], EYEB[:]))
                se.step("scalar", lambda e, mi=mi, tt=tt: e.copy(
                    out=KTOK[:, tt * 256 + mi * 128: tt * 256 + mi * 128 + 128], in_=PTRB[:, 0:128]))
        for mi in range(2):
            sl = slice(mi * T, mi * T + T)
            se.step("vector", lambda e, sl=sl: e.tensor_tensor_scan(
                out=KCUM[:, sl], data0=ZRO[:], data1=KH[:, sl], initial=0.0,
                op0=ALU.add, op1=ALU.add))
        for mi in range(2):
            sl = slice(mi * T, mi * T + T)
            se.step("vector", lambda e, sl=sl: e.tensor_mul(MMS[:], QH[:, sl], KCUM[:, sl]))
            for hh in range(2):
                hr = slice(hh * 64, hh * 64 + 64)
                for tc in range(TC4):
                    se.step("tensor", lambda e, hr=hr, tc=tc: e.matmul(
                        PN[:], ONESC[hr, :], MMS[hr, tc * 512:(tc + 1) * 512],
                        start=True, stop=True))
                    se.step("scalar", lambda e, tc=tc: e.copy(
                        out=ROWA[:, tc * 512:(tc + 1) * 512], in_=PN[:]))
                se.step("vector", lambda e: e.tensor_scalar(
                    out=ROWA[:], in0=ROWA[:], scalar1=ATT_EPS, scalar2=None, op0=ALU.add))
                se.step("vector", lambda e: e.reciprocal(ROWA[:], ROWA[:]))
                for tc in range(TC4):
                    se.step("tensor", lambda e, hr=hr, tc=tc: e.matmul(
                        PBB[hr, 0:512], ONESR[:, 0:64], ROWA[:, tc * 512:(tc + 1) * 512],
                        start=True, stop=True))
                    se.step("scalar", lambda e, hr=hr, mi=mi, tc=tc: e.copy(
                        out=DNB[hr, mi * T + tc * 512: mi * T + tc * 512 + 512],
                        in_=PBB[hr, 0:512]))
        se.emit(block)
    esQ.close()

    esC = ExitStack()
    SMT = esC.enter_context(nc.sbuf_tensor("a_smt", [128, 128], BF16))
    KVT = esC.enter_context(nc.sbuf_tensor("a_kvt", [128, 2 * 64], BF16))
    KVF = esC.enter_context(nc.sbuf_tensor("a_kvf", [128, 2 * 64], F32))
    S1F = esC.enter_context(nc.sbuf_tensor("a_s1f", [128, T], F32))
    PST = esC.enter_context(nc.psum_tensor("a_pst", [128, 512], F32))
    PNM = esC.enter_context(nc.psum_tensor("a_pnm", [128, 512], F32))
    PKV = esC.enter_context(nc.psum_tensor("a_pkv", [128, 2 * 64], F32))
    PSO = esC.enter_context(nc.psum_tensor("a_pso", [128, 2 * 512], F32))

    with nc.Block() as block:
        se = SerialEmitter(kb)
        for c in range(16):
            for h in range(4):
                mi, hh = divmod(h, 2)
                hr = slice(hh * 64, hh * 64 + 64)
                csl = slice(mi * T + c * 128, mi * T + c * 128 + 128)
                se.step("tensor", lambda e, hr=hr, csl=csl: e.matmul(
                    PST[:, 0:128], KH[hr, csl], QH[hr, csl], start=True, stop=True))
                se.step("vector", lambda e: e.tensor_tensor(
                    out=SMT[:], in0=PST[:, 0:128], in1=MASKT[:], op=ALU.mult))
                se.step("tensor", lambda e, c=c, h=h, hr=hr: e.matmul(
                    PNM[hr, 0:128], VTOK[:, c * 256 + h * 64: c * 256 + h * 64 + 64],
                    SMT[:], start=True, stop=(c == 0)))
                if c > 0:
                    se.step("tensor", lambda e, h=h, hr=hr, csl=csl, mi=mi: e.matmul(
                        PNM[hr, 0:128], KVT[hr, mi * 64:(mi + 1) * 64], QH[hr, csl],
                        start=False, stop=True))
                se.step("vector", lambda e, hr=hr, csl=csl: e.tensor_tensor(
                    out=YT[hr, csl], in0=PNM[hr, 0:128], in1=DNB[hr, csl], op=ALU.mult))
                se.step("tensor", lambda e, c=c, h=h, hr=hr, mi=mi: e.matmul(
                    PKV[hr, mi * 64:(mi + 1) * 64], KTOK[:, c * 256 + h * 64: c * 256 + h * 64 + 64],
                    VTOK[:, c * 256 + h * 64: c * 256 + h * 64 + 64],
                    start=True, stop=True))
                if c == 0:
                    se.step("vector", lambda e, hr=hr, mi=mi: e.tensor_copy(
                        KVF[hr, mi * 64:(mi + 1) * 64], PKV[hr, mi * 64:(mi + 1) * 64]))
                else:
                    se.step("vector", lambda e, hr=hr, mi=mi: e.tensor_add(
                        KVF[hr, mi * 64:(mi + 1) * 64], KVF[hr, mi * 64:(mi + 1) * 64],
                        PKV[hr, mi * 64:(mi + 1) * 64]))
                se.step("scalar", lambda e, hr=hr, mi=mi: e.copy(
                    out=KVT[hr, mi * 64:(mi + 1) * 64], in_=KVF[hr, mi * 64:(mi + 1) * 64]))
        se.emit(block)

    dof0 = kb.val("dof")
    pe2, act2, dve2 = (kb.val(x) for x in ("pe", "act", "dve"))
    with nc.Block() as block:
        @block.tensor
        def _(tensor):
            for dm in range(DT8):
                for tc in range(TC4):
                    gi = dm * 4 + tc
                    if gi >= 2:
                        tensor.wait_ge(kb.sem("act"), act2 + gi - 1)
                    for ji in range(2):
                        mm = tensor.matmul(PSO[:, (gi % 2) * 512:(gi % 2) * 512 + 512],
                                           WOT[:, ji * 1024 + dm * 128: ji * 1024 + dm * 128 + 128],
                                           YT[:, ji * T + tc * 512: ji * T + tc * 512 + 512],
                                           start=(ji == 0), stop=(ji == 1))
                    kb.tag(mm, "pe", 1)

        @block.scalar
        def _(scalar):
            for dm in range(DT8):
                for tc in range(TC4):
                    gi = dm * 4 + tc
                    scalar.wait_ge(kb.sem("pe"), pe2 + gi + 1)
                    if tc == 0 and dm >= 1:
                        scalar.wait_ge(kb.sem("dof"), dof0 + 16 * dm)
                    kb.tag(scalar.copy(out=S1F[:, tc * 512:(tc + 1) * 512],
                                       in_=PSO[:, (gi % 2) * 512:(gi % 2) * 512 + 512]), "act", 1)

        @block.sync
        def _(sync):
            s1h = hbm["s1f"].rearrange("(n p) t -> n p t", p=128)
            for dm in range(DT8):
                sync.wait_ge(kb.sem("act"), act2 + dm * 4 + 4)
                kb.tag(sync.dma_start(out=s1h[dm], in_=S1F[:]), "dof", 16)
            sync.wait_ge(kb.sem("dof"), kb.val("dof"))

        kb.drain_body(block)
    esC.close()
    esS.close()





def sec_moe1(kb, hbm):
    nc = kb.nc
    gw = hbm["L1_ffn_gate_w"]
    wg_, wu_, wd_ = hbm["L1_ffn_w_gate"], hbm["L1_ffn_w_up"], hbm["L1_ffn_w_down"]
    wsg, wsu, wsd = hbm["L1_ffn_ws_gate"], hbm["L1_ffn_ws_up"], hbm["L1_ffn_ws_down"]

    esS = ExitStack()
    EYE = esS.enter_context(nc.sbuf_tensor("q_eye", [128, 128], F32))
    SEL16 = esS.enter_context(nc.sbuf_tensor("q_sel16", [16, 16 * 128], F32))
    XN = esS.enter_context(nc.sbuf_tensor("q_xn", [128, DT8 * T], BF16))
    ACC = esS.enter_context(nc.sbuf_tensor("q_acc", [128, DT8 * T], F32))
    CWT = esS.enter_context(nc.sbuf_tensor("q_cwt", [16, T], F32))

    pe0, act0, dve0, gp0 = (kb.val(x) for x in ("pe", "act", "dve", "gp"))
    dxt0 = kb.val("dxt")

    with nc.Block() as block, ExitStack() as esG0:
        GWS = esG0.enter_context(nc.sbuf_tensor("q_gws", [16, 1024], BF16))
        GWT = esG0.enter_context(nc.sbuf_tensor("q_gwt", [128, DT8 * 16], BF16))
        LGA = esG0.enter_context(nc.sbuf_tensor("q_lga", [128, 16 * 16], F32))
        PRB = esG0.enter_context(nc.sbuf_tensor("q_prb", [128, 16], F32))
        MSK = esG0.enter_context(nc.sbuf_tensor("q_msk", [128, 4 * 16], F32))
        CWA = esG0.enter_context(nc.sbuf_tensor("q_cwa", [128, 16 * 16], F32))
        PL = esG0.enter_context(nc.psum_tensor("q_pl", [128, 2 * 512], F32))
        PT = esG0.enter_context(nc.psum_tensor("q_pt", [128, 2 * 512], F32))
        PTB_Q = esG0.enter_context(nc.psum_tensor("q_ptb", [128, 1024], BF16))
        EYEB_Q = esG0.enter_context(nc.sbuf_tensor("q_eyeb", [128, 128], BF16))

        @block.sync
        def _(sync):
            kb.tag(sync.dma_start(out=EYE[:], in_=hbm["c_eye_f"]), "ds", 16)
            kb.tag(sync.dma_start(out=EYEB_Q[:], in_=hbm["c_eye_b"]), "ds", 16)
            kb.tag(sync.dma_start(out=SEL16[:], in_=hbm["c_sel16_f"]), "ds", 16)
        q0_ld = kb.val("ds")

        @block.gpsimd
        def _(g):
            kb.tag(g.dma_start(out=XN[:].rearrange("p (n t) -> p n t", t=T),
                               in_=hbm["xn1b"].rearrange("(n p) t -> p n t", p=128)), "dxt", 16)
            kb.tag(g.dma_start(out=GWS[:], in_=gw), "dg", 16)
            kb.tag(g.memset(ACC[:], 0.0), "gp", 1)

        dg_gw = kb.val("dg")
        se = SerialEmitter(kb)
        for j in range(DT8):
            se.step("tensor", lambda e, j=j: e.transpose(PTB_Q[0:128, 0:16],
                                                         GWS[:, j * 128:(j + 1) * 128],
                                                         EYEB_Q[0:16, 0:16]))
            se.step("scalar", lambda e, j=j: e.copy(out=GWT[:, j * 16:(j + 1) * 16], in_=PTB_Q[0:128, 0:16]))

        @block.tensor
        def _(tensor):
            tensor.wait_ge(kb.sem("dg"), dg_gw)
            tensor.wait_ge(kb.sem("ds"), q0_ld)
        se.emit(block)
        act_t = kb.val("act")
        pe_t = kb.val("pe")

        @block.tensor
        def _(tensor):
            tensor.wait_ge(kb.sem("dxt"), dxt0 + 16)
            tensor.wait_ge(kb.sem("ds"), q0_ld)
            tensor.wait_ge(kb.sem("act"), act_t)
            for tt in range(TT16):
                if tt >= 2:
                    tensor.wait_ge(kb.sem("act"), act_t + tt - 1)
                for dk in range(DT8):
                    mm = tensor.matmul(PL[:, (tt % 2) * 512:(tt % 2) * 512 + 16],
                                       XN[:, dk * T + tt * 128: dk * T + tt * 128 + 128],
                                       GWT[:, dk * 16:(dk + 1) * 16],
                                       start=(dk == 0), stop=(dk == DT8 - 1))
                kb.tag(mm, "pe", 1)

        @block.scalar
        def _(scalar):
            for tt in range(TT16):
                scalar.wait_ge(kb.sem("pe"), pe_t + tt + 1)
                kb.tag(scalar.copy(out=LGA[:, tt * 16:(tt + 1) * 16],
                                   in_=PL[:, (tt % 2) * 512:(tt % 2) * 512 + 16]), "act", 1)
        act_lg = kb.val("act")

        sew = SerialEmitter(kb)
        for tt in range(TT16):
            sl = slice(tt * 16, (tt + 1) * 16)
            if tt == 0:
                sew.step("vector", lambda e, sl=sl: e.max(out=MSK[:, 0:8], in_=LGA[:, sl]))
            else:
                sew.step("vector", lambda e, sl=sl: e.max(out=MSK[:, 0:8], in_=LGA[:, sl]))
            sew.step("vector", lambda e: e.tensor_scalar(out=MSK[:, 8:9], in0=MSK[:, 0:1],
                                                         scalar1=-1.0, scalar2=None, op0=ALU.mult))
            sew.step("scalar", lambda e, sl=sl: e.activation(out=PRB[:], in_=LGA[:, sl], func=AF.Exp,
                                                             bias=MSK[:, 8:9], accum_out=MSK[:, 9:10]))
            sew.step("vector", lambda e: e.reciprocal(MSK[:, 10:11], MSK[:, 9:10]))
            sew.step("vector", lambda e: e.tensor_scalar(out=PRB[:], in0=PRB[:],
                                                         scalar1=MSK[:, 10:11], scalar2=None, op0=ALU.mult))
            sew.step("vector", lambda e: e.max(out=MSK[:, 16:24], in_=PRB[:]))
            sew.step("vector", lambda e: e.tensor_scalar(out=MSK[:, 32:48], in0=PRB[:],
                                                         scalar1=MSK[:, 16:17], scalar2=None, op0=ALU.is_ge))
            sew.step("vector", lambda e: e.tensor_scalar(out=MSK[:, 48:64], in0=PRB[:],
                                                         scalar1=MSK[:, 17:18], scalar2=None, op0=ALU.is_ge))
            sew.step("vector", lambda e: e.tensor_sub(MSK[:, 48:64], MSK[:, 48:64], MSK[:, 32:48]))
            sew.step("vector", lambda e, sl=sl: e.tensor_scalar(out=CWA[:, sl], in0=MSK[:, 32:48],
                                                                scalar1=MSK[:, 16:17], scalar2=None, op0=ALU.mult))
            sew.step("vector", lambda e, sl=sl: e.scalar_tensor_tensor(out=CWA[:, sl], in0=MSK[:, 48:64],
                                                                       scalar=MSK[:, 17:18], in1=CWA[:, sl],
                                                                       op0=ALU.mult, op1=ALU.add))
            sew.step("tensor", lambda e, sl=sl: e.transpose(PT[0:16, 512:512 + 128], CWA[:, sl], EYE[:]))
            sew.step("scalar", lambda e, tt=tt: e.copy(out=CWT[0:16, tt * 128:(tt + 1) * 128],
                                                       in_=PT[0:16, 512:512 + 128]))

        @block.vector
        def _(vector):
            vector.wait_ge(kb.sem("act"), act_lg)
        sew.emit(block)

    with nc.Block() as block, ExitStack() as esG1:
        WGB = esG1.enter_context(nc.sbuf_tensor("q_wgb", [128, DT8 * IE], BF16))
        WUB = esG1.enter_context(nc.sbuf_tensor("q_wub", [128, DT8 * IE], BF16))
        WDB = esG1.enter_context(nc.sbuf_tensor("q_wdb", [128, 4 * 1024], BF16))
        SG = esG1.enter_context(nc.sbuf_tensor("q_sg", [128, 2 * 512], F32))
        A2 = esG1.enter_context(nc.sbuf_tensor("q_a2", [128, 4 * 512], BF16))
        CWB = esG1.enter_context(nc.sbuf_tensor("q_cwb", [128, T], F32))
        Y1 = esG1.enter_context(nc.sbuf_tensor("q_y1", [128, 2 * 512], F32))
        PG = esG1.enter_context(nc.psum_tensor("q_pg", [128, 2 * 512], F32))
        PU = esG1.enter_context(nc.psum_tensor("q_pu", [128, 2 * 512], F32))
        PO = esG1.enter_context(nc.psum_tensor("q_po", [128, 2 * 512], F32))
        PCW = esG1.enter_context(nc.psum_tensor("q_pcw", [128, 2 * 512], F32))

        pe1, act1, dve1, gp1 = (kb.val(x) for x in ("pe", "act", "dve", "gp"))
        dwa0, dwb0 = kb.val("dw0"), kb.val("dw1")
        PE_PE, ACT_PE, DVE_PE, GP_PE = 68, 20, 48, 32

        def peb(e):
            return pe1 + e * PE_PE

        def actb(e):
            return act1 + e * ACT_PE

        def dveb(e):
            return dve1 + e * DVE_PE

        def gpb(e):
            return gp1 + e * GP_PE

        def silu_tag(gi):
            e_, r = divmod(gi, 16)
            tc_, it_ = divmod(r, 4)
            return actb(e_) + 4 + tc_ * 4 + it_ + 1

        def a2_tag(gi):
            e_, r = divmod(gi, 16)
            tc_, it_ = divmod(r, 4)
            return dveb(e_) + tc_ * 12 + it_ + 1

        def y1_tag(gj):
            e_, r = divmod(gj, 32)
            tc_, dm_ = divmod(r, 8)
            return dveb(e_) + tc_ * 12 + 4 + dm_ + 1

        def gp_tag(gj):
            e_, r = divmod(gj, 32)
            tc_, dm_ = divmod(r, 8)
            return gpb(e_) + tc_ * 8 + dm_ + 1

        @block.gpsimd
        def _(g):
            for e in range(NE):
                if e >= 1:
                    g.wait_ge(kb.sem("pe"), peb(e - 1) + 4 + 3 * 16 + 8)
                for dk in range(DT8):
                    kb.tag(g.dma_start(out=WGB[:, dk * IE:(dk + 1) * IE],
                                       in_=wg_[e][dk * 128:(dk + 1) * 128, :]), "dw0", 16)
                    kb.tag(g.dma_start(out=WUB[:, dk * IE:(dk + 1) * IE],
                                       in_=wu_[e][dk * 128:(dk + 1) * 128, :]), "dw0", 16)
                if e >= 1:
                    g.wait_ge(kb.sem("pe"), peb(e))
                for it in range(4):
                    kb.tag(g.dma_start(out=WDB[:, it * 1024:(it + 1) * 1024],
                                       in_=wd_[e][it * 128:(it + 1) * 128, :]), "dw1", 16)


        @block.tensor
        def _(tensor):
            tensor.wait_ge(kb.sem("dxt"), kb.val("dxt"))
            for e in range(NE):
                for q in range(4):
                    gq = e * 4 + q
                    if gq >= 2:
                        pq = gq - 2
                        tensor.wait_ge(kb.sem("act"), actb(pq // 4) + (pq % 4) + 1)
                    kb.tag(tensor.matmul(PCW[:, (q % 2) * 512:(q % 2) * 512 + 512],
                                         SEL16[:, e * 128:(e + 1) * 128],
                                         CWT[0:16, q * 512:(q + 1) * 512], start=True, stop=True), "pe", 1)
                tensor.wait_ge(kb.sem("dw0"), dwa0 + 16 * 16 * (e + 1))
                for tc in range(TC4):
                    for it in range(4):
                        gi = (e * 4 + tc) * 4 + it
                        if gi >= 2:
                            tensor.wait_ge(kb.sem("act"), silu_tag(gi - 2))
                            tensor.wait_ge(kb.sem("dve"), a2_tag(gi - 2))
                        for dk in range(DT8):
                            mm = tensor.matmul(PG[:, (gi % 2) * 512:(gi % 2) * 512 + 512],
                                               WGB[:, dk * IE + it * 128: dk * IE + it * 128 + 128],
                                               XN[:, dk * T + tc * 512: dk * T + tc * 512 + 512],
                                               start=(dk == 0), stop=(dk == DT8 - 1))
                        kb.tag(mm, "pe", 1)
                        for dk in range(DT8):
                            mm = tensor.matmul(PU[:, (gi % 2) * 512:(gi % 2) * 512 + 512],
                                               WUB[:, dk * IE + it * 128: dk * IE + it * 128 + 128],
                                               XN[:, dk * T + tc * 512: dk * T + tc * 512 + 512],
                                               start=(dk == 0), stop=(dk == DT8 - 1))
                        kb.tag(mm, "pe", 1)
                    if tc == 0:
                        tensor.wait_ge(kb.sem("dw1"), dwb0 + 16 * 4 * (e + 1))
                    tensor.wait_ge(kb.sem("dve"), dveb(e) + tc * 12 + 4)
                    for dm in range(DT8):
                        gj = (e * 4 + tc) * 8 + dm
                        if gj >= 2:
                            tensor.wait_ge(kb.sem("dve"), y1_tag(gj - 2))
                        for it in range(4):
                            mm = tensor.matmul(PO[:, (gj % 2) * 512:(gj % 2) * 512 + 512],
                                               WDB[:, it * 1024 + dm * 128: it * 1024 + dm * 128 + 128],
                                               A2[:, it * 512:(it + 1) * 512],
                                               start=(it == 0), stop=(it == 3))
                        kb.tag(mm, "pe", 1)

        @block.scalar
        def _(scalar):
            for e in range(NE):
                for q in range(4):
                    scalar.wait_ge(kb.sem("pe"), peb(e) + q + 1)
                    if e >= 1:
                        scalar.wait_ge(kb.sem("dve"), dveb(e - 1) + q * 12 + 4 + 8)
                    kb.tag(scalar.copy(out=CWB[:, q * 512:(q + 1) * 512],
                                       in_=PCW[:, (q % 2) * 512:(q % 2) * 512 + 512]), "act", 1)
                for tc in range(TC4):
                    for it in range(4):
                        gi = (e * 4 + tc) * 4 + it
                        scalar.wait_ge(kb.sem("pe"), peb(e) + 4 + tc * 16 + 2 * it + 1)
                        if gi >= 2:
                            scalar.wait_ge(kb.sem("dve"), a2_tag(gi - 2))
                        kb.tag(scalar.activation(out=SG[:, (gi % 2) * 512:(gi % 2) * 512 + 512],
                                                 in_=PG[:, (gi % 2) * 512:(gi % 2) * 512 + 512],
                                                 func=AF.Silu), "act", 1)

        @block.vector
        def _(vector):
            for e in range(NE):
                for tc in range(TC4):
                    for it in range(4):
                        gi = (e * 4 + tc) * 4 + it
                        vector.wait_ge(kb.sem("act"), silu_tag(gi))
                        vector.wait_ge(kb.sem("pe"), peb(e) + 4 + tc * 16 + 2 * it + 2)
                        kb.tag(vector.tensor_mul(A2[:, it * 512:(it + 1) * 512],
                                                 SG[:, (gi % 2) * 512:(gi % 2) * 512 + 512],
                                                 PU[:, (gi % 2) * 512:(gi % 2) * 512 + 512]), "dve")
                    for dm in range(DT8):
                        gj = (e * 4 + tc) * 8 + dm
                        vector.wait_ge(kb.sem("pe"), peb(e) + 4 + tc * 16 + 8 + dm + 1)
                        kb.tag(vector.tensor_mul(Y1[:, (gj % 2) * 512:(gj % 2) * 512 + 512],
                                                 PO[:, (gj % 2) * 512:(gj % 2) * 512 + 512],
                                                 CWB[:, tc * 512:(tc + 1) * 512]), "dve")
                        kb.tag(vector.tensor_tensor(out=ACC[:, dm * T + tc * 512: dm * T + tc * 512 + 512],
                                                    in0=ACC[:, dm * T + tc * 512: dm * T + tc * 512 + 512],
                                                    in1=Y1[:, (gj % 2) * 512:(gj % 2) * 512 + 512],
                                                    op=ALU.add), "gp")

    with nc.Block() as block, ExitStack() as esG2:
        WSG = esG2.enter_context(nc.sbuf_tensor("q_wsg", [128, DT8 * 1024], BF16))
        WSU = esG2.enter_context(nc.sbuf_tensor("q_wsu", [128, DT8 * 1024], BF16))
        WSD = esG2.enter_context(nc.sbuf_tensor("q_wsd", [128, DT8 * 1024], BF16))
        SGA = esG2.enter_context(nc.sbuf_tensor("q_sga", [128, DT8 * 512], BF16))
        SGS = esG2.enter_context(nc.sbuf_tensor("q_sgs", [128, 2 * 512], F32))
        Y1 = esG2.enter_context(nc.sbuf_tensor("q_y1b", [128, 2 * 512], F32))
        PG = esG2.enter_context(nc.psum_tensor("q_pg2", [128, 2 * 512], F32))
        PU = esG2.enter_context(nc.psum_tensor("q_pu2", [128, 2 * 512], F32))
        PO = esG2.enter_context(nc.psum_tensor("q_po2", [128, 2 * 512], F32))

        pe2, act2, dve2, gp2 = (kb.val(x) for x in ("pe", "act", "dve", "gp"))
        dwa1, dwb1 = kb.val("dw0"), kb.val("dw1")

        def sga_tag(gi):
            tc_, si_ = divmod(gi, 8)
            return dve2 + tc_ * 8 + si_ + 1

        def acc_tag(gj):
            return gp2 + gj + 1

        @block.gpsimd
        def _(g):
            for dk in range(DT8):
                kb.tag(g.dma_start(out=WSG[:, dk * 1024:(dk + 1) * 1024],
                                   in_=wsg[dk * 128:(dk + 1) * 128, :]), "dw0", 16)
                kb.tag(g.dma_start(out=WSU[:, dk * 1024:(dk + 1) * 1024],
                                   in_=wsu[dk * 128:(dk + 1) * 128, :]), "dw0", 16)
                kb.tag(g.dma_start(out=WSD[:, dk * 1024:(dk + 1) * 1024],
                                   in_=wsd[dk * 128:(dk + 1) * 128, :]), "dw1", 16)


        @block.tensor
        def _(tensor):
            tensor.wait_ge(kb.sem("dw0"), dwa1 + 16 * 16)
            for tc in range(TC4):
                for si in range(DT8):
                    gi = tc * 8 + si
                    if gi >= 2:
                        tensor.wait_ge(kb.sem("act"), act2 + gi - 1)
                        tensor.wait_ge(kb.sem("dve"), sga_tag(gi - 2))
                    for dk in range(DT8):
                        mm = tensor.matmul(PG[:, (gi % 2) * 512:(gi % 2) * 512 + 512],
                                           WSG[:, dk * 1024 + si * 128: dk * 1024 + si * 128 + 128],
                                           XN[:, dk * T + tc * 512: dk * T + tc * 512 + 512],
                                           start=(dk == 0), stop=(dk == DT8 - 1))
                    kb.tag(mm, "pe", 1)
                    for dk in range(DT8):
                        mm = tensor.matmul(PU[:, (gi % 2) * 512:(gi % 2) * 512 + 512],
                                           WSU[:, dk * 1024 + si * 128: dk * 1024 + si * 128 + 128],
                                           XN[:, dk * T + tc * 512: dk * T + tc * 512 + 512],
                                           start=(dk == 0), stop=(dk == DT8 - 1))
                    kb.tag(mm, "pe", 1)
                if tc == 0:
                    tensor.wait_ge(kb.sem("dw1"), dwb1 + 16 * 8)
                tensor.wait_ge(kb.sem("dve"), dve2 + tc * 8 + 8)
                for dm in range(DT8):
                    gj = tc * 8 + dm
                    if gj >= 2:
                        tensor.wait_ge(kb.sem("gp"), acc_tag(gj - 2))
                    for si in range(DT8):
                        mm = tensor.matmul(PO[:, (gj % 2) * 512:(gj % 2) * 512 + 512],
                                           WSD[:, si * 1024 + dm * 128: si * 1024 + dm * 128 + 128],
                                           SGA[:, si * 512:(si + 1) * 512],
                                           start=(si == 0), stop=(si == DT8 - 1))
                    kb.tag(mm, "pe", 1)

        @block.scalar
        def _(scalar):
            for tc in range(TC4):
                for si in range(DT8):
                    gi = tc * 8 + si
                    scalar.wait_ge(kb.sem("pe"), pe2 + tc * 24 + 2 * si + 1)
                    if gi >= 2:
                        scalar.wait_ge(kb.sem("dve"), sga_tag(gi - 2))
                    kb.tag(scalar.activation(out=SGS[:, (gi % 2) * 512:(gi % 2) * 512 + 512],
                                             in_=PG[:, (gi % 2) * 512:(gi % 2) * 512 + 512],
                                             func=AF.Silu), "act", 1)

        @block.vector
        def _(vector):
            for tc in range(TC4):
                for si in range(DT8):
                    gi = tc * 8 + si
                    vector.wait_ge(kb.sem("act"), act2 + gi + 1)
                    vector.wait_ge(kb.sem("pe"), pe2 + tc * 24 + 2 * si + 2)
                    kb.tag(vector.tensor_mul(SGA[:, si * 512:(si + 1) * 512],
                                             SGS[:, (gi % 2) * 512:(gi % 2) * 512 + 512],
                                             PU[:, (gi % 2) * 512:(gi % 2) * 512 + 512]), "dve")
                for dm in range(DT8):
                    gj = tc * 8 + dm
                    vector.wait_ge(kb.sem("pe"), pe2 + tc * 24 + 16 + dm + 1)
                    kb.tag(vector.tensor_tensor(out=ACC[:, dm * T + tc * 512: dm * T + tc * 512 + 512],
                                                in0=ACC[:, dm * T + tc * 512: dm * T + tc * 512 + 512],
                                                in1=PO[:, (gj % 2) * 512:(gj % 2) * 512 + 512],
                                                op=ALU.add), "gp")

    with nc.Block() as block, ExitStack() as esG3:
        GS = esG3.enter_context(nc.sbuf_tensor("q_gs", [128, T], F32))
        S1T = esG3.enter_context(nc.sbuf_tensor("q_s1t", [128, T], F32))
        X1T = esG3.enter_context(nc.sbuf_tensor("q_x1t", [128, T], F32))
        X2F = esG3.enter_context(nc.sbuf_tensor("q_x2f", [128, T], F32))

        pe3, act3, dve3 = (kb.val(x) for x in ("pe", "act", "dve"))
        du0, dzr0, dof0 = kb.val("du_"), kb.val("dzr"), kb.val("dof")

        @block.gpsimd
        def _(g):
            for dk in range(DT8):
                if dk >= 1:
                    g.wait_ge(kb.sem("dve"), dve3 + dk * 2)
                kb.tag(g.dma_start(out=S1T[:], in_=hbm["s1f"].rearrange("(n p) t -> n p t", p=128)[dk]),
                       "du_", 16)
                kb.tag(g.dma_start(out=X1T[:], in_=hbm["x1f"].rearrange("(n p) t -> n p t", p=128)[dk]),
                       "dzr", 16)

        @block.scalar
        def _(scalar):
            for dk in range(DT8):
                if dk >= 1:
                    scalar.wait_ge(kb.sem("dve"), dve3 + dk * 2)
                kb.tag(scalar.activation(out=GS[:], in_=ACC[:, dk * T:(dk + 1) * T], func=AF.Silu), "act", 1)

        @block.vector
        def _(vector):
            for dk in range(DT8):
                vector.wait_ge(kb.sem("act"), act3 + dk + 1)
                vector.wait_ge(kb.sem("du_"), du0 + 16 * (dk + 1))
                vector.wait_ge(kb.sem("dzr"), dzr0 + 16 * (dk + 1))
                if dk >= 1:
                    vector.wait_ge(kb.sem("dof"), dof0 + 16 * dk)
                t1 = kb.tag(vector.tensor_mul(GS[:], GS[:], S1T[:]), "dve")
                vector.wait_ge(kb.sem("dve"), t1)
                kb.tag(vector.tensor_add(X2F[:], GS[:], X1T[:]), "dve")

        @block.sync
        def _(sync):
            x2h = hbm["x2f"].rearrange("(n p) t -> n p t", p=128)
            for dk in range(DT8):
                sync.wait_ge(kb.sem("dve"), dve3 + dk * 2 + 2)
                kb.tag(sync.dma_start(out=x2h[dk], in_=X2F[:]), "dof", 16)
            sync.wait_ge(kb.sem("dof"), kb.val("dof"))

        kb.drain_body(block)
    esS.close()


def sec_fin(kb, hbm):
    nc = kb.nc
    esS = ExitStack()
    EYE = esS.enter_context(nc.sbuf_tensor("f_eye", [128, 128], F32))
    X2 = esS.enter_context(nc.sbuf_tensor("f_x2", [128, DT8 * T], F32))
    OT = esS.enter_context(nc.sbuf_tensor("f_ot", [128, 2 * D], F32))
    PTF = esS.enter_context(nc.psum_tensor("f_ptf", [128, 4 * 512], F32))

    pe0, act0 = kb.val("pe"), kb.val("act")
    ds0, dof0, dob0 = kb.val("ds"), kb.val("dof"), kb.val("dob")
    with nc.Block() as block:
        @block.sync
        def _(sync):
            kb.tag(sync.dma_start(out=EYE[:], in_=hbm["c_eye_f"]), "ds", 16)
            kb.tag(sync.dma_start(out=X2[:].rearrange("p (n t) -> p n t", t=T),
                                  in_=hbm["x2f"].rearrange("(n p) t -> p n t", p=128)), "ds", 16)
        f_ld = kb.val("ds")

        @block.tensor
        def _(tensor):
            tensor.wait_ge(kb.sem("ds"), f_ld)
            for i in range(128):
                tt, dk = divmod(i, DT8)
                if i >= 4:
                    tensor.wait_ge(kb.sem("act"), act0 + (i - 3))
                kb.tag(tensor.transpose(PTF[:, (i % 4) * 512:(i % 4) * 512 + 128],
                                        X2[:, dk * T + tt * 128: dk * T + tt * 128 + 128],
                                        EYE[:]), "pe", 1)

        @block.scalar
        def _(scalar):
            for i in range(128):
                tt, dk = divmod(i, DT8)
                scalar.wait_ge(kb.sem("pe"), pe0 + i + 1)
                if dk == 0 and tt >= 2:
                    par = "dof" if tt % 2 == 0 else "dob"
                    scalar.wait_ge(kb.sem(par), (dof0 if tt % 2 == 0 else dob0) + 16 * ((tt - 2) // 2 + 1))
                kb.tag(scalar.copy(out=OT[:, (tt % 2) * D + dk * 128:(tt % 2) * D + dk * 128 + 128],
                                   in_=PTF[:, (i % 4) * 512:(i % 4) * 512 + 128]), "act", 1)

        @block.sync
        def _(sync):
            outh = hbm["out"].rearrange("(n p) d -> n p d", p=128)
            for tt in range(TT16):
                sync.wait_ge(kb.sem("act"), act0 + (tt + 1) * DT8)
                kb.tag(sync.dma_start(out=outh[tt], in_=OT[:, (tt % 2) * D:(tt % 2) * D + D]),
                       "dof" if tt % 2 == 0 else "dob", 16)
            sync.wait_ge(kb.sem("dof"), kb.val("dof"))
            sync.wait_ge(kb.sem("dob"), kb.val("dob"))

        kb.drain_body(block)
    esS.close()



MW = ["L0_attn_in_proj_w", "L0_attn_conv_w", "L0_attn_conv_b", "L0_attn_x_proj_w",
      "L0_attn_dt_proj_w", "L0_attn_dt_proj_b", "L0_attn_A_log", "L0_attn_D_skip",
      "L0_attn_out_proj_w", "L0_attn_ln_w", "L0_attn_ln_b"]

MWE = MW + ["L0_ffn_W1", "L0_ffn_W2", "L0_ffn_b2"]

SECTIONS = {
    "in": (sec_in, ["L0_attn_x"], [], ["x0T", "xn0f", "xn0b"]),
    "gate0": (sec_gate0, ["L0_ffn_Wg", "L0_ffn_bg", "L0_ffn_bias", "L0_ffn_b1", "L0_ffn_b2"],
              ["xn0b"], ["cwt0", "b1c", "b2c"]),
    "m0": (lambda kb, hbm: sec_mamba(kb, hbm, 0, "xn0f", "xn0b", "mx1f", "mx1b"),
           MWE, ["xn0f", "xn0b", "cwt0", "b1c", "b2c"], ["mx1f", "mx1b", "zbuf", "xcf", "ygb", "ytmp", "g0f"]),
    "m1": (lambda kb, hbm: sec_mamba(kb, hbm, 1, "mx1f", "mx1b", "mx2f", "mx2b"),
           MWE, ["mx1f", "mx1b", "xn0b", "cwt0", "b1c", "b2c"], ["mx2f", "mx2b", "zbuf", "xcf", "ygb", "ytmp", "g0f"]),
    "m2": (lambda kb, hbm: sec_mamba(kb, hbm, 2, "mx2f", "mx2b", "mx3f", "mx3b"),
           MWE, ["mx2f", "mx2b", "xn0b", "cwt0", "b1c", "b2c"], ["mx3f", "mx3b", "zbuf", "xcf", "ygb", "ytmp", "g0f"]),
    "m3": (lambda kb, hbm: sec_mamba(kb, hbm, 3, "mx3f", "mx3b", "s0f", None),
           MWE, ["mx3f", "mx3b", "xn0b", "cwt0", "b1c", "b2c"], ["s0f", "zbuf", "xcf", "ygb", "ytmp", "g0f"]),
    "moe0fin": (sec_moe0fin, [], ["g0f", "s0f", "x0T"], ["x1f", "xn1b"]),
    "att1": (sec_att1, ["L1_attn_Wq", "L1_attn_Wk", "L1_attn_Wv", "L1_attn_Wo"],
             ["xn1b"], ["s1f"]),
    "moe1": (sec_moe1, ["L1_ffn_gate_w", "L1_ffn_w_gate", "L1_ffn_w_up", "L1_ffn_w_down",
                        "L1_ffn_ws_gate", "L1_ffn_ws_up", "L1_ffn_ws_down"],
             ["xn1b", "s1f", "x1f"], ["x2f"]),
    "fin": (sec_fin, [], ["x2f"], []),
}

CONST_KEYS = [k for k in WSPECS if k.startswith("c_")]

LAUNCH_PLAN = [["in", "gate0", "m0", "m1", "m2", "m3", "moe0fin", "att1", "moe1", "fin"]]


def make_constants():
    c = {}
    c["c_eye_f"] = np.eye(128, dtype=np.float32)
    c["c_eye_b"] = np.eye(128, dtype=np.float32).astype(ml_dtypes.bfloat16)
    c["c_onesrow_f"] = np.ones((1, 128), np.float32)
    c["c_onesrow_b"] = np.ones((1, 128), np.float32).astype(ml_dtypes.bfloat16)
    c["c_onescol_f"] = np.ones((128, 1), np.float32)
    sel16 = np.zeros((16, 16 * 128), np.float32)
    for n in range(16):
        sel16[n, n * 128:(n + 1) * 128] = 1.0
    c["c_sel16_f"] = sel16
    sel32 = np.zeros((32, 32 * 128), np.float32)
    for n in range(32):
        sel32[n, n * 128:(n + 1) * 128] = 1.0
    c["c_sel32_b"] = sel32.astype(ml_dtypes.bfloat16)
    m = np.tril(np.ones((128, 128), np.float32))
    c["c_maskT_b"] = m.T.copy().astype(ml_dtypes.bfloat16)
    return c


def build_launch(sections, ext_in, ext_out):
    nc = bass.Bass("TRN2", target_bir_lowering=False)
    hbm = {}
    need_w = set()
    reads = set()
    writes = set()
    for s in sections:
        fn, wk, rd, wr = SECTIONS[s]
        need_w.update(wk)
        reads.update(rd)
        writes.update(wr)
    need_w.update(CONST_KEYS)
    for k in sorted(need_w):
        shape, dt = WSPECS[k]
        hbm[k] = nc.dram_tensor(k, shape, dt, kind="ExternalInput").ap()
    for k in sorted(reads | writes):
        shape, dt = INTER[k]
        if k in ext_in and k not in writes:
            kind = "ExternalInput"
        elif k in ext_out:
            kind = "ExternalOutput"
        else:
            kind = "Internal"
        hbm[k] = nc.dram_tensor(k, shape, dt, kind=kind).ap()
    if "fin" in sections:
        hbm["out"] = nc.dram_tensor("out", [T, D], F32, kind="ExternalOutput").ap()

    kb = KB(nc)
    for s in sections:
        SECTIONS[s][0](kb, hbm)
    kb.es.close()
    in_names = sorted(need_w) + [k for k in sorted(reads) if k in ext_in and k not in writes]
    return nc, in_names


def kernel(**inputs):
    consts = make_constants()
    x = np.ascontiguousarray(inputs["L0_attn_x"])
    base = {}
    for k, v in inputs.items():
        if k == "L0_attn_x":
            continue
        a = np.ascontiguousarray(v)
        shape, _ = WSPECS[k]
        base[k] = a.reshape(shape)
    base.update(consts)

    prod_launch = {}
    for li, group in enumerate(LAUNCH_PLAN):
        for s in group:
            for w in SECTIONS[s][3]:
                prod_launch[w] = li
    cross = set()
    for li, group in enumerate(LAUNCH_PLAN):
        for s in group:
            for r in SECTIONS[s][2]:
                if prod_launch.get(r, li) != li:
                    cross.add(r)

    core_vals = [dict() for _ in range(NCORES)]
    out_final = None
    for li, group in enumerate(LAUNCH_PLAN):
        reads = set()
        writes = set()
        need_w = set()
        for s in group:
            fn, wk, rd, wr = SECTIONS[s]
            need_w.update(wk)
            reads.update(rd)
            writes.update(wr)
        ext_in = {r for r in reads if r in cross}
        ext_out = {w for w in writes if w in cross} | (writes & set(DEV_CAPTURE))
        nc, in_names = build_launch(group, ext_in, ext_out)
        in_maps = []
        for c in range(NCORES):
            m = {}
            for k in in_names:
                if k in base:
                    m[k] = base[k]
                elif k == "L0_attn_x":
                    pass
                else:
                    m[k] = core_vals[c][k]
            if "L0_attn_x" in in_names:
                m["L0_attn_x"] = np.ascontiguousarray(x[c])
            in_maps.append(m)
        res = run_bass_kernel_spmd(nc, in_maps, core_ids=list(range(NCORES)))
        for c in range(NCORES):
            for k, v in res.results[c].items():
                if k == "out":
                    continue
                core_vals[c][k] = np.asarray(v)
        if "out" in res.results[0]:
            out_final = np.stack([np.asarray(res.results[c]["out"]) for c in range(NCORES)], axis=0)
    if out_final is None:
        raise RuntimeError("LAUNCH_PLAN does not produce 'out'")
    return out_final


DEV_CAPTURE = []


# version 23
# speedup vs baseline: 1.0165x; 1.0165x over previous
import numpy as np
import ml_dtypes
from contextlib import ExitStack

import concourse.bass as bass
import concourse.mybir as mybir
import concourse.bass_utils as _bu
from concourse.bass_utils import run_bass_kernel_spmd

_orig_bvo = _bu.bir_verify_and_optimise


def _patched_bvo(tmpdir, inp="bir.json", outp="file.neff", arch=None, *, dve_root=None):
    orig_run = _bu.run_command

    def run_patched(cmd, **kw):
        cmd = ["--enable-ldw-opt=true" if c == "--enable-ldw-opt=false" else c for c in cmd]
        return orig_run(cmd, **kw)

    _bu.run_command = run_patched
    try:
        return _orig_bvo(tmpdir, inp, outp, arch, dve_root=dve_root)
    finally:
        _bu.run_command = orig_run


_bu.bir_verify_and_optimise = _patched_bvo

F32 = mybir.dt.float32
BF16 = mybir.dt.bfloat16
AF = mybir.ActivationFunctionType
ALU = mybir.AluOpType

D, T, DI, NST, DTR = 1024, 2048, 2048, 16, 64
NE, HEX, IE, ISH, NL = 16, 2048, 512, 1024, 4
LN_EPS, ATT_EPS = 1e-5, 1e-6
NCORES = 8
DT8 = D // 128
DIT = DI // 128
TC4 = T // 512
TT16 = T // 128


class KB:

    def __init__(self, nc):
        self.nc = nc
        self.es = ExitStack()
        self.cnt = {}
        self.sems = {}
        for name in ("pe", "act", "dve", "gp", "dg", "ds", "da", "dxt", "dw0", "dw1",
                     "dz0", "dz1", "dxc0", "dxc1", "du_", "dzr", "dyr", "dyt", "dyg",
                     "dof", "dob", "mpe", "mact", "mdve", "mw0", "mw1", "mxq", "mcw", "mgo"):
            self.sems[name] = self.es.enter_context(nc.semaphore("s_" + name))
            self.cnt[name] = 0

    def sem(self, n):
        return self.sems[n]

    def tag(self, instr, n, k=1):
        instr.then_inc(self.sems[n], k)
        self.cnt[n] += k
        return self.cnt[n]

    def val(self, n):
        return self.cnt[n]

    def sbuf(self, name, shape, dtype):
        return self.es.enter_context(self.nc.sbuf_tensor(name, shape, dtype))

    def psum(self, name, shape, dtype=F32):
        return self.es.enter_context(self.nc.psum_tensor(name, shape, dtype))

    DMA_SEMS = ("dg", "ds", "da", "dxt", "dw0", "dw1", "dz0", "dz1", "dxc0", "dxc1",
                "du_", "dzr", "dyr", "dyt", "dyg", "dof", "dob", "mw0", "mw1", "mxq", "mcw", "mgo")

    def drain_body(self, block):
        vals = [(self.sems[n], self.cnt[n]) for n in self.DMA_SEMS if self.cnt[n] > 0]

        def _(sync):
            for sem, v in vals:
                sync.wait_ge(sem, v)
        block.sync(_)


class SerialEmitter:

    ENGS = ("tensor", "vector", "scalar", "gpsimd", "sync")
    SEM = {"tensor": "pe", "vector": "dve", "scalar": "act", "gpsimd": "gp", "sync": "ds"}

    def __init__(self, kb):
        self.kb = kb
        self.steps = []

    def step(self, eng, fn, dma=False):
        self.steps.append((eng, fn, dma))

    def emit(self, block):
        kb = self.kb
        base = {e: kb.cnt[self.SEM[e]] for e in self.ENGS}
        cnt = dict(base)
        tagv = []
        for eng, fn, dma in self.steps:
            s = self.SEM[eng]
            cnt[eng] = cnt[eng]
            inc = 16 if dma else 1
            base[eng] += 0
            tagv.append((self.SEM[eng], None))
        run = {e: kb.cnt[self.SEM[e]] for e in self.ENGS}
        for i, (eng, fn, dma) in enumerate(self.steps):
            run[eng] += 16 if dma else 1
            tagv[i] = (self.SEM[eng], run[eng])

        def make_body(myeng):
            mysteps = [(i, fn, dma) for i, (e, fn, dma) in enumerate(self.steps) if e == myeng]
            if not mysteps:
                return None

            def body(engine):
                for i, fn, dma in mysteps:
                    if i > 0:
                        psem, pval = tagv[i - 1]
                        engine.wait_ge(kb.sems[psem], pval)
                    instr = fn(engine)
                    kb.tag(instr, tagv[i][0], 16 if dma else 1)
            return body

        for e in self.ENGS:
            b = make_body(e)
            if b is not None:
                getattr(block, {"tensor": "tensor", "vector": "vector", "scalar": "scalar",
                                "gpsimd": "gpsimd", "sync": "sync"}[e])(b)


INTER = {
    "x0T": ([D, T], F32),
    "xn0f": ([D, T], F32),
    "xn0b": ([D, T], BF16),
    "mx1f": ([D, T], F32), "mx1b": ([D, T], BF16),
    "mx2f": ([D, T], F32), "mx2b": ([D, T], BF16),
    "mx3f": ([D, T], F32), "mx3b": ([D, T], BF16),
    "s0f": ([D, T], F32), "s0b": ([D, T], BF16),
    "x1f": ([D, T], F32), "x1b": ([D, T], BF16),
    "xn1b": ([D, T], BF16),
    "s1f": ([D, T], F32),
    "x2f": ([D, T], F32),
    "zbuf": ([DI, T], BF16),
    "xcf": ([DI, T], F32),
    "ygb": ([DI, T], BF16),
    "ytmp": ([DI, T], BF16),
    "cwt0": ([16, T], F32),
    "b1c": ([128, 256], F32),
    "b2c": ([128, 128], F32),
    "g0f": ([D, T], F32),
}

WSPECS = {
    "L0_attn_x": ([T, D], F32),
    "L0_attn_in_proj_w": ([NL, 2 * DI, D], F32),
    "L0_attn_conv_w": ([NL, DI, 4], F32),
    "L0_attn_conv_b": ([NL, DI], F32),
    "L0_attn_x_proj_w": ([NL, DTR + 2 * NST, DI], F32),
    "L0_attn_dt_proj_w": ([NL, DI, DTR], F32),
    "L0_attn_dt_proj_b": ([NL, DI], F32),
    "L0_attn_A_log": ([NL, DI, NST], F32),
    "L0_attn_D_skip": ([NL, DI], F32),
    "L0_attn_out_proj_w": ([NL, D, DI], F32),
    "L0_attn_ln_w": ([NL, D], F32),
    "L0_attn_ln_b": ([NL, D], F32),
    "L0_ffn_Wg": ([D, NE], F32),
    "L0_ffn_bg": ([1, NE], F32),
    "L0_ffn_bias": ([1, NE], F32),
    "L0_ffn_W1": ([NE, D, HEX], F32),
    "L0_ffn_b1": ([NE, HEX], F32),
    "L0_ffn_W2": ([NE, HEX, D], F32),
    "L0_ffn_b2": ([NE, D], F32),
    "L1_attn_Wq": ([256, D], F32),
    "L1_attn_Wk": ([256, D], F32),
    "L1_attn_Wv": ([256, D], F32),
    "L1_attn_Wo": ([D, 256], F32),
    "L1_ffn_gate_w": ([NE, D], F32),
    "L1_ffn_w_gate": ([NE, D, IE], F32),
    "L1_ffn_w_up": ([NE, D, IE], F32),
    "L1_ffn_w_down": ([NE, IE, D], F32),
    "L1_ffn_ws_gate": ([D, ISH], F32),
    "L1_ffn_ws_up": ([D, ISH], F32),
    "L1_ffn_ws_down": ([ISH, D], F32),
    "c_eye_f": ([128, 128], F32),
    "c_eye_b": ([128, 128], BF16),
    "c_onesrow_f": ([1, 128], F32),
    "c_onesrow_b": ([1, 128], BF16),
    "c_onescol_f": ([128, 1], F32),
    "c_sel16_f": ([16, 16 * 128], F32),
    "c_sel32_b": ([32, 32 * 128], BF16),
    "c_maskT_b": ([128, 128], BF16),
}


def emit_ln_stats(kb, block_dummy, V, SQ, ROWS, PS1, PS2, ONESC, wait_list):
    raise RuntimeError("inlined")


TRUNC = 0


def sec_in(kb, hbm):
    nc = kb.nc
    esS = ExitStack()
    EYE = esS.enter_context(nc.sbuf_tensor("si_eye", [128, 128], F32))
    ONESC = esS.enter_context(nc.sbuf_tensor("si_onesc", [128, 1], F32))
    ONESR = esS.enter_context(nc.sbuf_tensor("si_onesr", [1, 128], F32))
    V = esS.enter_context(nc.sbuf_tensor("si_v", [128, DT8 * T], F32))
    ROWS = esS.enter_context(nc.sbuf_tensor("si_rows", [1, 4 * T], F32))
    SQ = esS.enter_context(nc.sbuf_tensor("si_sq", [128, 2 * 512], F32))
    XB16 = esS.enter_context(nc.sbuf_tensor("si_xb", [128, DT8 * T], BF16))
    XH = esS.enter_context(nc.sbuf_tensor("si_xh", [128, 512], F32))
    PT = esS.enter_context(nc.psum_tensor("si_pt", [128, 4 * 512], F32))
    PS1 = esS.enter_context(nc.psum_tensor("si_ps1", [1, 512], F32))
    PS2 = esS.enter_context(nc.psum_tensor("si_ps2", [1, 512], F32))
    PBa = esS.enter_context(nc.psum_tensor("si_pba", [128, 512], F32))
    PBb = esS.enter_context(nc.psum_tensor("si_pbb", [128, 512], F32))

    ds0, pe0, act0 = kb.val("ds"), kb.val("pe"), kb.val("act")

    with nc.Block() as block, nc.sbuf_tensor("si_xtok", [128, TT16 * D], F32) as XTOK:
        @block.sync
        def _(sync):
            kb.tag(sync.dma_start(out=EYE[:], in_=hbm["c_eye_f"]), "ds", 16)
            kb.tag(sync.dma_start(out=ONESC[:], in_=hbm["c_onescol_f"]), "ds", 16)
            kb.tag(sync.dma_start(out=ONESR[:], in_=hbm["c_onesrow_f"]), "ds", 16)
            xr = hbm["L0_attn_x"].rearrange("(n p) d -> n p d", p=128)
            for tt in range(TT16):
                kb.tag(sync.dma_start(out=XTOK[:, tt * D:(tt + 1) * D], in_=xr[tt]), "ds", 16)
        ld_done = ds0 + 16 * (3 + TT16)

        if TRUNC == 3:
            @block.sync
            def _(sync):
                sync.wait_ge(kb.sem("ds"), ld_done)
                kb.tag(sync.dma_start(out=hbm["x0T"][0:128, :], in_=XTOK[:, 0:T]), "ds", 16)
                sync.wait_ge(kb.sem("ds"), kb.val("ds"))
            return

        @block.tensor
        def _(tensor):
            tensor.wait_ge(kb.sem("ds"), ld_done)
            for i in range(128):
                tt, dk = divmod(i, DT8)
                if i >= 4:
                    tensor.wait_ge(kb.sem("act"), act0 + (i - 3))
                kb.tag(tensor.transpose(PT[:, (i % 4) * 512:(i % 4) * 512 + 128],
                                        XTOK[:, tt * D + dk * 128: tt * D + dk * 128 + 128],
                                        EYE[:]), "pe", 1)

        @block.scalar
        def _(scalar):
            for i in range(128):
                tt, dk = divmod(i, DT8)
                scalar.wait_ge(kb.sem("pe"), pe0 + i + 1)
                kb.tag(scalar.copy(out=V[:, dk * T + tt * 128: dk * T + tt * 128 + 128],
                                   in_=PT[:, (i % 4) * 512:(i % 4) * 512 + 128]), "act", 1)
    pe_b = pe0 + 128
    act_b = act0 + 128

    if TRUNC == 1:
        with nc.Block() as block:
            @block.sync
            def _(sync):
                kb.tag(sync.dma_start(out=hbm["x0T"].rearrange("(n p) t -> p n t", p=128),
                                      in_=V[:].rearrange("p (n t) -> p n t", t=T)), "ds", 16)
                kb.tag(sync.dma_start(out=hbm["xn0f"].rearrange("(n p) t -> p n t", p=128),
                                      in_=V[:].rearrange("p (n t) -> p n t", t=T)), "ds", 16)
                kb.tag(sync.dma_start(out=hbm["xn0b"].rearrange("(n p) t -> p n t", p=128),
                                      in_=XB16[:].rearrange("p (n t) -> p n t", t=T)), "ds", 16)
                sync.wait_ge(kb.sem("ds"), kb.val("ds"))
        return

    with nc.Block() as block:
        @block.sync
        def _(sync):
            kb.tag(sync.dma_start(out=hbm["x0T"].rearrange("(n p) t -> p n t", p=128),
                                  in_=V[:].rearrange("p (n t) -> p n t", t=T)), "ds", 16)
        x0T_done = kb.val("ds")

        @block.scalar
        def _(scalar):
            for tc in range(TC4):
                for dk in range(DT8):
                    i = tc * DT8 + dk
                    if i >= 2:
                        scalar.wait_ge(kb.sem("pe"), pe_b + (i - 2) + 1)
                    kb.tag(scalar.activation(out=SQ[:, (i % 2) * 512:(i % 2) * 512 + 512],
                                             in_=V[:, dk * T + tc * 512: dk * T + tc * 512 + 512],
                                             func=AF.Square), "act", 1)
                scalar.wait_ge(kb.sem("pe"), pe_b + (tc + 1) * DT8)
                scalar.copy(out=ROWS[:, 0 * T + tc * 512: 0 * T + tc * 512 + 512], in_=PS1[:])
                kb.tag(scalar.copy(out=ROWS[:, 1 * T + tc * 512: 1 * T + tc * 512 + 512], in_=PS2[:]), "act", 1)

        @block.tensor
        def _(tensor):
            for tc in range(TC4):
                for dk in range(DT8):
                    i = tc * DT8 + dk
                    tensor.wait_ge(kb.sem("act"), act_b + tc * 9 + dk + 1)
                    tensor.matmul(PS1[:], ONESC[:], V[:, dk * T + tc * 512: dk * T + tc * 512 + 512],
                                  start=(dk == 0), stop=(dk == DT8 - 1))
                    kb.tag(tensor.matmul(PS2[:], ONESC[:], SQ[:, (i % 2) * 512:(i % 2) * 512 + 512],
                                         start=(dk == 0), stop=(dk == DT8 - 1)), "pe", 1)
        pe_s = pe_b + 32
        act_s = act_b + 36

        dve0 = kb.val("dve")

        @block.vector
        def _(vector):
            vector.wait_ge(kb.sem("act"), act_s)
            sm = ROWS[:, 0:T]
            sq = ROWS[:, T:2 * T]
            al = ROWS[:, 2 * T:3 * T]
            be = ROWS[:, 3 * T:4 * T]
            t1 = kb.tag(vector.tensor_scalar(out=al, in0=sm, scalar1=1.0 / D, scalar2=None, op0=ALU.mult), "dve")
            vector.wait_ge(kb.sem("dve"), t1)
            t2 = kb.tag(vector.tensor_mul(be, al, al), "dve")
            t3 = kb.tag(vector.tensor_scalar(out=sm, in0=sq, scalar1=1.0 / D, scalar2=None, op0=ALU.mult), "dve")
            vector.wait_ge(kb.sem("dve"), t3)
            t4 = kb.tag(vector.tensor_sub(sm, sm, be), "dve")
            vector.wait_ge(kb.sem("dve"), t4)
            kb.tag(vector.tensor_scalar(out=sm, in0=sm, scalar1=LN_EPS, scalar2=None,
                                        op0=ALU.add), "dve", 1)

        @block.scalar
        def _(scalar):
            scalar.wait_ge(kb.sem("dve"), dve0 + 5)
            kb.tag(scalar.activation(out=ROWS[:, T:2 * T], in_=ROWS[:, 0:T],
                                     func=AF.Sqrt), "act", 1)

        @block.vector
        def _(vector):
            vector.wait_ge(kb.sem("act"), act_s + 1)
            t5 = kb.tag(vector.reciprocal(ROWS[:, 0:T], ROWS[:, T:2 * T]), "dve")
            vector.wait_ge(kb.sem("dve"), t5)
            t6 = kb.tag(vector.scalar_tensor_tensor(out=ROWS[:, 3 * T:4 * T], in0=ROWS[:, 2 * T:3 * T],
                                                    scalar=-1.0, in1=ROWS[:, 0:T],
                                                    op0=ALU.mult, op1=ALU.mult), "dve")
            vector.wait_ge(kb.sem("dve"), t6)
            kb.tag(vector.tensor_copy(ROWS[:, 2 * T:3 * T], ROWS[:, 0:T]), "dve", 1)
        dve_r = dve0 + 8

        pe_bc0 = pe_s

        @block.tensor
        def _(tensor):
            tensor.wait_ge(kb.sem("dve"), dve_r)
            for tc in range(TC4):
                if tc >= 1:
                    tensor.wait_ge(kb.sem("dve"), dve_r + tc * 16)
                tensor.matmul(PBa[:], ONESR[:], ROWS[:, 2 * T + tc * 512: 2 * T + tc * 512 + 512],
                              start=True, stop=True)
                kb.tag(tensor.matmul(PBb[:], ONESR[:], ROWS[:, 3 * T + tc * 512: 3 * T + tc * 512 + 512],
                                     start=True, stop=True), "pe", 1)

        @block.vector
        def _(vector):
            vector.wait_ge(kb.sem("ds"), x0T_done)
            prev_add = None
            for tc in range(TC4):
                vector.wait_ge(kb.sem("pe"), pe_bc0 + tc + 1)
                for dk in range(DT8):
                    sl = slice(dk * T + tc * 512, dk * T + tc * 512 + 512)
                    if prev_add is not None:
                        vector.wait_ge(kb.sem("dve"), prev_add)
                    tm = kb.tag(vector.tensor_mul(XH[:], V[:, sl], PBa[:]), "dve")
                    vector.wait_ge(kb.sem("dve"), tm)
                    prev_add = kb.tag(vector.tensor_add(V[:, sl], XH[:], PBb[:]), "dve", 1)

        @block.scalar
        def _(scalar):
            for tc in range(TC4):
                for dk in range(DT8):
                    scalar.wait_ge(kb.sem("dve"), dve_r + tc * 16 + 2 * dk + 2)
                    sl = slice(dk * T + tc * 512, dk * T + tc * 512 + 512)
                    kb.tag(scalar.copy(out=XB16[:, sl], in_=V[:, sl]), "act", 1)
        act_f = act_s + 1 + 32
        dve_f = dve_r + 64

        @block.sync
        def _(sync):
            sync.wait_ge(kb.sem("dve"), dve_f)
            kb.tag(sync.dma_start(out=hbm["xn0f"].rearrange("(n p) t -> p n t", p=128),
                                  in_=V[:].rearrange("p (n t) -> p n t", t=T)), "ds", 16)
            sync.wait_ge(kb.sem("act"), act_f)
            kb.tag(sync.dma_start(out=hbm["xn0b"].rearrange("(n p) t -> p n t", p=128),
                                  in_=XB16[:].rearrange("p (n t) -> p n t", t=T)), "ds", 16)
            sync.wait_ge(kb.sem("ds"), kb.val("ds"))

        kb.drain_body(block)
    esS.close()





def emit_ln_tail(kb, hbm, V, outf, outb, wvec, bvec, ONESC, ONESR, SQ, ROWS,
                 XOF, XOB, PS1, PS2, PBa, PBb):
    nc = kb.nc
    with nc.Block() as block:
        pe_b = kb.val("pe")
        act_b = kb.val("act")

        @block.scalar
        def _(scalar):
            for tc in range(TC4):
                for dk in range(DT8):
                    i = tc * DT8 + dk
                    if i >= 2:
                        scalar.wait_ge(kb.sem("pe"), pe_b + (i - 2) + 1)
                    kb.tag(scalar.activation(out=SQ[:, (i % 2) * 512:(i % 2) * 512 + 512],
                                             in_=V[:, dk * T + tc * 512: dk * T + tc * 512 + 512],
                                             func=AF.Square), "act", 1)
                scalar.wait_ge(kb.sem("pe"), pe_b + (tc + 1) * DT8)
                scalar.copy(out=ROWS[:, 0 * T + tc * 512: 0 * T + tc * 512 + 512], in_=PS1[:])
                kb.tag(scalar.copy(out=ROWS[:, 1 * T + tc * 512: 1 * T + tc * 512 + 512], in_=PS2[:]), "act", 1)

        @block.tensor
        def _(tensor):
            for tc in range(TC4):
                for dk in range(DT8):
                    i = tc * DT8 + dk
                    tensor.wait_ge(kb.sem("act"), act_b + tc * 9 + dk + 1)
                    tensor.matmul(PS1[:], ONESC[:], V[:, dk * T + tc * 512: dk * T + tc * 512 + 512],
                                  start=(dk == 0), stop=(dk == DT8 - 1))
                    kb.tag(tensor.matmul(PS2[:], ONESC[:], SQ[:, (i % 2) * 512:(i % 2) * 512 + 512],
                                         start=(dk == 0), stop=(dk == DT8 - 1)), "pe", 1)
        pe_s = pe_b + 32
        act_s = act_b + 36
        dve0 = kb.val("dve")

        @block.vector
        def _(vector):
            vector.wait_ge(kb.sem("act"), act_s)
            sm = ROWS[:, 0:T]
            sq = ROWS[:, T:2 * T]
            al = ROWS[:, 2 * T:3 * T]
            be = ROWS[:, 3 * T:4 * T]
            t1 = kb.tag(vector.tensor_scalar(out=al, in0=sm, scalar1=1.0 / D, scalar2=None, op0=ALU.mult), "dve")
            vector.wait_ge(kb.sem("dve"), t1)
            kb.tag(vector.tensor_mul(be, al, al), "dve")
            t3 = kb.tag(vector.tensor_scalar(out=sm, in0=sq, scalar1=1.0 / D, scalar2=None, op0=ALU.mult), "dve")
            vector.wait_ge(kb.sem("dve"), t3)
            t4 = kb.tag(vector.tensor_sub(sm, sm, be), "dve")
            vector.wait_ge(kb.sem("dve"), t4)
            kb.tag(vector.tensor_scalar(out=sm, in0=sm, scalar1=LN_EPS, scalar2=None, op0=ALU.add), "dve")

        @block.scalar
        def _(scalar):
            scalar.wait_ge(kb.sem("dve"), dve0 + 5)
            kb.tag(scalar.activation(out=ROWS[:, T:2 * T], in_=ROWS[:, 0:T], func=AF.Sqrt), "act", 1)

        @block.vector
        def _(vector):
            vector.wait_ge(kb.sem("act"), act_s + 1)
            t5 = kb.tag(vector.reciprocal(ROWS[:, 0:T], ROWS[:, T:2 * T]), "dve")
            vector.wait_ge(kb.sem("dve"), t5)
            t6 = kb.tag(vector.scalar_tensor_tensor(out=ROWS[:, 3 * T:4 * T], in0=ROWS[:, 2 * T:3 * T],
                                                    scalar=-1.0, in1=ROWS[:, 0:T],
                                                    op0=ALU.mult, op1=ALU.mult), "dve")
            vector.wait_ge(kb.sem("dve"), t6)
            kb.tag(vector.tensor_copy(ROWS[:, 2 * T:3 * T], ROWS[:, 0:T]), "dve")
        dve_r = dve0 + 8
        pe_bc0 = pe_s

        @block.tensor
        def _(tensor):
            tensor.wait_ge(kb.sem("dve"), dve_r)
            for tc in range(TC4):
                if tc >= 1:
                    tensor.wait_ge(kb.sem("dve"), dve_r + tc * 16)
                tensor.matmul(PBa[:], ONESR[:], ROWS[:, 2 * T + tc * 512: 2 * T + tc * 512 + 512],
                              start=True, stop=True)
                kb.tag(tensor.matmul(PBb[:], ONESR[:], ROWS[:, 3 * T + tc * 512: 3 * T + tc * 512 + 512],
                                     start=True, stop=True), "pe", 1)

        @block.vector
        def _(vector):
            prev_add = None
            for tc in range(TC4):
                vector.wait_ge(kb.sem("pe"), pe_bc0 + tc + 1)
                for dk in range(DT8):
                    sl = slice(dk * T + tc * 512, dk * T + tc * 512 + 512)
                    if prev_add is not None:
                        vector.wait_ge(kb.sem("dve"), prev_add)
                    tm = kb.tag(vector.tensor_mul(SQ[:, 0:512], V[:, sl], PBa[:]), "dve")
                    vector.wait_ge(kb.sem("dve"), tm)
                    prev_add = kb.tag(vector.tensor_add(V[:, sl], SQ[:, 0:512], PBb[:]), "dve")
        dve_a = dve_r + 64

        act_a0 = act_s + 1
        dof0 = kb.val("dof")
        dob0 = kb.val("dob")
        dve_a0 = dve_r + 64

        @block.vector
        def _(vector):
            for dk in range(DT8):
                for tc in range(TC4):
                    vector.wait_ge(kb.sem("dve"), dve_r + tc * 16 + 2 * (dk + 1))
                if dk >= 1:
                    if outf:
                        vector.wait_ge(kb.sem("dof"), dof0 + 16 * dk)
                    if outb:
                        vector.wait_ge(kb.sem("dob"), dob0 + 16 * dk)
                    vector.wait_ge(kb.sem("act"), act_a0 + dk)
                sl = slice(dk * T, (dk + 1) * T)
                if wvec is None:
                    kb.tag(vector.tensor_copy(XOF[:], V[:, sl]), "dve")
                else:
                    kb.tag(vector.tensor_scalar(out=XOF[:], in0=V[:, sl],
                                                scalar1=wvec[:, dk:dk + 1], scalar2=bvec[:, dk:dk + 1],
                                                op0=ALU.mult, op1=ALU.add), "dve")

        @block.scalar
        def _(scalar):
            for dk in range(DT8):
                scalar.wait_ge(kb.sem("dve"), dve_a0 + dk + 1)
                if dk >= 1 and outb:
                    scalar.wait_ge(kb.sem("dob"), dob0 + 16 * dk)
                kb.tag(scalar.copy(out=XOB[:], in_=XOF[:]), "act")

        @block.sync
        def _(sync):
            of = hbm[outf].rearrange("(n p) t -> n p t", p=128) if outf else None
            ob = hbm[outb].rearrange("(n p) t -> n p t", p=128) if outb else None
            for dk in range(DT8):
                if of is not None:
                    sync.wait_ge(kb.sem("dve"), dve_a0 + dk + 1)
                    kb.tag(sync.dma_start(out=of[dk], in_=XOF[:]), "dof", 16)
                if ob is not None:
                    sync.wait_ge(kb.sem("act"), act_a0 + dk + 1)
                    kb.tag(sync.dma_start(out=ob[dk], in_=XOB[:]), "dob", 16)
            if of is not None:
                sync.wait_ge(kb.sem("dof"), kb.val("dof"))
            if ob is not None:
                sync.wait_ge(kb.sem("dob"), kb.val("dob"))

        kb.drain_body(block)



class Moe0Slots:
    MPE_PE = 25
    MACT_PE = 17
    MDVE_PE = 16

    def __init__(self, kb, hbm, q, es, ONESR, ONESRB):
        nc = kb.nc
        self.kb = kb
        self.q = q
        self.ONESR = ONESR
        self.ONESRB = ONESRB
        self.W1 = hbm["L0_ffn_W1"]
        self.W2 = hbm["L0_ffn_W2"]
        self.b2c_h = hbm["b2c"]
        self.cwt = hbm["cwt0"]
        self.b1c_h = hbm["b1c"]
        self.xn_h = hbm["xn0b"].rearrange("(n p) t -> n p t", p=128)
        self.xn_pn = hbm["xn0b"].rearrange("(n p) t -> p n t", p=128)
        self.g_h = hbm["g0f"].rearrange("(n p) t -> n p t", p=128)
        self.XNQ = es.enter_context(nc.sbuf_tensor(f"o{q}_xnq", [128, DT8 * 512], BF16))
        self.W1B = es.enter_context(nc.sbuf_tensor(f"o{q}_w1b", [128, 2 * 4096], BF16))
        self.W2B = es.enter_context(nc.sbuf_tensor(f"o{q}_w2b", [128, 4 * 2048], BF16))
        self.HTC = es.enter_context(nc.sbuf_tensor(f"o{q}_htc", [128, 16 * 512], BF16))
        self.Y1 = es.enter_context(nc.sbuf_tensor(f"o{q}_y1", [128, 512], F32))
        self.CWB = es.enter_context(nc.sbuf_tensor(f"o{q}_cwb", [128, 512], F32))
        self.CWSTG = es.enter_context(nc.sbuf_tensor(f"o{q}_cwstg", [1, 2 * 512], BF16))
        self.B1C = es.enter_context(nc.sbuf_tensor(f"o{q}_b1c", [128, 256], BF16))
        self.ACC = es.enter_context(nc.sbuf_tensor(f"o{q}_acc", [128, DT8 * 512], F32))
        self.B2CQ = es.enter_context(nc.sbuf_tensor(f"o{q}_b2c", [128, 128], F32))
        self.PH = es.enter_context(nc.psum_tensor(f"o{q}_ph", [128, 2 * 512], F32))
        self.PO = es.enter_context(nc.psum_tensor(f"o{q}_po", [128, 2 * 512], F32))
        self.PCW = es.enter_context(nc.psum_tensor(f"o{q}_pcw", [128, 512], F32))
        self.mpe0 = kb.val("mpe")
        self.mact0 = kb.val("mact")
        self.mdve0 = kb.val("mdve")
        self.mw0_0 = kb.val("mw0")
        self.mw1_0 = kb.val("mw1")
        self.mxq0 = kb.val("mxq")
        self.mcw0 = kb.val("mcw")
        self.mgo0 = kb.val("mgo")
        self.gp_init = None
        self.ld_done = None

    def mpe_e(self, e):
        return self.mpe0 + e * self.MPE_PE

    def mact_e(self, e):
        return self.mact0 + e * self.MACT_PE

    def mdve_e(self, e):
        return self.mdve0 + e * self.MDVE_PE

    def preamble_sync(self, sy):
        kb = self.kb
        q = self.q
        sy.wait_ge(kb.sem("mpe"), self.mpe0)
        sy.wait_ge(kb.sem("mdve"), self.mdve0)
        kb.tag(sy.dma_start(out=self.XNQ[:].rearrange("p (dk c) -> p dk c", c=512),
                            in_=self.xn_pn[:, :, q * 512:(q + 1) * 512]), "mxq", 16)
        kb.tag(sy.dma_start(out=self.B2CQ[:], in_=self.b2c_h), "mxq", 16)

    def preamble_gp(self, g):
        kb = self.kb
        g.wait_ge(kb.sem("mact"), self.mact0)
        g.wait_ge(kb.sem("mpe"), self.mpe0)
        g.wait_ge(kb.sem("mdve"), self.mdve0)
        kb.tag(g.dma_start(out=self.B1C[:], in_=self.b1c_h), "mcw", 16)
        self.ld_done = kb.val("mxq")
        self.gp_init = kb.tag(g.memset(self.ACC[:], 0.0), "gp", 1)
        self.cw_fill(g, 0)
        self.w1_fill(g, 0)
        self.w1_fill(g, 1)

    def cw_fill(self, eng, e):
        kb = self.kb
        q = self.q
        if e >= 2:
            eng.wait_ge(kb.sem("mpe"), self.mpe_e(e - 2) + 1)
        kb.tag(eng.dma_start(out=self.CWSTG[0:1, (e % 2) * 512:(e % 2) * 512 + 512],
                             in_=self.cwt[e:e + 1, q * 512:(q + 1) * 512]), "mcw", 16)

    def w1_fill(self, g, f):
        kb = self.kb
        e, qr = divmod(f, 4)
        if f >= 2:
            ep, qp = divmod(f - 2, 4)
            g.wait_ge(kb.sem("mpe"), self.mpe_e(ep) + 4 * qp + 5)
        kb.tag(g.dma_start(
            out=self.W1B[:, (f % 2) * 4096:(f % 2) * 4096 + 4096].rearrange(
                "p (dk c) -> p dk c", c=512),
            in_=self.W1[e].rearrange("(dk p) h -> p dk h", p=128)[:, :, qr * 512:(qr + 1) * 512]),
            "mw0", 16)

    def w2_fill(self, g, f2):
        kb = self.kb
        e, dm = divmod(f2, 8)
        if f2 >= 4:
            ep, dp = divmod(f2 - 4, 8)
            g.wait_ge(kb.sem("mpe"), self.mpe_e(ep) + 18 + dp)
        kb.tag(g.dma_start(
            out=self.W2B[:, (f2 % 4) * 2048:(f2 % 4) * 2048 + 2048].rearrange(
                "p (ht c) -> p ht c", c=128),
            in_=self.W2[e].rearrange("(ht p) d -> p ht d", p=128)[:, :, dm * 128:(dm + 1) * 128]),
            "mw1", 16)

    def pe(self, tensor, s):
        kb = self.kb
        e, sub = divmod(s, 4)
        if sub == 0:
            tensor.wait_ge(kb.sem("mxq"), self.ld_done)
            tensor.wait_ge(kb.sem("mcw"), self.mcw0 + 16 * (e + 2))
            if e >= 1:
                tensor.wait_ge(kb.sem("mact"), self.mact_e(e - 1) + 1)
            else:
                tensor.wait_ge(kb.sem("mdve"), self.mdve0)
                tensor.wait_ge(kb.sem("mact"), self.mact0)
            kb.tag(tensor.matmul(self.PCW[:, 0:512], self.ONESRB[:],
                                 self.CWSTG[0:1, (e % 2) * 512:(e % 2) * 512 + 512],
                                 start=True, stop=True), "mpe", 1)
        if sub in (0, 1):
            for hh in range(8):
                ht = sub * 8 + hh
                f = 4 * e + ht // 4
                if ht % 4 == 0:
                    tensor.wait_ge(kb.sem("mw0"), self.mw0_0 + 16 * (max(f, 1) + 1))
                if ht >= 2:
                    tensor.wait_ge(kb.sem("mact"), self.mact_e(e) + 2 + (ht - 2))
                elif e >= 1:
                    tensor.wait_ge(kb.sem("mact"), self.mact_e(e - 1) + 16 + ht)
                for dk in range(DT8):
                    mm = tensor.matmul(self.PH[:, (ht % 2) * 512:(ht % 2) * 512 + 512],
                                       self.W1B[:, (f % 2) * 4096 + dk * 512 + (ht % 4) * 128:
                                                   (f % 2) * 4096 + dk * 512 + (ht % 4) * 128 + 128],
                                       self.XNQ[:, dk * 512:(dk + 1) * 512],
                                       start=(dk == 0), stop=(dk == DT8 - 1))
                kb.tag(mm, "mpe", 1)
        else:
            if sub == 2:
                tensor.wait_ge(kb.sem("mact"), self.mact_e(e) + 17)
            for dd in range(4):
                dm = (sub - 2) * 4 + dd
                f2 = 8 * e + dm
                tensor.wait_ge(kb.sem("mw1"), self.mw1_0 + 16 * ((f2 | 1 if f2 < 4 else f2) + 1))
                if dm >= 2:
                    tensor.wait_ge(kb.sem("mdve"), self.mdve_e(e) + 2 * (dm - 2) + 1)
                elif e >= 1:
                    tensor.wait_ge(kb.sem("mdve"), self.mdve_e(e - 1) + 2 * (dm + 6) + 1)
                for ht in range(16):
                    mm = tensor.matmul(self.PO[:, (dm % 2) * 512:(dm % 2) * 512 + 512],
                                       self.W2B[:, (f2 % 4) * 2048 + ht * 128:
                                                   (f2 % 4) * 2048 + ht * 128 + 128],
                                       self.HTC[:, ht * 512:(ht + 1) * 512],
                                       start=(ht == 0), stop=(ht == 15))
                kb.tag(mm, "mpe", 1)

    def act(self, scalar, s):
        kb = self.kb
        e, sub = divmod(s, 4)
        if sub == 0:
            scalar.wait_ge(kb.sem("mpe"), self.mpe_e(e) + 1)
            if e >= 1:
                scalar.wait_ge(kb.sem("mdve"), self.mdve_e(e - 1) + 15)
            else:
                scalar.wait_ge(kb.sem("mdve"), self.mdve0)
            kb.tag(scalar.copy(out=self.CWB[:], in_=self.PCW[:, 0:512]), "mact", 1)
        if sub in (0, 1):
            for hh in range(8):
                ht = sub * 8 + hh
                scalar.wait_ge(kb.sem("mpe"), self.mpe_e(e) + 2 + ht)
                kb.tag(scalar.activation(out=self.HTC[:, ht * 512:(ht + 1) * 512],
                                         in_=self.PH[:, (ht % 2) * 512:(ht % 2) * 512 + 512],
                                         func=AF.Relu,
                                         bias=self.B1C[:, ht * 16 + e: ht * 16 + e + 1]), "mact")

    def dve(self, v2, s):
        kb = self.kb
        e, sub = divmod(s, 4)
        if sub in (2, 3):
            if sub == 2:
                v2.wait_ge(kb.sem("mact"), self.mact_e(e) + 1)
                if e == 0:
                    v2.wait_ge(kb.sem("gp"), self.gp_init)
            for dd in range(4):
                dm = (sub - 2) * 4 + dd
                v2.wait_ge(kb.sem("mpe"), self.mpe_e(e) + 18 + dm)
                kb.tag(v2.scalar_tensor_tensor(out=self.Y1[:],
                                               in0=self.PO[:, (dm % 2) * 512:(dm % 2) * 512 + 512],
                                               scalar=self.B2CQ[:, dm * 16 + e: dm * 16 + e + 1],
                                               in1=self.CWB[:],
                                               op0=ALU.add, op1=ALU.mult), "mdve")
                kb.tag(v2.tensor_add(self.ACC[:, dm * 512:(dm + 1) * 512],
                                     self.ACC[:, dm * 512:(dm + 1) * 512], self.Y1[:]), "mdve")

    def tail_act(self, scalar):
        kb = self.kb
        for dm in range(DT8):
            if dm == 0:
                scalar.wait_ge(kb.sem("mdve"), self.mdve_e(16))
            else:
                scalar.wait_ge(kb.sem("mgo"), self.mgo0 + 16 * dm)
            kb.tag(scalar.activation(out=self.Y1[:],
                                     in_=self.ACC[:, dm * 512:(dm + 1) * 512],
                                     func=AF.Silu), "mact")

    def tail_sync(self, sy):
        kb = self.kb
        for dm in range(DT8):
            sy.wait_ge(kb.sem("mact"), self.mact_e(16) + dm + 1)
            kb.tag(sy.dma_start(out=self.g_h[dm][:, self.q * 512:(self.q + 1) * 512],
                                in_=self.Y1[:]), "mgo", 16)


def sec_mamba(kb, hbm, l, xin_f, xin_b, xout_f, xout_b):
    nc = kb.nc
    S = 2051

    iw = hbm["L0_attn_in_proj_w"][l]
    cw = hbm["L0_attn_conv_w"][l]
    cb = hbm["L0_attn_conv_b"][l]
    xpw = hbm["L0_attn_x_proj_w"][l]
    dtw = hbm["L0_attn_dt_proj_w"][l]
    dtb = hbm["L0_attn_dt_proj_b"][l]
    alog = hbm["L0_attn_A_log"][l]
    dsk = hbm["L0_attn_D_skip"][l]
    ow = hbm["L0_attn_out_proj_w"][l]
    lnw = hbm["L0_attn_ln_w"][l]
    lnb = hbm["L0_attn_ln_b"][l]
    zb_h = hbm["zbuf"].rearrange("(n p) t -> n p t", p=128)
    xc_h = hbm["xcf"].rearrange("(n p) t -> n p t", p=128)
    yg_h = hbm["ygb"].rearrange("(n p) t -> n p t", p=128)
    yt_h = hbm["ytmp"].rearrange("(n p) t -> n p t", p=128)

    esS = ExitStack()
    EYEB = esS.enter_context(nc.sbuf_tensor(f"m{l}_eyeb", [128, 128], BF16))
    ONESC = esS.enter_context(nc.sbuf_tensor(f"m{l}_onesc", [128, 1], F32))
    ONESR = esS.enter_context(nc.sbuf_tensor(f"m{l}_onesr", [1, 128], F32))
    CWC = esS.enter_context(nc.sbuf_tensor(f"m{l}_cwc", [128, 16 * 4], F32))
    CBC = esS.enter_context(nc.sbuf_tensor(f"m{l}_cbc", [128, 16], F32))
    DTBC = esS.enter_context(nc.sbuf_tensor(f"m{l}_dtbc", [128, 16], F32))
    DSKC = esS.enter_context(nc.sbuf_tensor(f"m{l}_dskc", [128, 16], F32))
    LNWC = esS.enter_context(nc.sbuf_tensor(f"m{l}_lnwc", [128, 8], F32))
    LNBC = esS.enter_context(nc.sbuf_tensor(f"m{l}_lnbc", [128, 8], F32))
    ACOL = esS.enter_context(nc.sbuf_tensor(f"m{l}_acol", [128, 16 * 16], F32))
    ONESRB = esS.enter_context(nc.sbuf_tensor(f"m{l}_onesrb", [1, 128], BF16))
    esCD = ExitStack()
    SEL32 = esCD.enter_context(nc.sbuf_tensor(f"m{l}_sel32", [32, 32 * 128], BF16))
    DTWT = esCD.enter_context(nc.sbuf_tensor(f"m{l}_dtwt", [64, 16 * 128], BF16))
    DTRB = esCD.enter_context(nc.sbuf_tensor(f"m{l}_dtrb", [64, T], BF16))
    BCB = esCD.enter_context(nc.sbuf_tensor(f"m{l}_bcb", [32, T], BF16))

    with nc.Block() as block:
        @block.sync
        def _(sync):
            kb.tag(sync.dma_start(out=EYEB[:], in_=hbm["c_eye_b"]), "ds", 16)
            kb.tag(sync.dma_start(out=ONESC[:], in_=hbm["c_onescol_f"]), "ds", 16)
            kb.tag(sync.dma_start(out=ONESR[:], in_=hbm["c_onesrow_f"]), "ds", 16)
            kb.tag(sync.dma_start(out=ONESRB[:], in_=hbm["c_onesrow_b"]), "ds", 16)
            kb.tag(sync.dma_start(out=SEL32[:], in_=hbm["c_sel32_b"]), "ds", 16)
            kb.tag(sync.dma_start(out=CWC[:].rearrange("p (a k) -> p a k", k=4), in_=cw.rearrange("(a p) k -> p a k", p=128)), "ds", 16)
            for a in range(16):
                kb.tag(sync.dma_start(out=CBC[:, a:a + 1],
                                      in_=cb[a * 128:(a + 1) * 128].rearrange("(p o) -> p o", o=1)), "ds", 16)
                kb.tag(sync.dma_start(out=DTBC[:, a:a + 1],
                                      in_=dtb[a * 128:(a + 1) * 128].rearrange("(p o) -> p o", o=1)), "ds", 16)
                kb.tag(sync.dma_start(out=DSKC[:, a:a + 1],
                                      in_=dsk[a * 128:(a + 1) * 128].rearrange("(p o) -> p o", o=1)), "ds", 16)
            for a in range(8):
                kb.tag(sync.dma_start(out=LNWC[:, a:a + 1],
                                      in_=lnw[a * 128:(a + 1) * 128].rearrange("(p o) -> p o", o=1)), "ds", 16)
                kb.tag(sync.dma_start(out=LNBC[:, a:a + 1],
                                      in_=lnb[a * 128:(a + 1) * 128].rearrange("(p o) -> p o", o=1)), "ds", 16)
            kb.tag(sync.dma_start(out=ACOL[:].rearrange("p (a n) -> p a n", n=NST), in_=alog.rearrange("(a p) n -> p a n", p=128)), "ds", 16)
        small_ld = kb.val("ds")

        @block.scalar
        def _(scalar):
            scalar.wait_ge(kb.sem("ds"), small_ld)
            kb.tag(scalar.activation(out=ACOL[:], in_=ACOL[:], func=AF.Exp), "act", 1)
        act_acol = kb.val("act")

        @block.vector
        def _(vector):
            vector.wait_ge(kb.sem("act"), act_acol)
            kb.tag(vector.tensor_scalar(out=ACOL[:], in0=ACOL[:], scalar1=-1.0, scalar2=None,
                                        op0=ALU.mult), "dve", 1)

    esA = ExitStack()
    XT = esA.enter_context(nc.sbuf_tensor(f"m{l}_xt", [128, DT8 * T], BF16))
    XS = esA.enter_context(nc.sbuf_tensor(f"m{l}_xs", [128, 16 * S], BF16))
    WST = esA.enter_context(nc.sbuf_tensor(f"m{l}_wst", [128, 2 * 1024], BF16))
    WT = esA.enter_context(nc.sbuf_tensor(f"m{l}_wt", [128, 2 * 1024], BF16))
    ZR = esA.enter_context(nc.sbuf_tensor(f"m{l}_zr", [128, 2 * T], BF16))
    CACC = esA.enter_context(nc.sbuf_tensor(f"m{l}_cacc", [128, 2 * T], F32))
    XCF = esA.enter_context(nc.sbuf_tensor(f"m{l}_xcf", [128, 2 * T], F32))
    DBLT = esA.enter_context(nc.sbuf_tensor(f"m{l}_dblt", [128, T], F32))
    XPWT = esA.enter_context(nc.sbuf_tensor(f"m{l}_xpwt", [128, 16 * 96], BF16))
    PTB = esA.enter_context(nc.psum_tensor(f"m{l}_ptb", [128, 4 * 1024], BF16))
    PZ = esA.enter_context(nc.psum_tensor(f"m{l}_pz", [128, 2 * 512], F32))

    dg0, pe0, act0, dve0, ds1 = (kb.val(x) for x in ("dg", "pe", "act", "dve", "ds"))
    dz_base = [kb.val("dz0"), kb.val("dz1")]
    dxc_base = [kb.val("dxc0"), kb.val("dxc1")]
    dxt0 = kb.val("dxt")
    dw_base = [kb.val("dw0"), kb.val("dw1")]
    gp0 = kb.val("gp")
    with nc.Block() as block:
        @block.gpsimd
        def _(g):
            kb.tag(g.dma_start(out=XT[:].rearrange("p (n t) -> p n t", t=T),
                               in_=hbm[xin_b].rearrange("(n p) t -> p n t", p=128)), "dxt", 16)
            for a in range(15):
                g.memset(XS[:, a * S:a * S + 3], 0.0)
            kb.tag(g.memset(XS[:, 15 * S:15 * S + 3], 0.0), "gp", 1)
            for et in range(32):
                if et >= 2:
                    g.wait_ge(kb.sem("pe"), pe0 + (et - 2) * 12 + 8)
                kb.tag(g.dma_start(out=WST[:, (et % 2) * 1024:(et % 2) * 1024 + 1024],
                                   in_=iw[et * 128:(et + 1) * 128, :]), "dw0" if et % 2 == 0 else "dw1", 16)
            for a in range(16):
                g.wait_ge(kb.sem("act"), act0 + 384 + 2 * (a + 1))
                kb.tag(g.dma_start(out=xc_h[a], in_=XCF[:, (a % 2) * T:(a % 2) * T + T]),
                       "dxc0" if a % 2 == 0 else "dxc1", 16)

        @block.tensor
        def _(tensor):
            tensor.wait_ge(kb.sem("dxt"), dxt0 + 16)
            for et in range(32):
                tensor.wait_ge(kb.sem("dw0" if et % 2 == 0 else "dw1"),
                               dw_base[et % 2] + 16 * (et // 2 + 1))
                for dk in range(DT8):
                    j = et * 8 + dk
                    if j >= 4:
                        et4, dk4 = divmod(j - 4, DT8)
                        tensor.wait_ge(kb.sem("act"), act0 + et4 * 12 + dk4 + 1)
                    kb.tag(tensor.transpose(PTB[:, (j % 4) * 1024:(j % 4) * 1024 + 128],
                                            WST[:, (et % 2) * 1024 + dk * 128:(et % 2) * 1024 + dk * 128 + 128],
                                            EYEB[:]), "pe", 1)
                tensor.wait_ge(kb.sem("act"), act0 + et * 12 + 8)
                for tc in range(TC4):
                    mmi = et * 4 + tc
                    if mmi >= 2:
                        et2, tc2 = divmod(mmi - 2, TC4)
                        tensor.wait_ge(kb.sem("act"), act0 + et2 * 12 + 8 + tc2 + 1)
                    for dk in range(DT8):
                        mm = tensor.matmul(PZ[:, (mmi % 2) * 512:(mmi % 2) * 512 + 512],
                                           WT[:, (et % 2) * 1024 + dk * 128:(et % 2) * 1024 + dk * 128 + 128],
                                           XT[:, dk * T + tc * 512: dk * T + tc * 512 + 512],
                                           start=(dk == 0), stop=(dk == DT8 - 1))
                    kb.tag(mm, "pe", 1)

        @block.scalar
        def _(scalar):
            for et in range(32):
                for dk in range(DT8):
                    scalar.wait_ge(kb.sem("pe"), pe0 + et * 12 + dk + 1)
                    j = et * 8 + dk
                    kb.tag(scalar.copy(out=WT[:, (et % 2) * 1024 + dk * 128:(et % 2) * 1024 + dk * 128 + 128],
                                       in_=PTB[:, (j % 4) * 1024:(j % 4) * 1024 + 128]), "act", 1)
                for tc in range(TC4):
                    mmi = et * 4 + tc
                    scalar.wait_ge(kb.sem("pe"), pe0 + et * 12 + 8 + tc + 1)
                    if et >= 18 and tc == 0:
                        zi = et - 16
                        scalar.wait_ge(kb.sem("dz0" if zi % 2 == 0 else "dz1"),
                                       dz_base[zi % 2] + 16 * ((zi - 2) // 2 + 1))
                    if et < 16:
                        dst = XS[:, et * S + 3 + tc * 512: et * S + 3 + tc * 512 + 512]
                    else:
                        dst = ZR[:, (et % 2) * T + tc * 512:(et % 2) * T + tc * 512 + 512]
                    kb.tag(scalar.copy(dst, PZ[:, (mmi % 2) * 512:(mmi % 2) * 512 + 512]), "act", 1)

        @block.vector
        def _(vector):
            vector.wait_ge(kb.sem("gp"), gp0 + 1)
            for a in range(16):
                base = a * S
                sl_out = CACC[:, (a % 2) * T:(a % 2) * T + T]
                vector.wait_ge(kb.sem("act"), act0 + a * 12 + 12)
                if a >= 2:
                    vector.wait_ge(kb.sem("act"), act0 + 384 + 2 * (a - 1))
                tprev = kb.tag(vector.tensor_scalar(out=sl_out, in0=XS[:, base + 0: base + 0 + T],
                                                    scalar1=CWC[:, a * 4 + 0:a * 4 + 1],
                                                    scalar2=CBC[:, a:a + 1],
                                                    op0=ALU.mult, op1=ALU.add), "dve")
                for k in range(1, 4):
                    vector.wait_ge(kb.sem("dve"), tprev)
                    tprev = kb.tag(vector.scalar_tensor_tensor(out=sl_out, in0=XS[:, base + k: base + k + T],
                                                               scalar=CWC[:, a * 4 + k:a * 4 + k + 1],
                                                               in1=sl_out, op0=ALU.mult, op1=ALU.add), "dve")

        @block.scalar
        def _(scalar):
            for a in range(16):
                scalar.wait_ge(kb.sem("dve"), dve0 + 4 * (a + 1))
                if a >= 2:
                    scalar.wait_ge(kb.sem("dxc0" if a % 2 == 0 else "dxc1"),
                                   dxc_base[a % 2] + 16 * ((a - 2) // 2 + 1))
                kb.tag(scalar.activation(out=XS[:, a * S + 3: a * S + 3 + T],
                                         in_=CACC[:, (a % 2) * T:(a % 2) * T + T], func=AF.Silu), "act", 1)
                kb.tag(scalar.activation(out=XCF[:, (a % 2) * T:(a % 2) * T + T],
                                         in_=CACC[:, (a % 2) * T:(a % 2) * T + T], func=AF.Silu), "act", 1)

        @block.sync
        def _(sync):
            for zi in range(16):
                et = 16 + zi
                sync.wait_ge(kb.sem("act"), act0 + et * 12 + 12)
                kb.tag(sync.dma_start(out=zb_h[zi], in_=ZR[:, (et % 2) * T:(et % 2) * T + T]),
                       "dz0" if zi % 2 == 0 else "dz1", 16)

    dg2, pe2, act2, dve2, ds3 = (kb.val(x) for x in ("dg", "pe", "act", "dve", "ds"))
    with nc.Block() as block:
        @block.gpsimd
        def _(g):
            kb.tag(g.dma_start(out=WST[0:96, 0:DI], in_=xpw), "dg", 16)
            kb.tag(g.dma_start(out=WT[:, 0:16 * 64].rearrange("p (a r) -> p a r", r=DTR), in_=dtw.rearrange("(a p) r -> p a r", p=128)), "dg", 16)

        @block.tensor
        def _(tensor):
            tensor.wait_ge(kb.sem("dg"), dg2 + 32)
            for j in range(16):
                if j >= 4:
                    tensor.wait_ge(kb.sem("act"), act2 + (j - 3))
                kb.tag(tensor.transpose(PTB[:, (j % 4) * 1024:(j % 4) * 1024 + 96],
                                        WST[0:96, j * 128:(j + 1) * 128], EYEB[0:96, 0:96]), "pe", 1)
            for j in range(16):
                jj = 16 + j
                tensor.wait_ge(kb.sem("act"), act2 + jj - 3)
                kb.tag(tensor.transpose(PTB[0:64, (jj % 4) * 1024:(jj % 4) * 1024 + 128],
                                        WT[:, j * 64:(j + 1) * 64], EYEB[:]), "pe", 1)
            tensor.wait_ge(kb.sem("act"), act2 + 32)
            for tc in range(TC4):
                if tc >= 2:
                    tensor.wait_ge(kb.sem("act"), act2 + 32 + tc - 1)
                for ht in range(16):
                    mm = tensor.matmul(PZ[0:96, (tc % 2) * 512:(tc % 2) * 512 + 512],
                                       XPWT[:, ht * 96:(ht + 1) * 96],
                                       XS[:, ht * S + 3 + tc * 512: ht * S + 3 + tc * 512 + 512],
                                       start=(ht == 0), stop=(ht == 15))
                kb.tag(mm, "pe", 1)

        @block.scalar
        def _(scalar):
            for j in range(16):
                scalar.wait_ge(kb.sem("pe"), pe2 + j + 1)
                kb.tag(scalar.copy(out=XPWT[:, j * 96:(j + 1) * 96],
                                   in_=PTB[:, (j % 4) * 1024:(j % 4) * 1024 + 96]), "act", 1)
            for j in range(16):
                jj = 16 + j
                scalar.wait_ge(kb.sem("pe"), pe2 + jj + 1)
                kb.tag(scalar.copy(out=DTWT[:, j * 128:(j + 1) * 128],
                                   in_=PTB[0:64, (jj % 4) * 1024:(jj % 4) * 1024 + 128]), "act", 1)
            for tc in range(TC4):
                scalar.wait_ge(kb.sem("pe"), pe2 + 32 + tc + 1)
                kb.tag(scalar.copy(out=DBLT[0:96, tc * 512:(tc + 1) * 512],
                                   in_=PZ[0:96, (tc % 2) * 512:(tc % 2) * 512 + 512]), "act", 1)

        @block.vector
        def _(vector):
            vector.wait_ge(kb.sem("act"), act2 + 36)
            kb.tag(vector.tensor_copy(DTRB[:], DBLT[0:64, :]), "dve", 1)
            kb.tag(vector.tensor_copy(BCB[:], DBLT[64:96, :]), "dve", 1)

        kb.drain_body(block)

    esA.close()
    esD = ExitStack()
    BBC = esD.enter_context(nc.sbuf_tensor(f"m{l}_bbc", [128, 8 * T], BF16))
    DELTA = esD.enter_context(nc.sbuf_tensor(f"m{l}_delta", [128, T], BF16))
    SPB = esD.enter_context(nc.sbuf_tensor(f"m{l}_spb", [128, 512], F32))
    DU = esD.enter_context(nc.sbuf_tensor(f"m{l}_du", [128, T], BF16))
    DAB = esD.enter_context(nc.sbuf_tensor(f"m{l}_dab", [128, 2 * T], F32))
    BSC = esD.enter_context(nc.sbuf_tensor(f"m{l}_bsc", [128, T], BF16))
    HH = esD.enter_context(nc.sbuf_tensor(f"m{l}_hh", [128, T], BF16))
    PP = esD.enter_context(nc.sbuf_tensor(f"m{l}_pp", [128, T], BF16))
    YY = esD.enter_context(nc.sbuf_tensor(f"m{l}_yy", [128, 2 * T], BF16))
    UB = esD.enter_context(nc.sbuf_tensor(f"m{l}_ub", [128, 2 * T], BF16))
    ZBL = esD.enter_context(nc.sbuf_tensor(f"m{l}_zbl", [128, T], BF16))
    YGS = esD.enter_context(nc.sbuf_tensor(f"m{l}_ygs", [128, T], BF16))
    PBC = esD.enter_context(nc.psum_tensor(f"m{l}_pbc", [128, 2 * 512], F32))
    PDL = esD.enter_context(nc.psum_tensor(f"m{l}_pdl", [128, 512], F32))

    esM = ExitStack()
    mo = Moe0Slots(kb, hbm, l, esM, ONESR, ONESRB)

    dyt_pass = {}
    dyr_pass = {}
    with nc.Block() as block:
        @block.sync
        def _(sy):
            mo.preamble_sync(sy)

        @block.gpsimd
        def _(g):
            mo.preamble_gp(g)

        for pss in range(4):
            pe3, act3, dve3 = kb.val("pe"), kb.val("act"), kb.val("dve")
            dyt_pass[pss] = kb.val("dyt")
            dyr_pass[pss] = kb.val("dyr")

            @block.tensor
            def _(tensor):
                for j in range(32):
                    ni, r = divmod(j, 8)
                    bc, tcq = divmod(r, 4)
                    row = (16 if bc else 0) + pss * 4 + ni
                    if pss == 0:
                        tensor.wait_ge(kb.sem("dve"), dve3)
                    else:
                        tensor.wait_ge(kb.sem("dve"), dve3 - 17 + 4 * ni + 4)
                    if j >= 2:
                        tensor.wait_ge(kb.sem("act"), act3 + j - 1)
                    kb.tag(tensor.matmul(PBC[:, (j % 2) * 512:(j % 2) * 512 + 512],
                                         SEL32[:, row * 128:(row + 1) * 128],
                                         BCB[:, tcq * 512:(tcq + 1) * 512],
                                         start=True, stop=True), "pe", 1)

            @block.scalar
            def _(scalar):
                for j in range(32):
                    ni, r = divmod(j, 8)
                    bc, tcq = divmod(r, 4)
                    scalar.wait_ge(kb.sem("pe"), pe3 + j + 1)
                    kb.tag(scalar.copy(out=BBC[:, (2 * ni + bc) * T + tcq * 512:(2 * ni + bc) * T + tcq * 512 + 512],
                                       in_=PBC[:, (j % 2) * 512:(j % 2) * 512 + 512]), "act", 1)
            pe_bc = pe3 + 32
            act_bc = act3 + 32

            SP_OFF = 1 if pss == 0 else 0
            ACT_PD = 8 + 4 + (1 if pss in (0, 3) else 0)
            DVE_PD = 17 + (1 if pss == 3 else 0)

            def pe_delta_tag(dblk, tc):
                return pe_bc + dblk * 4 + tc + 1

            def act_base(dblk):
                return act_bc + dblk * ACT_PD

            def act_exp_tag(dblk, tc):
                return act_base(dblk) + SP_OFF + 2 * tc + 1

            def act_sp_tag(dblk, tc):
                return act_base(dblk) + SP_OFF + 2 * tc + 2

            def act_da_tag(dblk, ni):
                return act_base(dblk) + SP_OFF + 8 + ni + 1

            def dve_base(dblk):
                return dve3 + dblk * DVE_PD

            def dve_scan_tag(dblk, ni):
                return dve_base(dblk) + 4 * ni + 3

            du0 = kb.val("du_")
            dzr0 = kb.val("dzr")
            dyg0 = kb.val("dyg")

            @block.tensor
            def _(tensor2):
                for dblk in range(16):
                    s = pss * 16 + dblk
                    for tc in range(TC4):
                        gi = dblk * 4 + tc
                        if gi >= 1:
                            tensor2.wait_ge(kb.sem("act"), act_exp_tag((gi - 1) // 4, (gi - 1) % 4))
                        else:
                            tensor2.wait_ge(kb.sem("act"), act3)
                        mm = tensor2.matmul(PDL[:, 0:512],
                                            DTWT[:, dblk * 128:(dblk + 1) * 128],
                                            DTRB[:, tc * 512:(tc + 1) * 512],
                                            start=True, stop=True)
                        kb.tag(mm, "pe", 1)
                    if dblk >= 1:
                        mo.pe(tensor2, s - 1)
                mo.pe(tensor2, pss * 16 + 15)

            @block.scalar
            def _(scalar2):
                for dblk in range(16):
                    s = pss * 16 + dblk
                    ab = act_base(dblk)
                    if pss == 0:
                        scalar2.wait_ge(kb.sem("du_"), du0 + 16 * (max(dblk, 1) + 1))
                        if dblk >= 2:
                            scalar2.wait_ge(kb.sem("dyt"), dyt_pass[0] + 16 * (dblk - 1))
                        kb.tag(scalar2.activation(out=YY[:, (dblk % 2) * T:(dblk % 2) * T + T],
                                                  in_=UB[:, (dblk % 2) * T:(dblk % 2) * T + T],
                                                  func=AF.Copy,
                                                  scale=DSKC[:, dblk:dblk + 1]), "act")
                    if dblk >= 1:
                        scalar2.wait_ge(kb.sem("dve"), dve_base(dblk - 1) + (18 if pss == 3 else 1))
                    else:
                        scalar2.wait_ge(kb.sem("dve"), dve3)
                    for tc in range(TC4):
                        scalar2.wait_ge(kb.sem("pe"), pe_delta_tag(dblk, tc))
                        kb.tag(scalar2.activation(out=SPB[:],
                                                  in_=PDL[:, 0:512],
                                                  func=AF.Exp, bias=DTBC[:, dblk:dblk + 1]), "act")
                        kb.tag(scalar2.activation(out=DELTA[:, tc * 512:(tc + 1) * 512],
                                                  in_=SPB[:], func=AF.Ln, bias=1.0), "act")
                    for ni in range(4):
                        n = pss * 4 + ni
                        if ni >= 2:
                            scalar2.wait_ge(kb.sem("dve"), dve_scan_tag(dblk, ni - 2))
                        elif dblk >= 1:
                            scalar2.wait_ge(kb.sem("dve"), dve_scan_tag(dblk - 1, 2 + ni))
                        else:
                            scalar2.wait_ge(kb.sem("dve"), dve3)
                        kb.tag(scalar2.activation(out=DAB[:, (ni % 2) * T:(ni % 2) * T + T],
                                                  in_=DELTA[:],
                                                  func=AF.Exp,
                                                  scale=ACOL[:, dblk * 16 + n: dblk * 16 + n + 1]), "act")
                    if pss == 3:
                        scalar2.wait_ge(kb.sem("dzr"), dzr0 + 16 * (dblk + 1))
                        scalar2.wait_ge(kb.sem("dve"), dve_base(dblk) + 1)
                        kb.tag(scalar2.activation(out=DELTA[:], in_=ZBL[:],
                                                  func=AF.Silu), "act")
                    if dblk >= 1:
                        mo.act(scalar2, s - 1)
                mo.act(scalar2, pss * 16 + 15)
                if pss == 3:
                    mo.tail_act(scalar2)

            @block.vector
            def _(v2):
                for dblk in range(16):
                    s = pss * 16 + dblk
                    db = dve_base(dblk)
                    if dblk >= 2:
                        mo.dve(v2, s - 2)
                    v2.wait_ge(kb.sem("act"), act_sp_tag(dblk, 3))
                    v2.wait_ge(kb.sem("du_"), du0 + 16 * (max(dblk, 1) + 1))
                    kb.tag(v2.tensor_mul(DU[:], DELTA[:], UB[:, (dblk % 2) * T:(dblk % 2) * T + T]), "dve")
                    for ni in range(4):
                        kb.tag(v2.tensor_tensor(out=BSC[:],
                                                in0=DU[:], in1=BBC[:, (2 * ni) * T:(2 * ni) * T + T],
                                                op=ALU.mult), "dve")
                        v2.wait_ge(kb.sem("act"), act_da_tag(dblk, ni))
                        kb.tag(v2.tensor_tensor_scan(out=HH[:], data0=DAB[:, (ni % 2) * T:(ni % 2) * T + T],
                                                     data1=BSC[:],
                                                     initial=0.0, op0=ALU.mult, op1=ALU.add), "dve")
                        kb.tag(v2.tensor_tensor(out=PP[:],
                                                in0=HH[:], in1=BBC[:, (2 * ni + 1) * T:(2 * ni + 1) * T + T],
                                                op=ALU.mult), "dve")
                        if ni == 0:
                            if pss == 0:
                                v2.wait_ge(kb.sem("act"), act_base(dblk) + 1)
                            else:
                                v2.wait_ge(kb.sem("dyr"), dyr_pass[pss] + 16 * (dblk + 1))
                        kb.tag(v2.tensor_tensor(out=YY[:, (dblk % 2) * T:(dblk % 2) * T + T],
                                                in0=YY[:, (dblk % 2) * T:(dblk % 2) * T + T],
                                                in1=PP[:], op=ALU.add), "dve")
                    if pss == 3:
                        v2.wait_ge(kb.sem("act"), act_base(dblk) + 13)
                        if dblk >= 1:
                            v2.wait_ge(kb.sem("dyg"), dyg0 + 16 * dblk)
                        kb.tag(v2.tensor_mul(YGS[:],
                                             YY[:, (dblk % 2) * T:(dblk % 2) * T + T],
                                             DELTA[:]), "dve")
                mo.dve(v2, pss * 16 + 14)
                mo.dve(v2, pss * 16 + 15)

            @block.gpsimd
            def _(g3):
                if pss == 0:
                    g3.wait_ge(kb.sem("dxc0"), kb.val("dxc0"))
                    g3.wait_ge(kb.sem("dxc1"), kb.val("dxc1"))
                if pss == 0:
                    g3.wait_ge(kb.sem("dve"), dve3)
                    g3.wait_ge(kb.sem("act"), act3)
                elif pss == 1:
                    g3.wait_ge(kb.sem("dve"), dve3 - 2 * 17 + 1)
                    g3.wait_ge(kb.sem("act"), act3 - 2 * 13 + 1)
                else:
                    g3.wait_ge(kb.sem("dve"), dve3 - 2 * 17 + 1)
                kb.tag(g3.dma_start(out=UB[:, 0:T], in_=xc_h[0]), "du_", 16)
                if pss == 1:
                    g3.wait_ge(kb.sem("act"), act3 - 13 + 1)
                if pss >= 1:
                    g3.wait_ge(kb.sem("dve"), dve3 - 17 + 1)
                kb.tag(g3.dma_start(out=UB[:, T:2 * T], in_=xc_h[1]), "du_", 16)
                for dblk in range(16):
                    s = pss * 16 + dblk
                    if dblk + 2 < 16:
                        g3.wait_ge(kb.sem("dve"), dve_base(dblk) + 1)
                        if pss == 0:
                            g3.wait_ge(kb.sem("act"), act_base(dblk) + 1)
                        kb.tag(g3.dma_start(out=UB[:, (dblk % 2) * T:(dblk % 2) * T + T],
                                            in_=xc_h[dblk + 2]), "du_", 16)
                    if s + 2 <= 63:
                        mo.w1_fill(g3, s + 2)
                    for f2 in (2 * s, 2 * s + 1):
                        if f2 <= 127:
                            mo.w2_fill(g3, f2)
                    if (s + 2) % 4 == 0 and (s + 2) // 4 <= 15:
                        mo.cw_fill(g3, (s + 2) // 4)

            @block.sync
            def _(sy):
                if pss >= 1:
                    sy.wait_ge(kb.sem("dyt"), dyt_pass[pss - 1] + 16 * 15)
                    kb.tag(sy.dma_start(out=YY[:, 0:T], in_=yt_h[0]), "dyr", 16)
                if pss == 3:
                    sy.wait_ge(kb.sem("dz0"), kb.val("dz0"))
                    sy.wait_ge(kb.sem("dz1"), kb.val("dz1"))
                    kb.tag(sy.dma_start(out=ZBL[:], in_=zb_h[0]), "dzr", 16)
                for dblk in range(16):
                    if pss >= 1 and dblk + 1 < 16:
                        d = dblk + 1
                        if d == 1:
                            sy.wait_ge(kb.sem("dyt"), dyt_pass[pss - 1] + 256)
                        elif pss < 3:
                            sy.wait_ge(kb.sem("dyt"), dyt_pass[pss] + 16 * (d - 1))
                        else:
                            sy.wait_ge(kb.sem("dyt"), dyt_pass[2] + 16 * (d + 1))
                            sy.wait_ge(kb.sem("dve"), dve_base(d - 2) + 18)
                        kb.tag(sy.dma_start(out=YY[:, (d % 2) * T:(d % 2) * T + T],
                                            in_=yt_h[d]), "dyr", 16)
                    if pss == 3 and dblk + 1 < 16:
                        sy.wait_ge(kb.sem("act"), act_base(dblk) + 13)
                        kb.tag(sy.dma_start(out=ZBL[:], in_=zb_h[dblk + 1]), "dzr", 16)
                    if pss < 3:
                        sy.wait_ge(kb.sem("dve"), dve_base(dblk) + 17)
                        kb.tag(sy.dma_start(out=yt_h[dblk], in_=YY[:, (dblk % 2) * T:(dblk % 2) * T + T]),
                               "dyt", 16)
                    else:
                        sy.wait_ge(kb.sem("dve"), dve_base(dblk) + 18)
                        kb.tag(sy.dma_start(out=yg_h[dblk], in_=YGS[:]), "dyg", 16)
                if pss == 3:
                    mo.tail_sync(sy)

            if pss == 3:
                kb.drain_body(block)

    esM.close()
    esD.close()
    esCD.close()


    esF = ExitStack()
    YG = esF.enter_context(nc.sbuf_tensor(f"m{l}_yg", [128, 16 * T], BF16))
    V = esF.enter_context(nc.sbuf_tensor(f"m{l}_v", [128, DT8 * T], F32))
    WSTF = esF.enter_context(nc.sbuf_tensor(f"m{l}_wstf", [128, 2 * DI], BF16))
    OWT = esF.enter_context(nc.sbuf_tensor(f"m{l}_owt", [128, 2 * DI], BF16))
    XCUR = esF.enter_context(nc.sbuf_tensor(f"m{l}_xcur", [128, T], F32))
    SQ = esF.enter_context(nc.sbuf_tensor(f"m{l}_sq", [128, 2 * 512], F32))
    ROWS = esF.enter_context(nc.sbuf_tensor(f"m{l}_rows", [1, 4 * T], F32))
    XOF = esF.enter_context(nc.sbuf_tensor(f"m{l}_xof", [128, T], F32))
    XOB = esF.enter_context(nc.sbuf_tensor(f"m{l}_xob", [128, T], BF16))
    PTB2 = esF.enter_context(nc.psum_tensor(f"m{l}_ptb2", [128, 2 * 1024], BF16))
    PO = esF.enter_context(nc.psum_tensor(f"m{l}_po", [128, 2 * 512], F32))
    PS1 = esF.enter_context(nc.psum_tensor(f"m{l}_ps1", [1, 512], F32))
    PS2 = esF.enter_context(nc.psum_tensor(f"m{l}_ps2", [1, 512], F32))
    PBa = esF.enter_context(nc.psum_tensor(f"m{l}_pba", [128, 512], F32))
    PBb = esF.enter_context(nc.psum_tensor(f"m{l}_pbb", [128, 512], F32))

    dg4, pe4, act4, dve4, ds5 = (kb.val(x) for x in ("dg", "pe", "act", "dve", "ds"))
    with nc.Block() as block:
        def pe_wst_done(r):
            return pe4 + r * 20 + 16

        def dve_v_done(r):
            return dve4 + (r + 1) * 4

        dxtF = kb.val("dxt")
        dwF = [kb.val("dw0"), kb.val("dw1")]
        duF = kb.val("du_")

        @block.gpsimd
        def _(g):
            g.wait_ge(kb.sem("dyg"), kb.val("dyg"))
            kb.tag(g.dma_start(out=YG[:].rearrange("p (n t) -> p n t", t=T),
                               in_=hbm["ygb"].rearrange("(n p) t -> p n t", p=128)), "dxt", 16)
            for r in range(DT8):
                if r >= 2:
                    g.wait_ge(kb.sem("pe"), pe_wst_done(r - 2))
                kb.tag(g.dma_start(out=WSTF[:, (r % 2) * DI:(r % 2) * DI + DI],
                                   in_=ow[r * 128:(r + 1) * 128, :]), "dw0" if r % 2 == 0 else "dw1", 16)
                if r >= 1:
                    g.wait_ge(kb.sem("dve"), dve_v_done(r - 1))
                kb.tag(g.dma_start(out=XCUR[:],
                                   in_=hbm[xin_f].rearrange("(n p) t -> n p t", p=128)[r]), "du_", 16)

        @block.tensor
        def _(tensor):
            tensor.wait_ge(kb.sem("dxt"), dxtF + 16)
            for r in range(DT8):
                tensor.wait_ge(kb.sem("dw0" if r % 2 == 0 else "dw1"),
                               dwF[r % 2] + 16 * (r // 2 + 1))
                for j in range(16):
                    jj = r * 16 + j
                    if jj >= 2:
                        tensor.wait_ge(kb.sem("act"), act4 + jj - 1)
                    kb.tag(tensor.transpose(PTB2[:, (jj % 2) * 1024:(jj % 2) * 1024 + 128],
                                            WSTF[:, (r % 2) * DI + j * 128:(r % 2) * DI + j * 128 + 128],
                                            EYEB[:]), "pe", 1)
                tensor.wait_ge(kb.sem("act"), act4 + (r + 1) * 16)
                for tc in range(TC4):
                    mmi = r * 4 + tc
                    if mmi >= 2:
                        tensor.wait_ge(kb.sem("dve"), dve4 + mmi - 1)
                    for ki in range(16):
                        mm = tensor.matmul(PO[:, (mmi % 2) * 512:(mmi % 2) * 512 + 512],
                                           OWT[:, (r % 2) * DI + ki * 128:(r % 2) * DI + ki * 128 + 128],
                                           YG[:, ki * T + tc * 512: ki * T + tc * 512 + 512],
                                           start=(ki == 0), stop=(ki == 15))
                    kb.tag(mm, "pe", 1)

        @block.scalar
        def _(scalar):
            for r in range(DT8):
                for j in range(16):
                    jj = r * 16 + j
                    scalar.wait_ge(kb.sem("pe"), pe4 + r * 20 + j + 1)
                    kb.tag(scalar.copy(out=OWT[:, (r % 2) * DI + j * 128:(r % 2) * DI + j * 128 + 128],
                                       in_=PTB2[:, (jj % 2) * 1024:(jj % 2) * 1024 + 128]), "act", 1)

        @block.vector
        def _(vector):
            for r in range(DT8):
                for tc in range(TC4):
                    mmi = r * 4 + tc
                    vector.wait_ge(kb.sem("pe"), pe4 + r * 20 + 16 + tc + 1)
                    vector.wait_ge(kb.sem("du_"), duF + 16 * (r + 1))
                    kb.tag(vector.tensor_add(V[:, r * T + tc * 512: r * T + tc * 512 + 512],
                                             PO[:, (mmi % 2) * 512:(mmi % 2) * 512 + 512],
                                             XCUR[:, tc * 512:tc * 512 + 512]), "dve")

    emit_ln_tail(kb, hbm, V, xout_f, xout_b, LNWC, LNBC,
                 ONESC, ONESR, SQ, ROWS, XOF, XOB, PS1, PS2, PBa, PBb)
    esF.close()
    esS.close()




def sec_gate0(kb, hbm):
    nc = kb.nc
    Wg, bg, bias = hbm["L0_ffn_Wg"], hbm["L0_ffn_bg"], hbm["L0_ffn_bias"]
    b1, b2 = hbm["L0_ffn_b1"], hbm["L0_ffn_b2"]

    esS = ExitStack()
    EYE = esS.enter_context(nc.sbuf_tensor("g_eye", [128, 128], F32))
    ONESR = esS.enter_context(nc.sbuf_tensor("g_onesr", [1, 128], F32))
    SEL16 = esS.enter_context(nc.sbuf_tensor("g_sel16", [16, 16 * 128], F32))
    XN = esS.enter_context(nc.sbuf_tensor("g_xn", [128, DT8 * T], BF16))
    CWT = esS.enter_context(nc.sbuf_tensor("g_cwt", [16, T], F32))
    B1C = esS.enter_context(nc.sbuf_tensor("g_b1c", [128, 16 * 16], F32))

    ds0 = kb.val("ds")
    pe0, act0, dve0, gp0 = (kb.val(x) for x in ("pe", "act", "dve", "gp"))
    dxt0 = kb.val("dxt")

    with nc.Block() as block, ExitStack() as esG0:
        WGB = esG0.enter_context(nc.sbuf_tensor("g_wgb", [128, DT8 * 16], BF16))
        BGB = esG0.enter_context(nc.sbuf_tensor("g_bgb", [128, 16], F32))
        RB = esG0.enter_context(nc.sbuf_tensor("g_rb", [1, 64], F32))
        B1S = esG0.enter_context(nc.sbuf_tensor("g_b1s", [16, HEX], F32))
        LGA = esG0.enter_context(nc.sbuf_tensor("g_lga", [128, 16 * 16], F32))
        MXA = esG0.enter_context(nc.sbuf_tensor("g_mxa", [128, 16 * 2], F32))
        DCOL = esG0.enter_context(nc.sbuf_tensor("g_dcol", [128, 16], F32))
        W2A = esG0.enter_context(nc.sbuf_tensor("g_w2a", [128, 16], F32))
        W1A = esG0.enter_context(nc.sbuf_tensor("g_w1a", [128, 16], F32))
        MSK = esG0.enter_context(nc.sbuf_tensor("g_msk", [128, 3 * 16], F32))
        CWA = esG0.enter_context(nc.sbuf_tensor("g_cwa", [128, 16 * 16], F32))
        PL = esG0.enter_context(nc.psum_tensor("g_pl", [128, 2 * 512], F32))
        PT = esG0.enter_context(nc.psum_tensor("g_pt", [128, 2 * 512], F32))

        @block.sync
        def _(sync):
            kb.tag(sync.dma_start(out=EYE[:], in_=hbm["c_eye_f"]), "ds", 16)
            kb.tag(sync.dma_start(out=ONESR[:], in_=hbm["c_onesrow_f"]), "ds", 16)
            kb.tag(sync.dma_start(out=SEL16[:], in_=hbm["c_sel16_f"]), "ds", 16)
            kb.tag(sync.dma_start(out=RB[:, 0:16], in_=bg), "ds", 16)
            kb.tag(sync.dma_start(out=RB[:, 16:32], in_=bias), "ds", 16)
            kb.tag(sync.dma_start(out=B1S[:], in_=b1), "ds", 16)
        g0_ld = kb.val("ds")

        @block.gpsimd
        def _(g):
            kb.tag(g.dma_start(out=WGB[:].rearrange("p (a e) -> p a e", e=16),
                               in_=Wg.rearrange("(a p) e -> p a e", p=128)), "dg", 16)
            kb.tag(g.dma_start(out=XN[:].rearrange("p (n t) -> p n t", t=T),
                               in_=hbm["xn0b"].rearrange("(n p) t -> p n t", p=128)), "dxt", 16)

        @block.vector
        def _(vector):
            vector.wait_ge(kb.sem("ds"), g0_ld)
            kb.tag(vector.tensor_add(RB[:, 32:48], RB[:, 0:16], RB[:, 16:32]), "dve", 1)

        @block.tensor
        def _(tensor):
            tensor.wait_ge(kb.sem("dve"), dve0 + 1)
            kb.tag(tensor.matmul(PL[:, 0:16], ONESR[:], RB[:, 32:48], start=True, stop=True), "pe", 1)
            for j in range(16):
                if j >= 2:
                    tensor.wait_ge(kb.sem("act"), act0 + j)
                kb.tag(tensor.transpose(PT[:, (j % 2) * 512:(j % 2) * 512 + 16],
                                        B1S[:, j * 128:(j + 1) * 128], EYE[0:16, 0:16]), "pe", 1)
            tensor.wait_ge(kb.sem("dxt"), dxt0 + 16)
            tensor.wait_ge(kb.sem("dg"), kb.val("dg"))
            tensor.wait_ge(kb.sem("act"), act0 + 1)
            for tt in range(TT16):
                if tt >= 2:
                    tensor.wait_ge(kb.sem("act"), act0 + 17 + tt - 1)
                for dk in range(DT8):
                    mm = tensor.matmul(PL[:, (tt % 2) * 512:(tt % 2) * 512 + 16],
                                       XN[:, dk * T + tt * 128: dk * T + tt * 128 + 128],
                                       WGB[:, dk * 16:(dk + 1) * 16],
                                       start=(dk == 0), stop=(dk == DT8 - 1))
                kb.tag(mm, "pe", 1)

        @block.scalar
        def _(scalar):
            scalar.wait_ge(kb.sem("pe"), pe0 + 1)
            kb.tag(scalar.copy(out=BGB[:], in_=PL[:, 0:16]), "act", 1)
            for j in range(16):
                scalar.wait_ge(kb.sem("pe"), pe0 + 1 + j + 1)
                kb.tag(scalar.copy(out=B1C[:, j * 16:(j + 1) * 16],
                                   in_=PT[:, (j % 2) * 512:(j % 2) * 512 + 16]), "act", 1)
            for tt in range(TT16):
                scalar.wait_ge(kb.sem("pe"), pe0 + 17 + tt + 1)
                kb.tag(scalar.copy(out=LGA[:, tt * 16:(tt + 1) * 16],
                                   in_=PL[:, (tt % 2) * 512:(tt % 2) * 512 + 16]), "act", 1)
        act_g = kb.val("act")

        @block.vector
        def _(vector):
            vector.wait_ge(kb.sem("act"), act_g)
            for tt in range(TT16):
                sl = slice(tt * 16, (tt + 1) * 16)
                t1 = kb.tag(vector.tensor_add(LGA[:, sl], LGA[:, sl], BGB[:]), "dve")
                vector.wait_ge(kb.sem("dve"), t1)
                t2 = kb.tag(vector.max(out=MSK[:, 0:8], in_=LGA[:, sl]), "dve")
                vector.wait_ge(kb.sem("dve"), t2)
                t3 = kb.tag(vector.tensor_copy(MXA[:, tt * 2:tt * 2 + 2], MSK[:, 0:2]), "dve")
                vector.wait_ge(kb.sem("dve"), t3)
                t4 = kb.tag(vector.tensor_sub(DCOL[:, tt:tt + 1], MSK[:, 1:2], MSK[:, 0:1]), "dve")
                vector.wait_ge(kb.sem("dve"), t4)
        dve_a = kb.val("dve")

        @block.scalar
        def _(scalar):
            scalar.wait_ge(kb.sem("dve"), dve_a)
            kb.tag(scalar.activation(out=W2A[:], in_=DCOL[:], func=AF.Sigmoid), "act", 1)
        act_sig = kb.val("act")

        @block.vector
        def _(vector):
            vector.wait_ge(kb.sem("act"), act_sig)
            t0 = kb.tag(vector.tensor_scalar(out=W1A[:], in0=W2A[:], scalar1=-1.0, scalar2=1.0,
                                             op0=ALU.mult, op1=ALU.add), "dve")
            vector.wait_ge(kb.sem("dve"), t0)
            for tt in range(TT16):
                sl = slice(tt * 16, (tt + 1) * 16)
                t1 = kb.tag(vector.tensor_scalar(out=MSK[:, 16:32], in0=LGA[:, sl],
                                                 scalar1=MXA[:, tt * 2:tt * 2 + 1], scalar2=None,
                                                 op0=ALU.is_ge), "dve")
                t2 = kb.tag(vector.tensor_scalar(out=MSK[:, 32:48], in0=LGA[:, sl],
                                                 scalar1=MXA[:, tt * 2 + 1:tt * 2 + 2], scalar2=None,
                                                 op0=ALU.is_ge), "dve")
                vector.wait_ge(kb.sem("dve"), t2)
                t3 = kb.tag(vector.tensor_sub(MSK[:, 32:48], MSK[:, 32:48], MSK[:, 16:32]), "dve")
                vector.wait_ge(kb.sem("dve"), t3)
                t4 = kb.tag(vector.tensor_scalar(out=CWA[:, sl], in0=MSK[:, 16:32],
                                                 scalar1=W1A[:, tt:tt + 1], scalar2=None,
                                                 op0=ALU.mult), "dve")
                vector.wait_ge(kb.sem("dve"), t4)
                t5 = kb.tag(vector.scalar_tensor_tensor(out=CWA[:, sl], in0=MSK[:, 32:48],
                                                        scalar=W2A[:, tt:tt + 1], in1=CWA[:, sl],
                                                        op0=ALU.mult, op1=ALU.add), "dve")
                vector.wait_ge(kb.sem("dve"), t5)
        dve_b0 = dve_a + 1

        @block.tensor
        def _(tensor):
            for tt in range(TT16):
                tensor.wait_ge(kb.sem("dve"), dve_b0 + tt * 5 + 5)
                if tt >= 2:
                    tensor.wait_ge(kb.sem("act"), act_sig + tt - 1)
                kb.tag(tensor.transpose(PT[0:16, (tt % 2) * 512:(tt % 2) * 512 + 128],
                                        CWA[:, tt * 16:(tt + 1) * 16], EYE[:]), "pe", 1)

        pe_cw = kb.val("pe")

        @block.scalar
        def _(scalar):
            for tt in range(TT16):
                scalar.wait_ge(kb.sem("pe"), pe_cw - 16 + tt + 1)
                kb.tag(scalar.copy(out=CWT[0:16, tt * 128:(tt + 1) * 128],
                                   in_=PT[0:16, (tt % 2) * 512:(tt % 2) * 512 + 128]), "act", 1)

    with nc.Block() as block, ExitStack() as esB2:
        B2Sg = esB2.enter_context(nc.sbuf_tensor("g_b2sg", [16, 1024], F32))
        B2CG = esB2.enter_context(nc.sbuf_tensor("g_b2cg", [128, 128], F32))
        PT2 = esB2.enter_context(nc.psum_tensor("g_pt2", [128, 2 * 512], F32))
        act_pre = kb.val("act")
        pe_pre = kb.val("pe")

        @block.sync
        def _(sync):
            kb.tag(sync.dma_start(out=B2Sg[:], in_=b2), "ds", 16)
        b2_ld = kb.val("ds")

        @block.tensor
        def _(tensor):
            tensor.wait_ge(kb.sem("ds"), b2_ld)
            for j in range(DT8):
                if j >= 2:
                    tensor.wait_ge(kb.sem("act"), act_pre + j - 1)
                kb.tag(tensor.transpose(PT2[:, (j % 2) * 512:(j % 2) * 512 + 16],
                                        B2Sg[:, j * 128:(j + 1) * 128], EYE[0:16, 0:16]), "pe", 1)

        @block.scalar
        def _(scalar):
            for j in range(DT8):
                scalar.wait_ge(kb.sem("pe"), pe_pre + j + 1)
                kb.tag(scalar.copy(out=B2CG[:, j * 16:(j + 1) * 16],
                                   in_=PT2[:, (j % 2) * 512:(j % 2) * 512 + 16]), "act", 1)

        @block.sync
        def _(sync):
            sync.wait_ge(kb.sem("act"), kb.val("act"))
            kb.tag(sync.dma_start(out=hbm["cwt0"], in_=CWT[:]), "dof", 16)
            kb.tag(sync.dma_start(out=hbm["b1c"], in_=B1C[:]), "dof", 16)
            kb.tag(sync.dma_start(out=hbm["b2c"], in_=B2CG[:]), "dof", 16)
            sync.wait_ge(kb.sem("dof"), kb.val("dof"))

        kb.drain_body(block)
    esS.close()


def sec_moe0fin(kb, hbm):
    nc = kb.nc
    esS = ExitStack()
    ONESC = esS.enter_context(nc.sbuf_tensor("g_onesc", [128, 1], F32))
    ONESR2 = esS.enter_context(nc.sbuf_tensor("g_onesr2", [1, 128], F32))
    V = esS.enter_context(nc.sbuf_tensor("g_v", [128, DT8 * T], F32))
    ROWS = esS.enter_context(nc.sbuf_tensor("g_rows", [1, 4 * T], F32))
    SQ = esS.enter_context(nc.sbuf_tensor("g_sq", [128, 2 * 512], F32))
    XOF = esS.enter_context(nc.sbuf_tensor("g_xof", [128, T], F32))
    XOB = esS.enter_context(nc.sbuf_tensor("g_xob", [128, T], BF16))
    PS1 = esS.enter_context(nc.psum_tensor("g_ps1", [1, 512], F32))
    PS2 = esS.enter_context(nc.psum_tensor("g_ps2", [1, 512], F32))
    PBa = esS.enter_context(nc.psum_tensor("g_pba", [128, 512], F32))
    PBb = esS.enter_context(nc.psum_tensor("g_pbb", [128, 512], F32))

    with nc.Block() as block, ExitStack() as esG2:
        GS = esG2.enter_context(nc.sbuf_tensor("g_gs", [128, T], F32))
        S0T = esG2.enter_context(nc.sbuf_tensor("g_s0t", [128, T], F32))
        X0T = esG2.enter_context(nc.sbuf_tensor("g_x0t", [128, T], F32))

        act2, dve2 = kb.val("act"), kb.val("dve")
        du0, dzr0, dg0 = kb.val("du_"), kb.val("dzr"), kb.val("dg")
        dof0 = kb.val("dof")
        ds0 = kb.val("ds")

        @block.sync
        def _(sync):
            kb.tag(sync.dma_start(out=ONESC[:], in_=hbm["c_onescol_f"]), "ds", 16)
            kb.tag(sync.dma_start(out=ONESR2[:], in_=hbm["c_onesrow_f"]), "ds", 16)

        @block.gpsimd
        def _(g):
            for dk in range(DT8):
                if dk >= 1:
                    g.wait_ge(kb.sem("dve"), dve2 + dk * 2)
                kb.tag(g.dma_start(out=S0T[:], in_=hbm["s0f"].rearrange("(n p) t -> n p t", p=128)[dk]), "du_", 16)
                kb.tag(g.dma_start(out=X0T[:], in_=hbm["x0T"].rearrange("(n p) t -> n p t", p=128)[dk]), "dzr", 16)
                kb.tag(g.dma_start(out=GS[:], in_=hbm["g0f"].rearrange("(n p) t -> n p t", p=128)[dk]), "dg", 16)

        @block.vector
        def _(vector):
            for dm in range(DT8):
                vector.wait_ge(kb.sem("du_"), du0 + 16 * (dm + 1))
                vector.wait_ge(kb.sem("dzr"), dzr0 + 16 * (dm + 1))
                vector.wait_ge(kb.sem("dg"), dg0 + 16 * (dm + 1))
                t1 = kb.tag(vector.tensor_mul(GS[:], GS[:], S0T[:]), "dve")
                vector.wait_ge(kb.sem("dve"), t1)
                kb.tag(vector.tensor_add(V[:, dm * T:(dm + 1) * T], GS[:], X0T[:]), "dve")

        @block.sync
        def _(sync):
            x1f_h = hbm["x1f"].rearrange("(n p) t -> n p t", p=128)
            for dm in range(DT8):
                sync.wait_ge(kb.sem("dve"), dve2 + dm * 2 + 2)
                kb.tag(sync.dma_start(out=x1f_h[dm], in_=V[:, dm * T:(dm + 1) * T]), "dof", 16)
            sync.wait_ge(kb.sem("dof"), kb.val("dof"))

        kb.drain_body(block)

    emit_ln_tail(kb, hbm, V, None, "xn1b", None, None,
                 ONESC, ONESR2, SQ, ROWS, XOF, XOB, PS1, PS2, PBa, PBb)
    esS.close()





def sec_ln1(kb, hbm):
    nc = kb.nc
    esS = ExitStack()
    ONESC = esS.enter_context(nc.sbuf_tensor("l1_onesc", [128, 1], F32))
    ONESR = esS.enter_context(nc.sbuf_tensor("l1_onesr", [1, 128], F32))
    V = esS.enter_context(nc.sbuf_tensor("l1_v", [128, DT8 * T], F32))
    ROWS = esS.enter_context(nc.sbuf_tensor("l1_rows", [1, 4 * T], F32))
    SQ = esS.enter_context(nc.sbuf_tensor("l1_sq", [128, 2 * 512], F32))
    XOF = esS.enter_context(nc.sbuf_tensor("l1_xof", [128, T], F32))
    XOB = esS.enter_context(nc.sbuf_tensor("l1_xob", [128, T], BF16))
    PS1 = esS.enter_context(nc.psum_tensor("l1_ps1", [1, 512], F32))
    PS2 = esS.enter_context(nc.psum_tensor("l1_ps2", [1, 512], F32))
    PBa = esS.enter_context(nc.psum_tensor("l1_pba", [128, 512], F32))
    PBb = esS.enter_context(nc.psum_tensor("l1_pbb", [128, 512], F32))

    ds0 = kb.val("ds")
    with nc.Block() as block:
        @block.sync
        def _(sync):
            kb.tag(sync.dma_start(out=ONESC[:], in_=hbm["c_onescol_f"]), "ds", 16)
            kb.tag(sync.dma_start(out=ONESR[:], in_=hbm["c_onesrow_f"]), "ds", 16)
            kb.tag(sync.dma_start(out=V[:].rearrange("p (n t) -> p n t", t=T),
                                  in_=hbm["x1f"].rearrange("(n p) t -> p n t", p=128)), "ds", 16)
            sync.wait_ge(kb.sem("ds"), kb.val("ds"))
    emit_ln_tail(kb, hbm, V, None, "xn1b", None, None,
                 ONESC, ONESR, SQ, ROWS, XOF, XOB, PS1, PS2, PBa, PBb)
    esS.close()




def sec_att1(kb, hbm):
    nc = kb.nc

    esS = ExitStack()
    EYE = esS.enter_context(nc.sbuf_tensor("a_eye", [128, 128], F32))
    EYEB = esS.enter_context(nc.sbuf_tensor("a_eyeb", [128, 128], BF16))
    ONESC = esS.enter_context(nc.sbuf_tensor("a_onesc", [128, 1], F32))
    ONESR = esS.enter_context(nc.sbuf_tensor("a_onesr", [1, 128], F32))
    MASKT = esS.enter_context(nc.sbuf_tensor("a_maskt", [128, 128], BF16))
    WOT = esS.enter_context(nc.sbuf_tensor("a_wot", [128, 2 * 1024], BF16))
    QH = esS.enter_context(nc.sbuf_tensor("a_qh", [128, 2 * T], BF16))
    KH = esS.enter_context(nc.sbuf_tensor("a_kh", [128, 2 * T], BF16))
    VTOK = esS.enter_context(nc.sbuf_tensor("a_vtok", [128, 16 * 256], BF16))
    KTOK = esS.enter_context(nc.sbuf_tensor("a_ktok", [128, 16 * 256], BF16))
    KCUM = esS.enter_context(nc.sbuf_tensor("a_kcum", [128, 2 * T], F32))
    DNB = esS.enter_context(nc.sbuf_tensor("a_dnb", [128, 2 * T], F32))
    YT = esS.enter_context(nc.sbuf_tensor("a_yt", [128, 2 * T], BF16))
    ROWA = esS.enter_context(nc.sbuf_tensor("a_rowa", [1, T], F32))
    ROWB = esS.enter_context(nc.sbuf_tensor("a_rowb", [1, T], F32))
    ZRO = esS.enter_context(nc.sbuf_tensor("a_zro", [128, T], F32))

    esQ = ExitStack()
    XN1 = esQ.enter_context(nc.sbuf_tensor("a_xn1", [128, DT8 * T], BF16))
    WS = esQ.enter_context(nc.sbuf_tensor("a_ws", [128, 2 * 1024], BF16))
    WS2 = esQ.enter_context(nc.sbuf_tensor("a_ws2", [128, 8 * 256], BF16))
    WQT = esQ.enter_context(nc.sbuf_tensor("a_wqt", [128, DT8 * 256], BF16))
    WKT = esQ.enter_context(nc.sbuf_tensor("a_wkt", [128, DT8 * 256], BF16))
    WVT = esQ.enter_context(nc.sbuf_tensor("a_wvt", [128, DT8 * 256], BF16))
    QT = esQ.enter_context(nc.sbuf_tensor("a_qt", [128, 2 * T], F32))
    KT = esQ.enter_context(nc.sbuf_tensor("a_kt", [128, 2 * T], F32))
    MMS = esQ.enter_context(nc.sbuf_tensor("a_mms", [128, T], F32))
    SQS = esQ.enter_context(nc.sbuf_tensor("a_sqs", [128, T], F32))
    PTRB = esQ.enter_context(nc.psum_tensor("a_ptrb", [128, 2 * 1024], BF16))
    PQ = esQ.enter_context(nc.psum_tensor("a_pq", [128, 2 * 512], F32))
    PN = esQ.enter_context(nc.psum_tensor("a_pn", [1, 512], F32))
    PBB = esQ.enter_context(nc.psum_tensor("a_pbb", [128, 512], F32))

    pe0, act0, dve0 = (kb.val(x) for x in ("pe", "act", "dve"))
    dxt0, dw00, dw10 = kb.val("dxt"), kb.val("dw0"), kb.val("dw1")
    with nc.Block() as block:
        @block.sync
        def _(sync):
            kb.tag(sync.dma_start(out=EYE[:], in_=hbm["c_eye_f"]), "ds", 16)
            kb.tag(sync.dma_start(out=EYEB[:], in_=hbm["c_eye_b"]), "ds", 16)
            kb.tag(sync.dma_start(out=ONESC[:], in_=hbm["c_onescol_f"]), "ds", 16)
            kb.tag(sync.dma_start(out=ONESR[:], in_=hbm["c_onesrow_f"]), "ds", 16)
            kb.tag(sync.dma_start(out=MASKT[:], in_=hbm["c_maskT_b"]), "ds", 16)
        a_ld = kb.val("ds")

        @block.gpsimd
        def _(g):
            kb.tag(g.dma_start(out=XN1[:].rearrange("p (n t) -> p n t", t=T),
                               in_=hbm["xn1b"].rearrange("(n p) t -> p n t", p=128)), "dxt", 16)
            for wi, wname in enumerate(("L1_attn_Wq", "L1_attn_Wk", "L1_attn_Wv")):
                if wi >= 1:
                    g.wait_ge(kb.sem("pe"), pe0 + wi * 16)
                kb.tag(g.dma_start(out=WS[:].rearrange("p (r d) -> p r d", d=1024),
                                   in_=hbm[wname].rearrange("(r p) d -> p r d", p=128)), "dw0", 16)
            kb.tag(g.dma_start(out=WS2[:].rearrange("p (a j) -> p a j", j=256),
                               in_=hbm["L1_attn_Wo"].rearrange("(a p) j -> p a j", p=128)), "dw1", 16)
            kb.tag(g.memset(ZRO[:], 0.0), "gp", 1)

        @block.tensor
        def _(tensor):
            tensor.wait_ge(kb.sem("ds"), a_ld)
            for wi in range(3):
                tensor.wait_ge(kb.sem("dw0"), dw00 + 16 * (wi + 1))
                for j in range(16):
                    r, dk = divmod(j, DT8)
                    gj = wi * 16 + j
                    if gj >= 2:
                        tensor.wait_ge(kb.sem("act"), act0 + gj - 1)
                    kb.tag(tensor.transpose(PTRB[:, (gj % 2) * 1024:(gj % 2) * 1024 + 128],
                                            WS[:, r * 1024 + dk * 128: r * 1024 + dk * 128 + 128],
                                            EYEB[:]), "pe", 1)
            tensor.wait_ge(kb.sem("dw1"), dw10 + 16)
            for j in range(16):
                a, ji = divmod(j, 2)
                gj = 48 + j
                tensor.wait_ge(kb.sem("act"), act0 + gj - 1)
                kb.tag(tensor.transpose(PTRB[:, (gj % 2) * 1024:(gj % 2) * 1024 + 128],
                                        WS2[:, a * 256 + ji * 128: a * 256 + ji * 128 + 128],
                                        EYEB[:]), "pe", 1)
            tensor.wait_ge(kb.sem("dxt"), dxt0 + 16)
            tensor.wait_ge(kb.sem("act"), act0 + 64)
            for gi in range(32):
                if gi >= 2:
                    tensor.wait_ge(kb.sem("act"), act0 + 64 + gi - 1)
                if gi < 8:
                    mi, tc = divmod(gi, 4)
                    for dk in range(DT8):
                        mm = tensor.matmul(PQ[:, (gi % 2) * 512:(gi % 2) * 512 + 512],
                                           WQT[:, dk * 256 + mi * 128: dk * 256 + mi * 128 + 128],
                                           XN1[:, dk * T + tc * 512: dk * T + tc * 512 + 512],
                                           start=(dk == 0), stop=(dk == DT8 - 1))
                elif gi < 16:
                    mi, tc = divmod(gi - 8, 4)
                    for dk in range(DT8):
                        mm = tensor.matmul(PQ[:, (gi % 2) * 512:(gi % 2) * 512 + 512],
                                           WKT[:, dk * 256 + mi * 128: dk * 256 + mi * 128 + 128],
                                           XN1[:, dk * T + tc * 512: dk * T + tc * 512 + 512],
                                           start=(dk == 0), stop=(dk == DT8 - 1))
                else:
                    tt = gi - 16
                    for dk in range(DT8):
                        mm = tensor.matmul(PQ[:, (gi % 2) * 512:(gi % 2) * 512 + 256],
                                           XN1[:, dk * T + tt * 128: dk * T + tt * 128 + 128],
                                           WVT[:, dk * 256:(dk + 1) * 256],
                                           start=(dk == 0), stop=(dk == DT8 - 1))
                kb.tag(mm, "pe", 1)

        @block.scalar
        def _(scalar):
            for gj in range(64):
                scalar.wait_ge(kb.sem("pe"), pe0 + gj + 1)
                if gj < 48:
                    wi, j = divmod(gj, 16)
                    r, dk = divmod(j, DT8)
                    WDST = (WQT, WKT, WVT)[wi]
                    kb.tag(scalar.copy(out=WDST[:, dk * 256 + r * 128: dk * 256 + r * 128 + 128],
                                       in_=PTRB[:, (gj % 2) * 1024:(gj % 2) * 1024 + 128]), "act", 1)
                else:
                    j = gj - 48
                    a, ji = divmod(j, 2)
                    kb.tag(scalar.copy(out=WOT[:, ji * 1024 + a * 128: ji * 1024 + a * 128 + 128],
                                       in_=PTRB[:, (gj % 2) * 1024:(gj % 2) * 1024 + 128]), "act", 1)
            for gi in range(32):
                scalar.wait_ge(kb.sem("pe"), pe0 + 64 + gi + 1)
                if gi < 8:
                    mi, tc = divmod(gi, 4)
                    kb.tag(scalar.copy(out=QT[:, mi * T + tc * 512: mi * T + tc * 512 + 512],
                                       in_=PQ[:, (gi % 2) * 512:(gi % 2) * 512 + 512]), "act", 1)
                elif gi < 16:
                    mi, tc = divmod(gi - 8, 4)
                    kb.tag(scalar.copy(out=KT[:, mi * T + tc * 512: mi * T + tc * 512 + 512],
                                       in_=PQ[:, (gi % 2) * 512:(gi % 2) * 512 + 512]), "act", 1)
                else:
                    tt = gi - 16
                    kb.tag(scalar.copy(out=VTOK[:, tt * 256:(tt + 1) * 256],
                                       in_=PQ[:, (gi % 2) * 512:(gi % 2) * 512 + 256]), "act", 1)

    with nc.Block() as block:
        se = SerialEmitter(kb)
        for SRC, DSTH in ((QT, QH), (KT, KH)):
            for mi in range(2):
                sl = slice(mi * T, mi * T + T)
                se.step("vector", lambda e, SRC=SRC, sl=sl: e.tensor_scalar(
                    out=MMS[:], in0=SRC[:, sl], scalar1=0.0, scalar2=None, op0=ALU.min))
                se.step("scalar", lambda e: e.activation(out=SQS[:], in_=MMS[:], func=AF.Exp))
                se.step("vector", lambda e, SRC=SRC, sl=sl: e.scalar_tensor_tensor(
                    out=SRC[:, sl], in0=SRC[:, sl], scalar=0.0, in1=SQS[:],
                    op0=ALU.max, op1=ALU.add))
                se.step("scalar", lambda e, SRC=SRC, sl=sl: e.activation(
                    out=SQS[:], in_=SRC[:, sl], func=AF.Square))
                for hh in range(2):
                    hr = slice(hh * 64, hh * 64 + 64)
                    for tc in range(TC4):
                        se.step("tensor", lambda e, hr=hr, tc=tc: e.matmul(
                            PN[:], ONESC[hr, :], SQS[hr, tc * 512:(tc + 1) * 512],
                            start=True, stop=True))
                        se.step("scalar", lambda e, tc=tc: e.copy(
                            out=ROWA[:, tc * 512:(tc + 1) * 512], in_=PN[:]))
                    se.step("scalar", lambda e: e.activation(out=ROWB[:], in_=ROWA[:], func=AF.Sqrt))
                    se.step("vector", lambda e: e.tensor_scalar(
                        out=ROWB[:], in0=ROWB[:], scalar1=ATT_EPS, scalar2=None, op0=ALU.add))
                    se.step("vector", lambda e: e.reciprocal(ROWB[:], ROWB[:]))
                    for tc in range(TC4):
                        se.step("tensor", lambda e, hr=hr, tc=tc: e.matmul(
                            PBB[hr, 0:512], ONESR[:, 0:64], ROWB[:, tc * 512:(tc + 1) * 512],
                            start=True, stop=True))
                        se.step("vector", lambda e, SRC=SRC, DSTH=DSTH, hr=hr, mi=mi, tc=tc: e.tensor_mul(
                            DSTH[hr, mi * T + tc * 512: mi * T + tc * 512 + 512],
                            SRC[hr, mi * T + tc * 512: mi * T + tc * 512 + 512],
                            PBB[hr, 0:512]))
        for mi in range(2):
            for tt in range(TT16):
                se.step("tensor", lambda e, mi=mi, tt=tt: e.transpose(
                    PTRB[:, 0:128], KH[:, mi * T + tt * 128: mi * T + tt * 128 + 128], EYEB[:]))
                se.step("scalar", lambda e, mi=mi, tt=tt: e.copy(
                    out=KTOK[:, tt * 256 + mi * 128: tt * 256 + mi * 128 + 128], in_=PTRB[:, 0:128]))
        for mi in range(2):
            sl = slice(mi * T, mi * T + T)
            se.step("vector", lambda e, sl=sl: e.tensor_tensor_scan(
                out=KCUM[:, sl], data0=ZRO[:], data1=KH[:, sl], initial=0.0,
                op0=ALU.add, op1=ALU.add))
        for mi in range(2):
            sl = slice(mi * T, mi * T + T)
            se.step("vector", lambda e, sl=sl: e.tensor_mul(MMS[:], QH[:, sl], KCUM[:, sl]))
            for hh in range(2):
                hr = slice(hh * 64, hh * 64 + 64)
                for tc in range(TC4):
                    se.step("tensor", lambda e, hr=hr, tc=tc: e.matmul(
                        PN[:], ONESC[hr, :], MMS[hr, tc * 512:(tc + 1) * 512],
                        start=True, stop=True))
                    se.step("scalar", lambda e, tc=tc: e.copy(
                        out=ROWA[:, tc * 512:(tc + 1) * 512], in_=PN[:]))
                se.step("vector", lambda e: e.tensor_scalar(
                    out=ROWA[:], in0=ROWA[:], scalar1=ATT_EPS, scalar2=None, op0=ALU.add))
                se.step("vector", lambda e: e.reciprocal(ROWA[:], ROWA[:]))
                for tc in range(TC4):
                    se.step("tensor", lambda e, hr=hr, tc=tc: e.matmul(
                        PBB[hr, 0:512], ONESR[:, 0:64], ROWA[:, tc * 512:(tc + 1) * 512],
                        start=True, stop=True))
                    se.step("scalar", lambda e, hr=hr, mi=mi, tc=tc: e.copy(
                        out=DNB[hr, mi * T + tc * 512: mi * T + tc * 512 + 512],
                        in_=PBB[hr, 0:512]))
        se.emit(block)
    esQ.close()

    esC = ExitStack()
    SMT = esC.enter_context(nc.sbuf_tensor("a_smt", [128, 128], BF16))
    KVT = esC.enter_context(nc.sbuf_tensor("a_kvt", [128, 2 * 64], BF16))
    KVF = esC.enter_context(nc.sbuf_tensor("a_kvf", [128, 2 * 64], F32))
    S1F = esC.enter_context(nc.sbuf_tensor("a_s1f", [128, T], F32))
    PST = esC.enter_context(nc.psum_tensor("a_pst", [128, 512], F32))
    PNM = esC.enter_context(nc.psum_tensor("a_pnm", [128, 512], F32))
    PKV = esC.enter_context(nc.psum_tensor("a_pkv", [128, 2 * 64], F32))
    PSO = esC.enter_context(nc.psum_tensor("a_pso", [128, 2 * 512], F32))

    with nc.Block() as block:
        se = SerialEmitter(kb)
        for c in range(16):
            for h in range(4):
                mi, hh = divmod(h, 2)
                hr = slice(hh * 64, hh * 64 + 64)
                csl = slice(mi * T + c * 128, mi * T + c * 128 + 128)
                se.step("tensor", lambda e, hr=hr, csl=csl: e.matmul(
                    PST[:, 0:128], KH[hr, csl], QH[hr, csl], start=True, stop=True))
                se.step("vector", lambda e: e.tensor_tensor(
                    out=SMT[:], in0=PST[:, 0:128], in1=MASKT[:], op=ALU.mult))
                se.step("tensor", lambda e, c=c, h=h, hr=hr: e.matmul(
                    PNM[hr, 0:128], VTOK[:, c * 256 + h * 64: c * 256 + h * 64 + 64],
                    SMT[:], start=True, stop=(c == 0)))
                if c > 0:
                    se.step("tensor", lambda e, h=h, hr=hr, csl=csl, mi=mi: e.matmul(
                        PNM[hr, 0:128], KVT[hr, mi * 64:(mi + 1) * 64], QH[hr, csl],
                        start=False, stop=True))
                se.step("vector", lambda e, hr=hr, csl=csl: e.tensor_tensor(
                    out=YT[hr, csl], in0=PNM[hr, 0:128], in1=DNB[hr, csl], op=ALU.mult))
                se.step("tensor", lambda e, c=c, h=h, hr=hr, mi=mi: e.matmul(
                    PKV[hr, mi * 64:(mi + 1) * 64], KTOK[:, c * 256 + h * 64: c * 256 + h * 64 + 64],
                    VTOK[:, c * 256 + h * 64: c * 256 + h * 64 + 64],
                    start=True, stop=True))
                if c == 0:
                    se.step("vector", lambda e, hr=hr, mi=mi: e.tensor_copy(
                        KVF[hr, mi * 64:(mi + 1) * 64], PKV[hr, mi * 64:(mi + 1) * 64]))
                else:
                    se.step("vector", lambda e, hr=hr, mi=mi: e.tensor_add(
                        KVF[hr, mi * 64:(mi + 1) * 64], KVF[hr, mi * 64:(mi + 1) * 64],
                        PKV[hr, mi * 64:(mi + 1) * 64]))
                se.step("scalar", lambda e, hr=hr, mi=mi: e.copy(
                    out=KVT[hr, mi * 64:(mi + 1) * 64], in_=KVF[hr, mi * 64:(mi + 1) * 64]))
        se.emit(block)

    dof0 = kb.val("dof")
    pe2, act2, dve2 = (kb.val(x) for x in ("pe", "act", "dve"))
    with nc.Block() as block:
        @block.tensor
        def _(tensor):
            for dm in range(DT8):
                for tc in range(TC4):
                    gi = dm * 4 + tc
                    if gi >= 2:
                        tensor.wait_ge(kb.sem("act"), act2 + gi - 1)
                    for ji in range(2):
                        mm = tensor.matmul(PSO[:, (gi % 2) * 512:(gi % 2) * 512 + 512],
                                           WOT[:, ji * 1024 + dm * 128: ji * 1024 + dm * 128 + 128],
                                           YT[:, ji * T + tc * 512: ji * T + tc * 512 + 512],
                                           start=(ji == 0), stop=(ji == 1))
                    kb.tag(mm, "pe", 1)

        @block.scalar
        def _(scalar):
            for dm in range(DT8):
                for tc in range(TC4):
                    gi = dm * 4 + tc
                    scalar.wait_ge(kb.sem("pe"), pe2 + gi + 1)
                    if tc == 0 and dm >= 1:
                        scalar.wait_ge(kb.sem("dof"), dof0 + 16 * dm)
                    kb.tag(scalar.copy(out=S1F[:, tc * 512:(tc + 1) * 512],
                                       in_=PSO[:, (gi % 2) * 512:(gi % 2) * 512 + 512]), "act", 1)

        @block.sync
        def _(sync):
            s1h = hbm["s1f"].rearrange("(n p) t -> n p t", p=128)
            for dm in range(DT8):
                sync.wait_ge(kb.sem("act"), act2 + dm * 4 + 4)
                kb.tag(sync.dma_start(out=s1h[dm], in_=S1F[:]), "dof", 16)
            sync.wait_ge(kb.sem("dof"), kb.val("dof"))

        kb.drain_body(block)
    esC.close()
    esS.close()





def sec_moe1(kb, hbm):
    nc = kb.nc
    gw = hbm["L1_ffn_gate_w"]
    wg_, wu_, wd_ = hbm["L1_ffn_w_gate"], hbm["L1_ffn_w_up"], hbm["L1_ffn_w_down"]
    wsg, wsu, wsd = hbm["L1_ffn_ws_gate"], hbm["L1_ffn_ws_up"], hbm["L1_ffn_ws_down"]

    esS = ExitStack()
    EYE = esS.enter_context(nc.sbuf_tensor("q_eye", [128, 128], F32))
    SEL16 = esS.enter_context(nc.sbuf_tensor("q_sel16", [16, 16 * 128], F32))
    XN = esS.enter_context(nc.sbuf_tensor("q_xn", [128, DT8 * T], BF16))
    ACC = esS.enter_context(nc.sbuf_tensor("q_acc", [128, DT8 * T], F32))
    CWT = esS.enter_context(nc.sbuf_tensor("q_cwt", [16, T], F32))

    pe0, act0, dve0, gp0 = (kb.val(x) for x in ("pe", "act", "dve", "gp"))
    dxt0 = kb.val("dxt")

    with nc.Block() as block, ExitStack() as esG0:
        GWS = esG0.enter_context(nc.sbuf_tensor("q_gws", [16, 1024], BF16))
        GWT = esG0.enter_context(nc.sbuf_tensor("q_gwt", [128, DT8 * 16], BF16))
        LGA = esG0.enter_context(nc.sbuf_tensor("q_lga", [128, 16 * 16], F32))
        PRB = esG0.enter_context(nc.sbuf_tensor("q_prb", [128, 16], F32))
        MSK = esG0.enter_context(nc.sbuf_tensor("q_msk", [128, 4 * 16], F32))
        CWA = esG0.enter_context(nc.sbuf_tensor("q_cwa", [128, 16 * 16], F32))
        PL = esG0.enter_context(nc.psum_tensor("q_pl", [128, 2 * 512], F32))
        PT = esG0.enter_context(nc.psum_tensor("q_pt", [128, 2 * 512], F32))
        PTB_Q = esG0.enter_context(nc.psum_tensor("q_ptb", [128, 1024], BF16))
        EYEB_Q = esG0.enter_context(nc.sbuf_tensor("q_eyeb", [128, 128], BF16))

        @block.sync
        def _(sync):
            kb.tag(sync.dma_start(out=EYE[:], in_=hbm["c_eye_f"]), "ds", 16)
            kb.tag(sync.dma_start(out=EYEB_Q[:], in_=hbm["c_eye_b"]), "ds", 16)
            kb.tag(sync.dma_start(out=SEL16[:], in_=hbm["c_sel16_f"]), "ds", 16)
        q0_ld = kb.val("ds")

        @block.gpsimd
        def _(g):
            kb.tag(g.dma_start(out=XN[:].rearrange("p (n t) -> p n t", t=T),
                               in_=hbm["xn1b"].rearrange("(n p) t -> p n t", p=128)), "dxt", 16)
            kb.tag(g.dma_start(out=GWS[:], in_=gw), "dg", 16)
            kb.tag(g.memset(ACC[:], 0.0), "gp", 1)

        dg_gw = kb.val("dg")
        se = SerialEmitter(kb)
        for j in range(DT8):
            se.step("tensor", lambda e, j=j: e.transpose(PTB_Q[0:128, 0:16],
                                                         GWS[:, j * 128:(j + 1) * 128],
                                                         EYEB_Q[0:16, 0:16]))
            se.step("scalar", lambda e, j=j: e.copy(out=GWT[:, j * 16:(j + 1) * 16], in_=PTB_Q[0:128, 0:16]))

        @block.tensor
        def _(tensor):
            tensor.wait_ge(kb.sem("dg"), dg_gw)
            tensor.wait_ge(kb.sem("ds"), q0_ld)
        se.emit(block)
        act_t = kb.val("act")
        pe_t = kb.val("pe")

        @block.tensor
        def _(tensor):
            tensor.wait_ge(kb.sem("dxt"), dxt0 + 16)
            tensor.wait_ge(kb.sem("ds"), q0_ld)
            tensor.wait_ge(kb.sem("act"), act_t)
            for tt in range(TT16):
                if tt >= 2:
                    tensor.wait_ge(kb.sem("act"), act_t + tt - 1)
                for dk in range(DT8):
                    mm = tensor.matmul(PL[:, (tt % 2) * 512:(tt % 2) * 512 + 16],
                                       XN[:, dk * T + tt * 128: dk * T + tt * 128 + 128],
                                       GWT[:, dk * 16:(dk + 1) * 16],
                                       start=(dk == 0), stop=(dk == DT8 - 1))
                kb.tag(mm, "pe", 1)

        @block.scalar
        def _(scalar):
            for tt in range(TT16):
                scalar.wait_ge(kb.sem("pe"), pe_t + tt + 1)
                kb.tag(scalar.copy(out=LGA[:, tt * 16:(tt + 1) * 16],
                                   in_=PL[:, (tt % 2) * 512:(tt % 2) * 512 + 16]), "act", 1)
        act_lg = kb.val("act")

        sew = SerialEmitter(kb)
        for tt in range(TT16):
            sl = slice(tt * 16, (tt + 1) * 16)
            if tt == 0:
                sew.step("vector", lambda e, sl=sl: e.max(out=MSK[:, 0:8], in_=LGA[:, sl]))
            else:
                sew.step("vector", lambda e, sl=sl: e.max(out=MSK[:, 0:8], in_=LGA[:, sl]))
            sew.step("vector", lambda e: e.tensor_scalar(out=MSK[:, 8:9], in0=MSK[:, 0:1],
                                                         scalar1=-1.0, scalar2=None, op0=ALU.mult))
            sew.step("scalar", lambda e, sl=sl: e.activation(out=PRB[:], in_=LGA[:, sl], func=AF.Exp,
                                                             bias=MSK[:, 8:9], accum_out=MSK[:, 9:10]))
            sew.step("vector", lambda e: e.reciprocal(MSK[:, 10:11], MSK[:, 9:10]))
            sew.step("vector", lambda e: e.tensor_scalar(out=PRB[:], in0=PRB[:],
                                                         scalar1=MSK[:, 10:11], scalar2=None, op0=ALU.mult))
            sew.step("vector", lambda e: e.max(out=MSK[:, 16:24], in_=PRB[:]))
            sew.step("vector", lambda e: e.tensor_scalar(out=MSK[:, 32:48], in0=PRB[:],
                                                         scalar1=MSK[:, 16:17], scalar2=None, op0=ALU.is_ge))
            sew.step("vector", lambda e: e.tensor_scalar(out=MSK[:, 48:64], in0=PRB[:],
                                                         scalar1=MSK[:, 17:18], scalar2=None, op0=ALU.is_ge))
            sew.step("vector", lambda e: e.tensor_sub(MSK[:, 48:64], MSK[:, 48:64], MSK[:, 32:48]))
            sew.step("vector", lambda e, sl=sl: e.tensor_scalar(out=CWA[:, sl], in0=MSK[:, 32:48],
                                                                scalar1=MSK[:, 16:17], scalar2=None, op0=ALU.mult))
            sew.step("vector", lambda e, sl=sl: e.scalar_tensor_tensor(out=CWA[:, sl], in0=MSK[:, 48:64],
                                                                       scalar=MSK[:, 17:18], in1=CWA[:, sl],
                                                                       op0=ALU.mult, op1=ALU.add))
            sew.step("tensor", lambda e, sl=sl: e.transpose(PT[0:16, 512:512 + 128], CWA[:, sl], EYE[:]))
            sew.step("scalar", lambda e, tt=tt: e.copy(out=CWT[0:16, tt * 128:(tt + 1) * 128],
                                                       in_=PT[0:16, 512:512 + 128]))

        @block.vector
        def _(vector):
            vector.wait_ge(kb.sem("act"), act_lg)
        sew.emit(block)

    with nc.Block() as block, ExitStack() as esG1:
        WGB = esG1.enter_context(nc.sbuf_tensor("q_wgb", [128, DT8 * IE], BF16))
        WUB = esG1.enter_context(nc.sbuf_tensor("q_wub", [128, DT8 * IE], BF16))
        WDB = esG1.enter_context(nc.sbuf_tensor("q_wdb", [128, 4 * 1024], BF16))
        SG = esG1.enter_context(nc.sbuf_tensor("q_sg", [128, 2 * 512], F32))
        A2 = esG1.enter_context(nc.sbuf_tensor("q_a2", [128, 4 * 512], BF16))
        CWB = esG1.enter_context(nc.sbuf_tensor("q_cwb", [128, T], F32))
        Y1 = esG1.enter_context(nc.sbuf_tensor("q_y1", [128, 2 * 512], F32))
        PG = esG1.enter_context(nc.psum_tensor("q_pg", [128, 2 * 512], F32))
        PU = esG1.enter_context(nc.psum_tensor("q_pu", [128, 2 * 512], F32))
        PO = esG1.enter_context(nc.psum_tensor("q_po", [128, 2 * 512], F32))
        PCW = esG1.enter_context(nc.psum_tensor("q_pcw", [128, 2 * 512], F32))

        pe1, act1, dve1, gp1 = (kb.val(x) for x in ("pe", "act", "dve", "gp"))
        dwa0, dwb0 = kb.val("dw0"), kb.val("dw1")
        PE_PE, ACT_PE, DVE_PE, GP_PE = 68, 20, 48, 32

        def peb(e):
            return pe1 + e * PE_PE

        def actb(e):
            return act1 + e * ACT_PE

        def dveb(e):
            return dve1 + e * DVE_PE

        def gpb(e):
            return gp1 + e * GP_PE

        def silu_tag(gi):
            e_, r = divmod(gi, 16)
            tc_, it_ = divmod(r, 4)
            return actb(e_) + 4 + tc_ * 4 + it_ + 1

        def a2_tag(gi):
            e_, r = divmod(gi, 16)
            tc_, it_ = divmod(r, 4)
            return dveb(e_) + tc_ * 12 + it_ + 1

        def y1_tag(gj):
            e_, r = divmod(gj, 32)
            tc_, dm_ = divmod(r, 8)
            return dveb(e_) + tc_ * 12 + 4 + dm_ + 1

        def gp_tag(gj):
            e_, r = divmod(gj, 32)
            tc_, dm_ = divmod(r, 8)
            return gpb(e_) + tc_ * 8 + dm_ + 1

        @block.gpsimd
        def _(g):
            for e in range(NE):
                if e >= 1:
                    g.wait_ge(kb.sem("pe"), peb(e - 1) + 4 + 3 * 16 + 8)
                for dk in range(DT8):
                    kb.tag(g.dma_start(out=WGB[:, dk * IE:(dk + 1) * IE],
                                       in_=wg_[e][dk * 128:(dk + 1) * 128, :]), "dw0", 16)
                    kb.tag(g.dma_start(out=WUB[:, dk * IE:(dk + 1) * IE],
                                       in_=wu_[e][dk * 128:(dk + 1) * 128, :]), "dw0", 16)
                if e >= 1:
                    g.wait_ge(kb.sem("pe"), peb(e))
                for it in range(4):
                    kb.tag(g.dma_start(out=WDB[:, it * 1024:(it + 1) * 1024],
                                       in_=wd_[e][it * 128:(it + 1) * 128, :]), "dw1", 16)


        @block.tensor
        def _(tensor):
            tensor.wait_ge(kb.sem("dxt"), kb.val("dxt"))
            for e in range(NE):
                for q in range(4):
                    gq = e * 4 + q
                    if gq >= 2:
                        pq = gq - 2
                        tensor.wait_ge(kb.sem("act"), actb(pq // 4) + (pq % 4) + 1)
                    kb.tag(tensor.matmul(PCW[:, (q % 2) * 512:(q % 2) * 512 + 512],
                                         SEL16[:, e * 128:(e + 1) * 128],
                                         CWT[0:16, q * 512:(q + 1) * 512], start=True, stop=True), "pe", 1)
                tensor.wait_ge(kb.sem("dw0"), dwa0 + 16 * 16 * (e + 1))
                for tc in range(TC4):
                    for it in range(4):
                        gi = (e * 4 + tc) * 4 + it
                        if gi >= 2:
                            tensor.wait_ge(kb.sem("act"), silu_tag(gi - 2))
                            tensor.wait_ge(kb.sem("dve"), a2_tag(gi - 2))
                        for dk in range(DT8):
                            mm = tensor.matmul(PG[:, (gi % 2) * 512:(gi % 2) * 512 + 512],
                                               WGB[:, dk * IE + it * 128: dk * IE + it * 128 + 128],
                                               XN[:, dk * T + tc * 512: dk * T + tc * 512 + 512],
                                               start=(dk == 0), stop=(dk == DT8 - 1))
                        kb.tag(mm, "pe", 1)
                        for dk in range(DT8):
                            mm = tensor.matmul(PU[:, (gi % 2) * 512:(gi % 2) * 512 + 512],
                                               WUB[:, dk * IE + it * 128: dk * IE + it * 128 + 128],
                                               XN[:, dk * T + tc * 512: dk * T + tc * 512 + 512],
                                               start=(dk == 0), stop=(dk == DT8 - 1))
                        kb.tag(mm, "pe", 1)
                    if tc == 0:
                        tensor.wait_ge(kb.sem("dw1"), dwb0 + 16 * 4 * (e + 1))
                    tensor.wait_ge(kb.sem("dve"), dveb(e) + tc * 12 + 4)
                    for dm in range(DT8):
                        gj = (e * 4 + tc) * 8 + dm
                        if gj >= 2:
                            tensor.wait_ge(kb.sem("dve"), y1_tag(gj - 2))
                        for it in range(4):
                            mm = tensor.matmul(PO[:, (gj % 2) * 512:(gj % 2) * 512 + 512],
                                               WDB[:, it * 1024 + dm * 128: it * 1024 + dm * 128 + 128],
                                               A2[:, it * 512:(it + 1) * 512],
                                               start=(it == 0), stop=(it == 3))
                        kb.tag(mm, "pe", 1)

        @block.scalar
        def _(scalar):
            for e in range(NE):
                for q in range(4):
                    scalar.wait_ge(kb.sem("pe"), peb(e) + q + 1)
                    if e >= 1:
                        scalar.wait_ge(kb.sem("dve"), dveb(e - 1) + q * 12 + 4 + 8)
                    kb.tag(scalar.copy(out=CWB[:, q * 512:(q + 1) * 512],
                                       in_=PCW[:, (q % 2) * 512:(q % 2) * 512 + 512]), "act", 1)
                for tc in range(TC4):
                    for it in range(4):
                        gi = (e * 4 + tc) * 4 + it
                        scalar.wait_ge(kb.sem("pe"), peb(e) + 4 + tc * 16 + 2 * it + 1)
                        if gi >= 2:
                            scalar.wait_ge(kb.sem("dve"), a2_tag(gi - 2))
                        kb.tag(scalar.activation(out=SG[:, (gi % 2) * 512:(gi % 2) * 512 + 512],
                                                 in_=PG[:, (gi % 2) * 512:(gi % 2) * 512 + 512],
                                                 func=AF.Silu), "act", 1)

        @block.vector
        def _(vector):
            for e in range(NE):
                for tc in range(TC4):
                    for it in range(4):
                        gi = (e * 4 + tc) * 4 + it
                        vector.wait_ge(kb.sem("act"), silu_tag(gi))
                        vector.wait_ge(kb.sem("pe"), peb(e) + 4 + tc * 16 + 2 * it + 2)
                        kb.tag(vector.tensor_mul(A2[:, it * 512:(it + 1) * 512],
                                                 SG[:, (gi % 2) * 512:(gi % 2) * 512 + 512],
                                                 PU[:, (gi % 2) * 512:(gi % 2) * 512 + 512]), "dve")
                    for dm in range(DT8):
                        gj = (e * 4 + tc) * 8 + dm
                        vector.wait_ge(kb.sem("pe"), peb(e) + 4 + tc * 16 + 8 + dm + 1)
                        kb.tag(vector.tensor_mul(Y1[:, (gj % 2) * 512:(gj % 2) * 512 + 512],
                                                 PO[:, (gj % 2) * 512:(gj % 2) * 512 + 512],
                                                 CWB[:, tc * 512:(tc + 1) * 512]), "dve")
                        kb.tag(vector.tensor_tensor(out=ACC[:, dm * T + tc * 512: dm * T + tc * 512 + 512],
                                                    in0=ACC[:, dm * T + tc * 512: dm * T + tc * 512 + 512],
                                                    in1=Y1[:, (gj % 2) * 512:(gj % 2) * 512 + 512],
                                                    op=ALU.add), "gp")

    with nc.Block() as block, ExitStack() as esG2:
        WSG = esG2.enter_context(nc.sbuf_tensor("q_wsg", [128, DT8 * 1024], BF16))
        WSU = esG2.enter_context(nc.sbuf_tensor("q_wsu", [128, DT8 * 1024], BF16))
        WSD = esG2.enter_context(nc.sbuf_tensor("q_wsd", [128, DT8 * 1024], BF16))
        SGA = esG2.enter_context(nc.sbuf_tensor("q_sga", [128, DT8 * 512], BF16))
        SGS = esG2.enter_context(nc.sbuf_tensor("q_sgs", [128, 2 * 512], F32))
        Y1 = esG2.enter_context(nc.sbuf_tensor("q_y1b", [128, 2 * 512], F32))
        PG = esG2.enter_context(nc.psum_tensor("q_pg2", [128, 2 * 512], F32))
        PU = esG2.enter_context(nc.psum_tensor("q_pu2", [128, 2 * 512], F32))
        PO = esG2.enter_context(nc.psum_tensor("q_po2", [128, 2 * 512], F32))

        pe2, act2, dve2, gp2 = (kb.val(x) for x in ("pe", "act", "dve", "gp"))
        dwa1, dwb1 = kb.val("dw0"), kb.val("dw1")

        def sga_tag(gi):
            tc_, si_ = divmod(gi, 8)
            return dve2 + tc_ * 8 + si_ + 1

        def acc_tag(gj):
            return gp2 + gj + 1

        @block.gpsimd
        def _(g):
            for dk in range(DT8):
                kb.tag(g.dma_start(out=WSG[:, dk * 1024:(dk + 1) * 1024],
                                   in_=wsg[dk * 128:(dk + 1) * 128, :]), "dw0", 16)
                kb.tag(g.dma_start(out=WSU[:, dk * 1024:(dk + 1) * 1024],
                                   in_=wsu[dk * 128:(dk + 1) * 128, :]), "dw0", 16)
                kb.tag(g.dma_start(out=WSD[:, dk * 1024:(dk + 1) * 1024],
                                   in_=wsd[dk * 128:(dk + 1) * 128, :]), "dw1", 16)


        @block.tensor
        def _(tensor):
            tensor.wait_ge(kb.sem("dw0"), dwa1 + 16 * 16)
            for tc in range(TC4):
                for si in range(DT8):
                    gi = tc * 8 + si
                    if gi >= 2:
                        tensor.wait_ge(kb.sem("act"), act2 + gi - 1)
                        tensor.wait_ge(kb.sem("dve"), sga_tag(gi - 2))
                    for dk in range(DT8):
                        mm = tensor.matmul(PG[:, (gi % 2) * 512:(gi % 2) * 512 + 512],
                                           WSG[:, dk * 1024 + si * 128: dk * 1024 + si * 128 + 128],
                                           XN[:, dk * T + tc * 512: dk * T + tc * 512 + 512],
                                           start=(dk == 0), stop=(dk == DT8 - 1))
                    kb.tag(mm, "pe", 1)
                    for dk in range(DT8):
                        mm = tensor.matmul(PU[:, (gi % 2) * 512:(gi % 2) * 512 + 512],
                                           WSU[:, dk * 1024 + si * 128: dk * 1024 + si * 128 + 128],
                                           XN[:, dk * T + tc * 512: dk * T + tc * 512 + 512],
                                           start=(dk == 0), stop=(dk == DT8 - 1))
                    kb.tag(mm, "pe", 1)
                if tc == 0:
                    tensor.wait_ge(kb.sem("dw1"), dwb1 + 16 * 8)
                tensor.wait_ge(kb.sem("dve"), dve2 + tc * 8 + 8)
                for dm in range(DT8):
                    gj = tc * 8 + dm
                    if gj >= 2:
                        tensor.wait_ge(kb.sem("gp"), acc_tag(gj - 2))
                    for si in range(DT8):
                        mm = tensor.matmul(PO[:, (gj % 2) * 512:(gj % 2) * 512 + 512],
                                           WSD[:, si * 1024 + dm * 128: si * 1024 + dm * 128 + 128],
                                           SGA[:, si * 512:(si + 1) * 512],
                                           start=(si == 0), stop=(si == DT8 - 1))
                    kb.tag(mm, "pe", 1)

        @block.scalar
        def _(scalar):
            for tc in range(TC4):
                for si in range(DT8):
                    gi = tc * 8 + si
                    scalar.wait_ge(kb.sem("pe"), pe2 + tc * 24 + 2 * si + 1)
                    if gi >= 2:
                        scalar.wait_ge(kb.sem("dve"), sga_tag(gi - 2))
                    kb.tag(scalar.activation(out=SGS[:, (gi % 2) * 512:(gi % 2) * 512 + 512],
                                             in_=PG[:, (gi % 2) * 512:(gi % 2) * 512 + 512],
                                             func=AF.Silu), "act", 1)

        @block.vector
        def _(vector):
            for tc in range(TC4):
                for si in range(DT8):
                    gi = tc * 8 + si
                    vector.wait_ge(kb.sem("act"), act2 + gi + 1)
                    vector.wait_ge(kb.sem("pe"), pe2 + tc * 24 + 2 * si + 2)
                    kb.tag(vector.tensor_mul(SGA[:, si * 512:(si + 1) * 512],
                                             SGS[:, (gi % 2) * 512:(gi % 2) * 512 + 512],
                                             PU[:, (gi % 2) * 512:(gi % 2) * 512 + 512]), "dve")
                for dm in range(DT8):
                    gj = tc * 8 + dm
                    vector.wait_ge(kb.sem("pe"), pe2 + tc * 24 + 16 + dm + 1)
                    kb.tag(vector.tensor_tensor(out=ACC[:, dm * T + tc * 512: dm * T + tc * 512 + 512],
                                                in0=ACC[:, dm * T + tc * 512: dm * T + tc * 512 + 512],
                                                in1=PO[:, (gj % 2) * 512:(gj % 2) * 512 + 512],
                                                op=ALU.add), "gp")

    with nc.Block() as block, ExitStack() as esG3:
        GS = esG3.enter_context(nc.sbuf_tensor("q_gs", [128, T], F32))
        S1T = esG3.enter_context(nc.sbuf_tensor("q_s1t", [128, T], F32))
        X1T = esG3.enter_context(nc.sbuf_tensor("q_x1t", [128, T], F32))
        X2F = esG3.enter_context(nc.sbuf_tensor("q_x2f", [128, T], F32))

        pe3, act3, dve3 = (kb.val(x) for x in ("pe", "act", "dve"))
        du0, dzr0, dof0 = kb.val("du_"), kb.val("dzr"), kb.val("dof")

        @block.gpsimd
        def _(g):
            for dk in range(DT8):
                if dk >= 1:
                    g.wait_ge(kb.sem("dve"), dve3 + dk * 2)
                kb.tag(g.dma_start(out=S1T[:], in_=hbm["s1f"].rearrange("(n p) t -> n p t", p=128)[dk]),
                       "du_", 16)
                kb.tag(g.dma_start(out=X1T[:], in_=hbm["x1f"].rearrange("(n p) t -> n p t", p=128)[dk]),
                       "dzr", 16)

        @block.scalar
        def _(scalar):
            for dk in range(DT8):
                if dk >= 1:
                    scalar.wait_ge(kb.sem("dve"), dve3 + dk * 2)
                kb.tag(scalar.activation(out=GS[:], in_=ACC[:, dk * T:(dk + 1) * T], func=AF.Silu), "act", 1)

        @block.vector
        def _(vector):
            for dk in range(DT8):
                vector.wait_ge(kb.sem("act"), act3 + dk + 1)
                vector.wait_ge(kb.sem("du_"), du0 + 16 * (dk + 1))
                vector.wait_ge(kb.sem("dzr"), dzr0 + 16 * (dk + 1))
                if dk >= 1:
                    vector.wait_ge(kb.sem("dof"), dof0 + 16 * dk)
                t1 = kb.tag(vector.tensor_mul(GS[:], GS[:], S1T[:]), "dve")
                vector.wait_ge(kb.sem("dve"), t1)
                kb.tag(vector.tensor_add(X2F[:], GS[:], X1T[:]), "dve")

        @block.sync
        def _(sync):
            x2h = hbm["x2f"].rearrange("(n p) t -> n p t", p=128)
            for dk in range(DT8):
                sync.wait_ge(kb.sem("dve"), dve3 + dk * 2 + 2)
                kb.tag(sync.dma_start(out=x2h[dk], in_=X2F[:]), "dof", 16)
            sync.wait_ge(kb.sem("dof"), kb.val("dof"))

        kb.drain_body(block)
    esS.close()


def sec_fin(kb, hbm):
    nc = kb.nc
    esS = ExitStack()
    EYE = esS.enter_context(nc.sbuf_tensor("f_eye", [128, 128], F32))
    X2 = esS.enter_context(nc.sbuf_tensor("f_x2", [128, DT8 * T], F32))
    OT = esS.enter_context(nc.sbuf_tensor("f_ot", [128, 2 * D], F32))
    PTF = esS.enter_context(nc.psum_tensor("f_ptf", [128, 4 * 512], F32))

    pe0, act0 = kb.val("pe"), kb.val("act")
    ds0, dof0, dob0 = kb.val("ds"), kb.val("dof"), kb.val("dob")
    with nc.Block() as block:
        @block.sync
        def _(sync):
            kb.tag(sync.dma_start(out=EYE[:], in_=hbm["c_eye_f"]), "ds", 16)
            kb.tag(sync.dma_start(out=X2[:].rearrange("p (n t) -> p n t", t=T),
                                  in_=hbm["x2f"].rearrange("(n p) t -> p n t", p=128)), "ds", 16)
        f_ld = kb.val("ds")

        @block.tensor
        def _(tensor):
            tensor.wait_ge(kb.sem("ds"), f_ld)
            for i in range(128):
                tt, dk = divmod(i, DT8)
                if i >= 4:
                    tensor.wait_ge(kb.sem("act"), act0 + (i - 3))
                kb.tag(tensor.transpose(PTF[:, (i % 4) * 512:(i % 4) * 512 + 128],
                                        X2[:, dk * T + tt * 128: dk * T + tt * 128 + 128],
                                        EYE[:]), "pe", 1)

        @block.scalar
        def _(scalar):
            for i in range(128):
                tt, dk = divmod(i, DT8)
                scalar.wait_ge(kb.sem("pe"), pe0 + i + 1)
                if dk == 0 and tt >= 2:
                    par = "dof" if tt % 2 == 0 else "dob"
                    scalar.wait_ge(kb.sem(par), (dof0 if tt % 2 == 0 else dob0) + 16 * ((tt - 2) // 2 + 1))
                kb.tag(scalar.copy(out=OT[:, (tt % 2) * D + dk * 128:(tt % 2) * D + dk * 128 + 128],
                                   in_=PTF[:, (i % 4) * 512:(i % 4) * 512 + 128]), "act", 1)

        @block.sync
        def _(sync):
            outh = hbm["out"].rearrange("(n p) d -> n p d", p=128)
            for tt in range(TT16):
                sync.wait_ge(kb.sem("act"), act0 + (tt + 1) * DT8)
                kb.tag(sync.dma_start(out=outh[tt], in_=OT[:, (tt % 2) * D:(tt % 2) * D + D]),
                       "dof" if tt % 2 == 0 else "dob", 16)
            sync.wait_ge(kb.sem("dof"), kb.val("dof"))
            sync.wait_ge(kb.sem("dob"), kb.val("dob"))

        kb.drain_body(block)
    esS.close()



MW = ["L0_attn_in_proj_w", "L0_attn_conv_w", "L0_attn_conv_b", "L0_attn_x_proj_w",
      "L0_attn_dt_proj_w", "L0_attn_dt_proj_b", "L0_attn_A_log", "L0_attn_D_skip",
      "L0_attn_out_proj_w", "L0_attn_ln_w", "L0_attn_ln_b"]

MWE = MW + ["L0_ffn_W1", "L0_ffn_W2", "L0_ffn_b2"]

SECTIONS = {
    "in": (sec_in, ["L0_attn_x"], [], ["x0T", "xn0f", "xn0b"]),
    "gate0": (sec_gate0, ["L0_ffn_Wg", "L0_ffn_bg", "L0_ffn_bias", "L0_ffn_b1", "L0_ffn_b2"],
              ["xn0b"], ["cwt0", "b1c", "b2c"]),
    "m0": (lambda kb, hbm: sec_mamba(kb, hbm, 0, "xn0f", "xn0b", "mx1f", "mx1b"),
           MWE, ["xn0f", "xn0b", "cwt0", "b1c", "b2c"], ["mx1f", "mx1b", "zbuf", "xcf", "ygb", "ytmp", "g0f"]),
    "m1": (lambda kb, hbm: sec_mamba(kb, hbm, 1, "mx1f", "mx1b", "mx2f", "mx2b"),
           MWE, ["mx1f", "mx1b", "xn0b", "cwt0", "b1c", "b2c"], ["mx2f", "mx2b", "zbuf", "xcf", "ygb", "ytmp", "g0f"]),
    "m2": (lambda kb, hbm: sec_mamba(kb, hbm, 2, "mx2f", "mx2b", "mx3f", "mx3b"),
           MWE, ["mx2f", "mx2b", "xn0b", "cwt0", "b1c", "b2c"], ["mx3f", "mx3b", "zbuf", "xcf", "ygb", "ytmp", "g0f"]),
    "m3": (lambda kb, hbm: sec_mamba(kb, hbm, 3, "mx3f", "mx3b", "s0f", None),
           MWE, ["mx3f", "mx3b", "xn0b", "cwt0", "b1c", "b2c"], ["s0f", "zbuf", "xcf", "ygb", "ytmp", "g0f"]),
    "moe0fin": (sec_moe0fin, [], ["g0f", "s0f", "x0T"], ["x1f", "xn1b"]),
    "att1": (sec_att1, ["L1_attn_Wq", "L1_attn_Wk", "L1_attn_Wv", "L1_attn_Wo"],
             ["xn1b"], ["s1f"]),
    "moe1": (sec_moe1, ["L1_ffn_gate_w", "L1_ffn_w_gate", "L1_ffn_w_up", "L1_ffn_w_down",
                        "L1_ffn_ws_gate", "L1_ffn_ws_up", "L1_ffn_ws_down"],
             ["xn1b", "s1f", "x1f"], ["x2f"]),
    "fin": (sec_fin, [], ["x2f"], []),
}

CONST_KEYS = [k for k in WSPECS if k.startswith("c_")]

LAUNCH_PLAN = [["in", "gate0", "m0", "m1", "m2", "m3", "moe0fin", "att1", "moe1", "fin"]]


def make_constants():
    c = {}
    c["c_eye_f"] = np.eye(128, dtype=np.float32)
    c["c_eye_b"] = np.eye(128, dtype=np.float32).astype(ml_dtypes.bfloat16)
    c["c_onesrow_f"] = np.ones((1, 128), np.float32)
    c["c_onesrow_b"] = np.ones((1, 128), np.float32).astype(ml_dtypes.bfloat16)
    c["c_onescol_f"] = np.ones((128, 1), np.float32)
    sel16 = np.zeros((16, 16 * 128), np.float32)
    for n in range(16):
        sel16[n, n * 128:(n + 1) * 128] = 1.0
    c["c_sel16_f"] = sel16
    sel32 = np.zeros((32, 32 * 128), np.float32)
    for n in range(32):
        sel32[n, n * 128:(n + 1) * 128] = 1.0
    c["c_sel32_b"] = sel32.astype(ml_dtypes.bfloat16)
    m = np.tril(np.ones((128, 128), np.float32))
    c["c_maskT_b"] = m.T.copy().astype(ml_dtypes.bfloat16)
    return c


def build_launch(sections, ext_in, ext_out):
    nc = bass.Bass("TRN2", target_bir_lowering=False)
    hbm = {}
    need_w = set()
    reads = set()
    writes = set()
    for s in sections:
        fn, wk, rd, wr = SECTIONS[s]
        need_w.update(wk)
        reads.update(rd)
        writes.update(wr)
    need_w.update(CONST_KEYS)
    for k in sorted(need_w):
        shape, dt = WSPECS[k]
        hbm[k] = nc.dram_tensor(k, shape, dt, kind="ExternalInput").ap()
    for k in sorted(reads | writes):
        shape, dt = INTER[k]
        if k in ext_in and k not in writes:
            kind = "ExternalInput"
        elif k in ext_out:
            kind = "ExternalOutput"
        else:
            kind = "Internal"
        hbm[k] = nc.dram_tensor(k, shape, dt, kind=kind).ap()
    if "fin" in sections:
        hbm["out"] = nc.dram_tensor("out", [T, D], F32, kind="ExternalOutput").ap()

    kb = KB(nc)
    for s in sections:
        SECTIONS[s][0](kb, hbm)
    kb.es.close()
    in_names = sorted(need_w) + [k for k in sorted(reads) if k in ext_in and k not in writes]
    return nc, in_names


def kernel(**inputs):
    consts = make_constants()
    x = np.ascontiguousarray(inputs["L0_attn_x"])
    base = {}
    for k, v in inputs.items():
        if k == "L0_attn_x":
            continue
        a = np.ascontiguousarray(v)
        shape, _ = WSPECS[k]
        base[k] = a.reshape(shape)
    base.update(consts)

    prod_launch = {}
    for li, group in enumerate(LAUNCH_PLAN):
        for s in group:
            for w in SECTIONS[s][3]:
                prod_launch[w] = li
    cross = set()
    for li, group in enumerate(LAUNCH_PLAN):
        for s in group:
            for r in SECTIONS[s][2]:
                if prod_launch.get(r, li) != li:
                    cross.add(r)

    core_vals = [dict() for _ in range(NCORES)]
    out_final = None
    for li, group in enumerate(LAUNCH_PLAN):
        reads = set()
        writes = set()
        need_w = set()
        for s in group:
            fn, wk, rd, wr = SECTIONS[s]
            need_w.update(wk)
            reads.update(rd)
            writes.update(wr)
        ext_in = {r for r in reads if r in cross}
        ext_out = {w for w in writes if w in cross} | (writes & set(DEV_CAPTURE))
        nc, in_names = build_launch(group, ext_in, ext_out)
        in_maps = []
        for c in range(NCORES):
            m = {}
            for k in in_names:
                if k in base:
                    m[k] = base[k]
                elif k == "L0_attn_x":
                    pass
                else:
                    m[k] = core_vals[c][k]
            if "L0_attn_x" in in_names:
                m["L0_attn_x"] = np.ascontiguousarray(x[c])
            in_maps.append(m)
        res = run_bass_kernel_spmd(nc, in_maps, core_ids=list(range(NCORES)))
        for c in range(NCORES):
            for k, v in res.results[c].items():
                if k == "out":
                    continue
                core_vals[c][k] = np.asarray(v)
        if "out" in res.results[0]:
            out_final = np.stack([np.asarray(res.results[c]["out"]) for c in range(NCORES)], axis=0)
    if out_final is None:
        raise RuntimeError("LAUNCH_PLAN does not produce 'out'")
    return out_final


DEV_CAPTURE = []
